# version 01
# baseline (speedup 1.0000x reference)
import math
import numpy as np
import ml_dtypes
from contextlib import ExitStack

import concourse.bass as bass
import concourse.mybir as mybir
from concourse.bass_utils import run_bass_kernel_spmd

F32 = mybir.dt.float32
BF16 = mybir.dt.bfloat16
AF = mybir.ActivationFunctionType
ALU = mybir.AluOpType
AX = mybir.AxisListType
BF16NP = ml_dtypes.bfloat16

B, S, D = 2, 8192, 2048
TQ = 2048
NCORE = 8
FC = D // 128
INTER = 4096
MH, MHD, MST, MCH = 32, 128, 128, 256
CONV_DIM = INTER + 2 * MST
PROJ_M = INTER + CONV_DIM + MH
HBC_MT = CONV_DIM // 128
HALO = 3
XOFF = 4
TQH = TQ + XOFF
NCHK = TQ // MCH
AH, AHD = 16, 128
HG = 4
FF = 8192
EXP_SHIFT = 12.0
NEG = -1.0e9
RV = 1.0 / math.sqrt(AHD)
GROUPS4 = [[0, 1, 2, 3], [4, 5, 6, 7]]

STAGES = ("mamba", "ffn0", "attnb", "ffn1")
M_PARTS = ("ln", "inprojA", "inprojB", "dt", "sloc", "gather", "sbk", "ssd")
DEBUG_TAPS = ()


def _act_consts():
    return [1e-5, 1.0, -1.0, RV, -EXP_SHIFT, 0.5, 1.0 / D, 1.0 / INTER,
            math.sqrt(2.0 / math.pi), 1.0 / MCH, 100.0]


ENGS = ("tensor", "vector", "scalar", "gpsimd", "sync")


class Sem:
    def __init__(self, nc, ctx, name):
        self.h = ctx.enter_context(nc.semaphore(name))
        self.n = 0


class ES:

    def __init__(self, nc):
        self.nc = nc
        self.q = {e: [] for e in ENGS}

    def _emit(self, eng, fn):
        self.q[eng].append(fn)

    def __getattr__(self, eng):
        if eng in ENGS:
            return lambda fn: self._emit(eng, fn)
        raise AttributeError(eng)

    def wait(self, eng, sem, val=None):
        n = sem.n if val is None else val
        assert n <= sem.n, "waiting on a future semaphore value"
        if n > 0:
            self._emit(eng, lambda E: E.wait_ge(sem.h, n))

    def flush(self, name):
        q = self.q
        self.q = {e: [] for e in ENGS}
        with self.nc.Block(name) as blk:
            for eng in ENGS:
                fns = q[eng]
                if fns:
                    getattr(blk, eng)(lambda E, fns=fns: [f(E) for f in fns])


def inc(sem, k=1):
    sem.n += k
    return lambda inst: inst.then_inc(sem.h, k)


def _fmajor(a):
    return np.ascontiguousarray(np.asarray(a).T)


def stage_inputs(inp):
    x = np.asarray(inp["L0_attn_a_hidden_states"])

    shared = {}
    shared["ident_bf"] = np.eye(128, dtype=BF16NP)
    shared["ident_f"] = np.eye(128, dtype=np.float32)
    shared["ones_col_bf"] = np.ones((128, 1), BF16NP)
    shared["ones_row_bf"] = np.ones((1, 128), BF16NP)
    shared["ones_row_f"] = np.ones((1, 128), np.float32)
    shared["ones_col_f"] = np.ones((128, 1), np.float32)
    m01 = np.zeros((128, 8 * 1024), BF16NP)
    for t in range(8):
        kv = t * 128 + np.arange(128)[:, None]
        qq = np.arange(1024)[None, :]
        m01[:, t * 1024:(t + 1) * 1024] = (kv <= qq).astype(BF16NP)
    shared["mask01"] = m01
    sm = np.zeros((128, 2 * MCH), BF16NP)
    for sh in range(2):
        ss = sh * 128 + np.arange(128)[:, None]
        ll = np.arange(MCH)[None, :]
        sm[:, sh * MCH:(sh + 1) * MCH] = (ss <= ll).astype(BF16NP)
    shared["ssd_mask"] = sm
    shared["consts"] = np.repeat(np.asarray(_act_consts(), np.float32)[None, :],
                                 128, axis=0)

    shared["w_inproj_T"] = _fmajor(inp["L0_attn_a_in_proj_w"])
    shared["conv_w_c"] = np.ascontiguousarray(
        np.asarray(inp["L0_attn_a_conv_w"])[:, 0, :]
        .reshape(HBC_MT, 128, 4).transpose(1, 0, 2).reshape(128, -1))
    shared["conv_b_c"] = np.ascontiguousarray(
        np.asarray(inp["L0_attn_a_conv_b"]).reshape(HBC_MT, 128).T)
    shared["A_log"] = np.asarray(inp["L0_attn_a_A_log"]).reshape(MH, 1)
    shared["dt_bias"] = np.asarray(inp["L0_attn_a_dt_bias"]).reshape(MH, 1)
    shared["D_row"] = np.ascontiguousarray(
        np.asarray(inp["L0_attn_a_D"]).reshape(1, MH))
    shared["norm_w_row"] = np.ascontiguousarray(
        np.asarray(inp["L0_attn_a_norm_w"]).reshape(1, INTER))
    shared["w_outproj_T"] = _fmajor(inp["L0_attn_a_out_proj_w"])

    Wqkv = np.asarray(inp["L0_attn_b_Wqkv"])
    bqkv = np.asarray(inp["L0_attn_b_bqkv"])
    Wo = np.asarray(inp["L0_attn_b_Wo"])
    shared["bo_c"] = np.ascontiguousarray(
        np.asarray(inp["L0_attn_b_bo"]).reshape(FC, 128).T)

    shared["w_router_T"] = _fmajor(inp["L0_ffn_w_router"])
    shared["w_down_T"] = _fmajor(inp["L0_ffn_w_down"])
    shared["w_up_T"] = _fmajor(inp["L0_ffn_w_up"])
    shared["w_ff1_T"] = _fmajor(inp["L0_ffn_w_ff1"])
    shared["w_ff2_T"] = _fmajor(inp["L0_ffn_w_ff2"])
    for nm in ["ln1_scale", "ln1_bias", "ln2_scale", "ln2_bias", "simple_bias"]:
        shared[nm + "_c"] = np.ascontiguousarray(
            np.asarray(inp["L0_ffn_" + nm]).reshape(FC, 128).T)
    wq_T = _fmajor(inp["L0_ffn_wq"])
    wk_T = _fmajor(inp["L0_ffn_wk"])
    wv_T = _fmajor(inp["L0_ffn_wv"])
    wo_T = _fmajor(inp["L0_ffn_wo"])

    maps = []
    for c in range(NCORE):
        b, q = divmod(c, 4)
        m = dict(shared)
        xt = np.ascontiguousarray(x[b].T)
        sl = np.zeros((D, TQH), np.float32)
        t0 = q * TQ
        sl[:, XOFF:] = xt[:, t0:t0 + TQ]
        if t0 >= HALO:
            sl[:, XOFF - HALO:XOFF] = xt[:, t0 - HALO:t0]
        m["x_f"] = sl
        qmask = np.zeros((MH, 4), np.float32)
        scmask = np.zeros((MH, 16), np.float32)
        pmask = np.zeros((MH, 4), np.float32)
        for mm in range(4):
            if mm > q:
                qmask[:, mm] = 1.0
                for k in range(q + 1, mm):
                    scmask[:, mm * 4 + k] = 1.0
            if mm == q - 1:
                pmask[:, mm] = 1.0
        m["qmask"] = qmask
        m["scmask"] = scmask
        m["pmask"] = pmask
        h0 = q * HG * AHD
        m["wqkv_hg"] = np.ascontiguousarray(np.concatenate(
            [Wqkv[:, h0:h0 + 512], Wqkv[:, 2048 + h0:2048 + h0 + 512],
             Wqkv[:, 4096 + h0:4096 + h0 + 512]], axis=1))
        bq = np.concatenate([bqkv[h0:h0 + 512], bqkv[2048 + h0:2048 + h0 + 512],
                             bqkv[4096 + h0:4096 + h0 + 512]])
        m["bqkv_hg"] = np.ascontiguousarray(bq.reshape(12, 128).T)
        m["bqv_row"] = np.ascontiguousarray(bqkv[4096 + h0:4096 + h0 + 512]
                                            .reshape(1, 512))
        m["wo_b_hg"] = np.ascontiguousarray(Wo[h0:h0 + 512, :])
        m["wq_hg"] = np.ascontiguousarray(wq_T[:, h0:h0 + 512])
        m["wk_hg"] = np.ascontiguousarray(wk_T[:, h0:h0 + 512])
        m["wv_hg"] = np.ascontiguousarray(wv_T[:, h0:h0 + 512])
        m["wo_f_hg"] = np.ascontiguousarray(wo_T[h0:h0 + 512, :])
        maps.append(m)
    return maps


_TAPS = {
    "tap_xn": ([128, FC * TQH], BF16),
    "tap_hbc": ([CONV_DIM, TQ], BF16),
    "tap_gate": ([TQ, INTER], BF16),
    "tap_dt": ([MH, TQ], F32),
    "tap_acum": ([MH, TQ], F32),
    "tap_sloc": ([NCHK * INTER, MHD], BF16),
    "tap_sin": ([NCHK * INTER, MHD], BF16),
    "tap_stq": ([INTER, MHD], BF16),
    "tap_dec": ([128, 2 * MH * NCHK], F32),
    "tap_yg": ([TQ, INTER], BF16),
    "tap_ybase": ([128, MH * MHD], F32),
    "tap_kd": ([HG * AHD, S], BF16),
    "tap_vd": ([S, HG * AHD], BF16),
    "tap_ctx": ([HG * AHD, S], BF16),
    "tap_xtl": ([D, TQ], BF16),
    "tap_logits": ([3, TQ], F32),
}


def build_program():
    nc = bass.Bass("TRN2", target_bir_lowering=False, debug=False)
    g = {}

    def IN(name, shape, dtype=F32):
        g[name] = nc.dram_tensor(name, list(shape), dtype, kind="ExternalInput").ap()

    def DRI(name, shape, dtype):
        g[name] = nc.dram_tensor(name, list(shape), dtype, kind="Internal").ap()

    IN("x_f", [D, TQH])
    IN("ident_bf", [128, 128], BF16); IN("ident_f", [128, 128])
    IN("ones_col_bf", [128, 1], BF16)
    IN("ones_row_bf", [1, 128], BF16); IN("ones_row_f", [1, 128])
    IN("ones_col_f", [128, 1])
    IN("mask01", [128, 8 * 1024], BF16); IN("ssd_mask", [128, 2 * MCH], BF16)
    IN("consts", [128, len(_act_consts())])
    IN("w_inproj_T", [D, PROJ_M])
    IN("conv_w_c", [128, HBC_MT * 4]); IN("conv_b_c", [128, HBC_MT])
    IN("A_log", [MH, 1]); IN("dt_bias", [MH, 1])
    IN("D_row", [1, MH]); IN("norm_w_row", [1, INTER])
    IN("w_outproj_T", [INTER, D])
    IN("qmask", [MH, 4]); IN("scmask", [MH, 16]); IN("pmask", [MH, 4])
    IN("wqkv_hg", [D, 3 * 512]); IN("bqkv_hg", [128, 12]); IN("bqv_row", [1, 512])
    IN("wo_b_hg", [512, D]); IN("bo_c", [128, FC])
    IN("w_router_T", [D, 3]); IN("w_down_T", [D, 32]); IN("w_up_T", [32, D])
    IN("w_ff1_T", [D, FF]); IN("w_ff2_T", [FF, D])
    for nm in ["ln1_scale", "ln1_bias", "ln2_scale", "ln2_bias", "simple_bias"]:
        IN(nm + "_c", [128, FC])
    IN("wq_hg", [D, 512]); IN("wk_hg", [D, 512]); IN("wv_hg", [D, 512])
    IN("wo_f_hg", [512, D])

    g["out_f"] = nc.dram_tensor("out_f", [D, TQ], F32, kind="ExternalOutput").ap()

    DRI("x_res", [D, TQ], F32)
    DRI("gate_d", [TQ, INTER], BF16)
    DRI("hbc_d", [CONV_DIM, TQ], BF16)
    DRI("acum_d", [MH, TQ], F32)
    DRI("alast_d", [1, MH * NCHK], F32)
    DRI("pfx_d", [1, MH * NCHK], F32)
    DRI("pvrow_d", [1, 4 * MH], F32)
    DRI("sin_d", [NCHK * INTER, MHD], BF16)
    DRI("lastloc_in", [INTER, MHD], BF16)
    DRI("lastloc_out", [4 * INTER, MHD], BF16)
    DRI("wbase_d", [1, 4 * MH], F32)
    DRI("sloc_d", [NCHK * INTER, MHD], BF16)
    DRI("stq_in", [INTER, MHD], BF16)
    DRI("stq_out", [4 * INTER, MHD], BF16)
    DRI("adsum_in", [MH, 1], F32)
    DRI("adsum_out", [4 * MH, 1], F32)
    for tch in range(4):
        DRI(f"xn_ag_{tch}", [D, 512], BF16)
        DRI(f"ag_out_{tch}", [4 * D, 512], BF16)
    DRI("xn_loc", [D, TQ], BF16)
    DRI("y1_d", [D, TQ], BF16)
    DRI("xtl_d", [D, TQ], BF16)
    DRI("K_d", [HG * AHD, S], BF16)
    DRI("Q_d", [HG * AHD, S], BF16)
    DRI("V_d", [S, HG * AHD], BF16)
    DRI("ctx_d", [HG * AHD, S], BF16)
    DRI("rs_in2", [4 * D, TQ], BF16)
    DRI("rs_out2", [D, TQ], BF16)
    DRI("wff1_bf", [D, FF], BF16)
    DRI("wff2_bf", [FF, D], BF16)

    for nm in DEBUG_TAPS:
        shp, dt = _TAPS[nm]
        g[nm] = nc.dram_tensor(nm, list(shp), dt, kind="ExternalOutput").ap()

    with ExitStack() as ctx:
        _build(nc, ctx, g)
    return nc


def _build(nc, ctx, g):
    cp = {"g": g, "nc": nc, "es": ES(nc)}
    for nm in ["s_wld", "s_xld", "s_pe", "s_ve", "s_se", "s_ge", "s_cc", "s_st"]:
        cp[nm] = Sem(nc, ctx, nm)

    cvals = _act_consts()
    c_tile = ctx.enter_context(nc.sbuf_tensor("c_consts", [128, len(cvals)], F32))
    for i, v in enumerate(cvals):
        nc.const_aps.register(F32, float(v), c_tile[:, i:i + 1])
    cp["c_tile"] = c_tile

    def PSB(name, shape, dtype=F32):
        t = ctx.enter_context(nc.sbuf_tensor("sb_" + name, list(shape), dtype))
        cp[name] = t
        return t

    PSB("ident_bf", [128, 128], BF16)
    PSB("ident_f", [128, 128])
    PSB("ones_col", [128, 1], BF16)
    PSB("ones_row", [1, 128], BF16)
    PSB("ones_row_f", [1, 128])
    PSB("ones_col_f", [128, 1])
    PSB("ssd_mask", [128, 2 * MCH], BF16)
    for nm in ["r_mean", "r_ex2", "r_m2", "r_var", "r_std", "r_rstd", "r_b"]:
        PSB(nm, [1, 512])
    PSB("ln_a", [128, 512])
    PSB("ln_b", [128, 512])

    cp["pk"] = [ctx.enter_context(nc.psum_tensor(f"pk{i}", [128, 512], F32))
                for i in range(6)]
    cp["pb"] = ctx.enter_context(nc.psum_tensor("pb", [128, 1024], BF16))

    es = cp["es"]
    s_wld = cp["s_wld"]
    for dst, src in [(c_tile, g["consts"]), (cp["ident_bf"], g["ident_bf"]),
                     (cp["ident_f"], g["ident_f"]), (cp["ones_col"], g["ones_col_bf"]),
                     (cp["ones_row"], g["ones_row_bf"]),
                     (cp["ones_row_f"], g["ones_row_f"]),
                     (cp["ones_col_f"], g["ones_col_f"]),
                     (cp["ssd_mask"], g["ssd_mask"])]:
        fin = inc(s_wld, 16)
        es.gpsimd(lambda gp, dst=dst, src=src, fin=fin:
                  fin(gp.dma_start(out=dst[:], in_=src)))
    es.flush("preload")

    if "mamba" in STAGES:
        stage_mamba(nc, ctx, cp)
    if "ffn0" in STAGES:
        stage_attn_block(nc, ctx, cp, kind="ffn", call=0)
    if "attnb" in STAGES:
        stage_attn_block(nc, ctx, cp, kind="attnb", call=0)
    if "ffn1" in STAGES:
        stage_attn_block(nc, ctx, cp, kind="ffn", call=1)

    _finalize(nc, ctx, cp)


def _finalize(nc, ctx, cp):
    g, es = cp["g"], cp["es"]
    s_st = cp["s_st"]
    for s in ["s_pe", "s_ve", "s_se", "s_ge", "s_st", "s_wld", "s_xld", "s_cc"]:
        es.wait("sync", cp[s])
    fin = inc(s_st, 16)
    es.sync(lambda sy, fin=fin: fin(sy.dma_start(out=g["out_f"], in_=g["x_res"])))
    es.wait("sync", s_st)
    es.flush("finalize")


def emit_ln(nc, cp, src_sb, ntok, dst_sb, sq_sb, gamma=None, beta=None):
    es = cp["es"]
    pk = cp["pk"]
    s_pe, s_ve, s_se = cp["s_pe"], cp["s_ve"], cp["s_se"]

    es.wait("scalar", cp["s_wld"])
    es.wait("scalar", s_ve)
    es.wait("vector", cp["s_wld"])
    fin = inc(s_se)
    es.scalar(lambda sc, fin=fin:
              fin(sc.activation(sq_sb[:, 0:FC * ntok], src_sb[:, 0:FC * ntok],
                                AF.Square)))
    tiles = []
    t0 = 0
    while t0 < ntok:
        w = min(512, ntok - t0)
        tiles.append((t0, w))
        t0 += w

    es.wait("tensor", s_se)
    for (t0_, w) in tiles:
        fin = inc(s_pe)
        def stats(t, t0_=t0_, w=w, fin=fin):
            for cc in range(FC):
                t.matmul(pk[0][0:1, 0:w], cp["ones_col"][:, :],
                         src_sb[:, cc * ntok + t0_: cc * ntok + t0_ + w],
                         start=(cc == 0), stop=(cc == FC - 1))
            ins = None
            for cc in range(FC):
                ins = t.matmul(pk[1][0:1, 0:w], cp["ones_col"][:, :],
                               sq_sb[:, cc * ntok + t0_: cc * ntok + t0_ + w],
                               start=(cc == 0), stop=(cc == FC - 1))
            fin(ins)
        es.tensor(stats)

        es.wait("vector", s_pe)
        fin = inc(s_ve)
        es.vector(lambda v, w=w, fin=fin: (
            v.tensor_scalar(cp["r_mean"][0:1, 0:w], pk[0][0:1, 0:w],
                            1.0 / D, None, ALU.mult),
            fin(v.tensor_scalar(cp["r_ex2"][0:1, 0:w], pk[1][0:1, 0:w],
                                1.0 / D, None, ALU.mult))))
        es.wait("scalar", s_ve)
        fin = inc(s_se)
        es.scalar(lambda sc, w=w, fin=fin:
                  fin(sc.activation(cp["r_m2"][0:1, 0:w], cp["r_mean"][0:1, 0:w],
                                    AF.Square)))
        es.wait("vector", s_se)
        fin = inc(s_ve)
        es.vector(lambda v, w=w, fin=fin:
                  fin(v.tensor_tensor(cp["r_var"][0:1, 0:w], cp["r_ex2"][0:1, 0:w],
                                      cp["r_m2"][0:1, 0:w], ALU.subtract)))
        es.wait("scalar", s_ve)
        fin = inc(s_se)
        es.scalar(lambda sc, w=w, fin=fin:
                  fin(sc.activation(cp["r_std"][0:1, 0:w], cp["r_var"][0:1, 0:w],
                                    AF.Sqrt, bias=1e-5)))
        es.wait("vector", s_se)
        fin = inc(s_ve)
        es.vector(lambda v, w=w, fin=fin:
                  fin(v.reciprocal(cp["r_rstd"][0:1, 0:w], cp["r_std"][0:1, 0:w])))
        es.wait("vector", s_ve)
        fin = inc(s_ve)
        es.vector(lambda v, w=w, fin=fin:
                  fin(v.scalar_tensor_tensor(cp["r_b"][0:1, 0:w],
                                             cp["r_mean"][0:1, 0:w], -1.0,
                                             cp["r_rstd"][0:1, 0:w],
                                             ALU.mult, ALU.mult)))
        es.wait("tensor", s_ve)
        fin = inc(s_pe)
        es.tensor(lambda t, w=w, fin=fin: (
            t.matmul(pk[2][:, 0:w], cp["ones_row_f"][:, :], cp["r_rstd"][0:1, 0:w],
                     start=True, stop=True),
            fin(t.matmul(pk[3][:, 0:w], cp["ones_row_f"][:, :], cp["r_b"][0:1, 0:w],
                         start=True, stop=True))))
        es.wait("vector", s_pe)
        fin = inc(s_ve)
        es.vector(lambda v, w=w, fin=fin: (
            v.tensor_copy(cp["ln_a"][:, 0:w], pk[2][:, 0:w]),
            fin(v.tensor_copy(cp["ln_b"][:, 0:w], pk[3][:, 0:w]))))
        es.wait("vector", s_ve)
        fin = inc(s_ve)
        def apply1(v, t0_=t0_, w=w, fin=fin):
            ins = None
            for cc in range(FC):
                ins = v.tensor_tensor(
                    sq_sb[:, cc * ntok + t0_: cc * ntok + t0_ + w],
                    src_sb[:, cc * ntok + t0_: cc * ntok + t0_ + w],
                    cp["ln_a"][:, 0:w], ALU.mult)
            fin(ins)
        es.vector(apply1)
        es.wait("vector", s_ve)
        fin = inc(s_ve)
        def apply2(v, t0_=t0_, w=w, fin=fin):
            ins = None
            for cc in range(FC):
                ins = v.tensor_tensor(
                    dst_sb[:, cc * ntok + t0_: cc * ntok + t0_ + w],
                    sq_sb[:, cc * ntok + t0_: cc * ntok + t0_ + w],
                    cp["ln_b"][:, 0:w], ALU.add)
            fin(ins)
        es.vector(apply2)
        if gamma is not None:
            es.wait("vector", s_ve)
            fin = inc(s_ve)
            def apply3(v, t0_=t0_, w=w, fin=fin):
                ins = None
                for cc in range(FC):
                    ins = v.tensor_scalar(
                        dst_sb[:, cc * ntok + t0_: cc * ntok + t0_ + w],
                        dst_sb[:, cc * ntok + t0_: cc * ntok + t0_ + w],
                        gamma[:, cc:cc + 1], beta[:, cc:cc + 1],
                        ALU.mult, ALU.add)
                fin(ins)
            es.vector(apply3)
        es.wait("tensor", s_ve)


def stage_mamba(nc, ctx, cp):
    g, es = cp["g"], cp["es"]

    with ExitStack() as octx:
        def OSB(name, shape, dtype=F32):
            t = octx.enter_context(nc.sbuf_tensor("sb_" + name, list(shape), dtype))
            cp[name] = t
            return t

        OSB("normw_bc", [128, INTER], BF16)
        OSB("D_bc", [128, MH])
        OSB("row_D", [1, MH])
        OSB("m_dt", [MH, TQ])
        OSB("m_acum", [MH, TQ])
        OSB("m_decbc", [128, MH * NCHK])
        OSB("m_pfxbc", [128, MH * NCHK])
        OSB("m_alrow", [1, MH * NCHK])
        OSB("m_convw", [128, HBC_MT * 4])
        OSB("m_convb", [128, HBC_MT])
        OSB("m_Alog", [MH, 1]); OSB("m_dtb", [MH, 1]); OSB("m_A", [MH, 1])
        OSB("m_alast", [MH, NCHK]); OSB("m_adsum", [MH, 1])
        OSB("m_pfx", [MH, NCHK])
        OSB("m_qmask", [MH, 4]); OSB("m_scmask", [MH, 16])
        OSB("m_pmask", [MH, 4])
        OSB("m_dtraw", [MH, TQH])

        with ExitStack() as cprep:
            cp["row_nw"] = cprep.enter_context(
                nc.sbuf_tensor("sb_row_nw", [1, INTER], F32))
            _mamba_prep(nc, cp)

        with ExitStack() as cx:
            xb = cx.enter_context(nc.sbuf_tensor("sb_m_xb", [128, FC * TQH], BF16))
            cp["m_xb"] = xb
            with ExitStack() as c1:
                cp["lnsq"] = c1.enter_context(
                    nc.sbuf_tensor("sb_lnsq", [128, FC * TQH], BF16))
                fin = inc(cp["s_wld"], 16)
                es.gpsimd(lambda gp, fin=fin: fin(gp.dma_start(
                    out=xb[:].rearrange("p (cc t) -> p cc t", cc=FC),
                    in_=g["x_f"].rearrange("(cc p) t -> p cc t", p=128))))
                es.flush("m_loadx")
                if "ln" in M_PARTS:
                    emit_ln(nc, cp, xb, TQH, xb, cp["lnsq"])
                if "tap_xn" in DEBUG_TAPS:
                    es.wait("sync", cp["s_ve"])
                    fin = inc(cp["s_st"], 16)
                    es.sync(lambda sy, fin=fin:
                            fin(sy.dma_start(out=g["tap_xn"], in_=xb[:, :])))
                es.flush("m_ln")

            with ExitStack() as c2:
                def SB2(name, shape, dtype=F32):
                    t = c2.enter_context(
                        nc.sbuf_tensor("sb_" + name, list(shape), dtype))
                    cp[name] = t
                    return t
                SB2("m_wslab0", [128, FC * 128], BF16)
                SB2("m_wslab1", [128, FC * 128], BF16)
                SB2("m_craw0", [128, TQH], BF16)
                SB2("m_craw1", [128, TQH], BF16)
                SB2("m_ctmp0", [128, TQ], BF16)
                SB2("m_ctmp1", [128, TQ], BF16)
                SB2("m_hstage0", [128, TQ], BF16)
                SB2("m_hstage1", [128, TQ], BF16)
                if "inprojA" in M_PARTS:
                    _inproj_hbc_dt(nc, cp)
                es.flush("m_inprojA")
            with ExitStack() as c2:
                def SB2(name, shape, dtype=F32):
                    t = c2.enter_context(
                        nc.sbuf_tensor("sb_" + name, list(shape), dtype))
                    cp[name] = t
                    return t
                SB2("m_gslab0", [128, FC * 512], BF16)
                SB2("m_gslab1", [128, FC * 512], BF16)
                SB2("m_gstage0", [128, 512], BF16)
                SB2("m_gstage1", [128, 512], BF16)
                if "inprojB" in M_PARTS:
                    _inproj_gate(nc, cp)
                es.flush("m_inprojB")
        with ExitStack() as c2:
            def SB2(name, shape, dtype=F32):
                t = c2.enter_context(nc.sbuf_tensor("sb_" + name, list(shape), dtype))
                cp[name] = t
                return t
            SB2("m_t1", [MH, TQ]); SB2("m_t2", [MH, TQ]); SB2("m_zeros", [MH, MCH])
            if "dt" in M_PARTS:
                _dt_path(nc, cp)
            es.flush("m_dt")

        with ExitStack() as c3:
            def SB3(name, shape, dtype=F32):
                t = c3.enter_context(nc.sbuf_tensor("sb_" + name, list(shape), dtype))
                cp[name] = t
                return t
            SB3("m_hbc", [128, HBC_MT * MCH], BF16)
            SB3("m_dtT", [128, 2 * MH])
            SB3("m_acT", [128, 2 * MH])
            SB3("m_dsT", [128, 2 * MH])
            SB3("m_ds", [MH, MCH])
            SB3("m_xdtT", [128, 2 * MHD], BF16)
            SB3("m_xrawT", [128, 2 * MHD], BF16)
            SB3("m_bT", [128, 2 * MST], BF16)
            SB3("m_bdTh", [128, 2 * MST], BF16)
            SB3("m_sinc", [128, MH * MHD], BF16)
            with ExitStack() as c4:
                def SB4(name, shape, dtype=F32):
                    t = c4.enter_context(
                        nc.sbuf_tensor("sb_" + name, list(shape), dtype))
                    cp[name] = t
                    return t
                SB4("m_srun", [128, MH * MHD])
                SB4("m_prev", [128, MH * MHD])
                SB4("m_slocp", [128, MH * MHD], BF16)
                SB4("m_slocc", [128, MH * MHD], BF16)
                SB4("m_sstage", [128, MH * MHD], BF16)
                if "sloc" in M_PARTS:
                    _ssd_local_states(nc, cp)
                es.flush("m_sloc")
                with ExitStack() as c5:
                    def SB5(name, shape, dtype=F32):
                        t = c5.enter_context(
                            nc.sbuf_tensor("sb_" + name, list(shape), dtype))
                        cp[name] = t
                        return t
                    SB5("m_adsq", [MH, 4]); SB5("m_wexp", [MH, 4])
                    SB5("m_stq", [128, 4 * MH * MHD], BF16)
                    SB5("m_wrow", [1, 4 * MH]); SB5("m_wbc", [128, 4 * MH])
                    if "gather" in M_PARTS:
                        _state_gather_combine(nc, cp)
                    es.flush("m_gather")
                if "sbk" in M_PARTS:
                    _state_backward(nc, cp)
                es.flush("m_sbk")
            SB3("m_acrow", [1, (MH // 2) * MCH])
            SB3("m_gm", [128, 2 * MCH])
            SB3("m_eh", [128, 2 * MCH])
            SB3("m_mh", [128, 2 * MCH], BF16)
            SB3("m_csc", [128, MCH], BF16)
            SB3("m_sdbc", [128, MCH])
            SB3("m_ych", [128, 2 * INTER], BF16)
            SB3("m_rsum", [128, 1]); SB3("m_rstd", [128, 1])
            SB3("m_gatec", [128, 2 * INTER], BF16)
            SB3("m_ynf", [128, (INTER // 128) * 1024], BF16)
            if "ssd" in M_PARTS:
                _ssd_main(nc, cp)
            es.flush("m_ssd")


def _mamba_prep(nc, cp):
    g, es = cp["g"], cp["es"]
    pk = cp["pk"]
    s_wld, s_pe, s_se = cp["s_wld"], cp["s_pe"], cp["s_se"]

    for dst, src in [(cp["row_nw"], g["norm_w_row"]), (cp["row_D"], g["D_row"]),
                     (cp["m_convw"], g["conv_w_c"]), (cp["m_convb"], g["conv_b_c"]),
                     (cp["m_Alog"], g["A_log"]), (cp["m_dtb"], g["dt_bias"]),
                     (cp["m_qmask"], g["qmask"]), (cp["m_scmask"], g["scmask"]),
                     (cp["m_pmask"], g["pmask"])]:
        fin = inc(s_wld, 16)
        es.gpsimd(lambda gp, dst=dst, src=src, fin=fin:
                  fin(gp.dma_start(out=dst[:], in_=src)))
    es.wait("tensor", s_wld)
    nseg = INTER // 512
    for j4 in range(0, nseg, 4):
        fin = inc(s_pe)
        def bc4(t, j4=j4, fin=fin):
            ins = None
            for j in range(j4, min(j4 + 4, nseg)):
                ins = t.matmul(pk[j % 4][:, :], cp["ones_row_f"][:, :],
                               cp["row_nw"][0:1, j * 512:(j + 1) * 512],
                               start=True, stop=True)
            fin(ins)
        es.tensor(bc4)
        es.wait("scalar", s_pe)
        fin = inc(s_se)
        def ev4(sc, j4=j4, fin=fin):
            ins = None
            for j in range(j4, min(j4 + 4, nseg)):
                ins = sc.copy(cp["normw_bc"][:, j * 512:(j + 1) * 512],
                              pk[j % 4][:, :])
            fin(ins)
        es.scalar(ev4)
        es.wait("tensor", s_se)
    fin = inc(s_pe)
    es.tensor(lambda t, fin=fin:
              fin(t.matmul(pk[0][:, 0:MH], cp["ones_row_f"][:, :],
                           cp["row_D"][0:1, :], start=True, stop=True)))
    es.wait("scalar", s_pe)
    fin = inc(s_se)
    es.scalar(lambda sc, fin=fin: fin(sc.copy(cp["D_bc"][:, :], pk[0][:, 0:MH])))
    es.wait("tensor", s_se)
    es.flush("m_prep")


def _inproj_hbc_dt(nc, cp):
    g, es = cp["g"], cp["es"]
    pk = cp["pk"]
    s_wld, s_pe, s_ve, s_se, s_st = (
        cp[k] for k in ["s_wld", "s_pe", "s_ve", "s_se", "s_st"])
    xn = cp["m_xb"]

    ntiles = [(t0, min(512, TQH - t0)) for t0 in range(0, TQH, 512)]
    bank_guard = {0: 0, 1: 0}
    NMT = HBC_MT + 1
    pe_after_m = []
    cons_after_m = []

    for m in range(NMT):
        mw = 128 if m < HBC_MT else MH
        m0 = INTER + m * 128
        buf = m % 2
        if m >= 2:
            es.wait("gpsimd", s_pe, pe_after_m[m - 1])
        fin = inc(s_wld, 16)
        es.gpsimd(lambda gp, m0=m0, mw=mw, buf=buf, fin=fin: fin(gp.dma_start(
            out=cp[f"m_wslab{buf}"][:, 0:FC * mw].rearrange(
                "p (kc m) -> p kc m", kc=FC),
            in_=g["w_inproj_T"][:, m0:m0 + mw].rearrange(
                "(kc p) m -> p kc m", p=128))))
        es.wait("tensor", s_wld)
        if m >= 2:
            es.wait("tensor", s_se, cons_after_m[m - 2][1])
            es.wait("tensor", s_ve, cons_after_m[m - 2][0])
        for ni, (t0_, w) in enumerate(ntiles):
            if bank_guard[ni % 2]:
                es.wait("tensor", s_se, bank_guard[ni % 2])
            fin = inc(s_pe)
            def mms(t, mw=mw, buf=buf, ni=ni, t0_=t0_, w=w, fin=fin):
                bank = pk[ni % 2]
                ins = None
                for kc in range(FC):
                    ins = t.matmul(bank[0:mw, 0:w],
                                   cp[f"m_wslab{buf}"][:, kc * mw:kc * mw + mw],
                                   xn[:, kc * TQH + t0_: kc * TQH + t0_ + w],
                                   start=(kc == 0), stop=(kc == FC - 1))
                fin(ins)
            es.tensor(mms)
            es.wait("scalar", s_pe)
            fin = inc(s_se)
            es.scalar(lambda sc, mw=mw, buf=buf, ni=ni, t0_=t0_, w=w, fin=fin:
                      fin(sc.copy(cp[f"m_craw{buf}"][0:mw, t0_:t0_ + w],
                                  pk[ni % 2][0:mw, 0:w])))
            bank_guard[ni % 2] = s_se.n
        if m < HBC_MT:
            cw = cp["m_convw"]
            es.wait("vector", s_se)
            fin = inc(s_ve)
            es.vector(lambda v, m=m, buf=buf, fin=fin:
                      fin(v.tensor_scalar(cp[f"m_ctmp{buf}"][:, 0:TQ],
                                          cp[f"m_craw{buf}"][:, 1:1 + TQ],
                                          cw[:, 4 * m:4 * m + 1], None, ALU.mult)))
            for j in range(1, 4):
                es.wait("vector", s_ve)
                fin = inc(s_ve)
                es.vector(lambda v, m=m, j=j, buf=buf, fin=fin:
                          fin(v.scalar_tensor_tensor(
                              cp[f"m_ctmp{buf}"][:, 0:TQ],
                              cp[f"m_craw{buf}"][:, 1 + j:1 + j + TQ],
                              cw[:, 4 * m + j:4 * m + j + 1],
                              cp[f"m_ctmp{buf}"][:, 0:TQ], ALU.mult, ALU.add)))
            es.wait("scalar", s_ve)
            fin = inc(s_se)
            es.scalar(lambda sc, m=m, buf=buf, fin=fin:
                      fin(sc.activation(cp[f"m_hstage{buf}"][:, 0:TQ],
                                        cp[f"m_ctmp{buf}"][:, 0:TQ], AF.Silu,
                                        bias=cp["m_convb"][:, m:m + 1])))
            es.wait("sync", s_se)
            fin = inc(s_st, 16)
            es.sync(lambda sy, m=m, buf=buf, fin=fin:
                    fin(sy.dma_start(out=g["hbc_d"][m * 128:(m + 1) * 128, :],
                                     in_=cp[f"m_hstage{buf}"][:, 0:TQ])))
        else:
            es.wait("vector", s_se)
            fin = inc(s_ve)
            es.vector(lambda v, buf=buf, fin=fin:
                      fin(v.tensor_copy(cp["m_dtraw"][:, 0:TQH],
                                        cp[f"m_craw{buf}"][0:MH, 0:TQH])))
        pe_after_m.append(s_pe.n)
        cons_after_m.append((s_ve.n, s_se.n))

    if "tap_hbc" in DEBUG_TAPS:
        es.wait("sync", s_st)
        fin = inc(s_st, 16)
        es.sync(lambda sy, fin=fin:
                fin(sy.dma_start(out=g["tap_hbc"], in_=g["hbc_d"])))


def _inproj_gate(nc, cp):
    g, es = cp["g"], cp["es"]
    pk = cp["pk"]
    s_wld, s_pe, s_se, s_st = (cp[k] for k in ["s_wld", "s_pe", "s_se", "s_st"])
    xn = cp["m_xb"]

    NMB = INTER // 512
    NTT = TQ // 128
    pe_after_b = []
    se_at = {}
    st_hist = {}

    for mb in range(NMB):
        buf = mb % 2
        if mb >= 2:
            es.wait("gpsimd", s_pe, pe_after_b[mb - 1])
        fin = inc(s_wld, 16)
        es.gpsimd(lambda gp, mb=mb, buf=buf, fin=fin: fin(gp.dma_start(
            out=cp[f"m_gslab{buf}"][:].rearrange("p (kc m) -> p kc m", kc=FC),
            in_=g["w_inproj_T"][:, mb * 512:(mb + 1) * 512].rearrange(
                "(kc p) m -> p kc m", p=128))))
        es.wait("tensor", s_wld)
        for tt in range(NTT):
            it = mb * NTT + tt
            if it >= 2:
                es.wait("tensor", s_se, se_at[it - 2])
            fin = inc(s_pe)
            def mms(t, tt=tt, buf=buf, fin=fin):
                bank = pk[tt % 2]
                ins = None
                for kc in range(FC):
                    ins = t.matmul(
                        bank[:, 0:512],
                        xn[:, kc * TQH + XOFF + tt * 128:
                           kc * TQH + XOFF + tt * 128 + 128],
                        cp[f"m_gslab{buf}"][:, kc * 512:(kc + 1) * 512],
                        start=(kc == 0), stop=(kc == FC - 1))
                fin(ins)
            es.tensor(mms)
            es.wait("scalar", s_pe)
            if it >= 2:
                es.wait("scalar", s_st, st_hist[it - 2])
            fin = inc(s_se)
            es.scalar(lambda sc, tt=tt, fin=fin:
                      fin(sc.activation(cp[f"m_gstage{tt % 2}"][:, 0:512],
                                        pk[tt % 2][:, 0:512], AF.Silu)))
            se_at[it] = s_se.n
            es.wait("sync", s_se)
            fin = inc(s_st, 16)
            es.sync(lambda sy, mb=mb, tt=tt, fin=fin:
                    fin(sy.dma_start(
                        out=g["gate_d"][tt * 128:(tt + 1) * 128,
                                        mb * 512:(mb + 1) * 512],
                        in_=cp[f"m_gstage{tt % 2}"][:, 0:512])))
            st_hist[it] = s_st.n
        pe_after_b.append(s_pe.n)

    if "tap_gate" in DEBUG_TAPS:
        es.wait("sync", s_st)
        fin = inc(s_st, 16)
        es.sync(lambda sy, fin=fin:
                fin(sy.dma_start(out=g["tap_gate"], in_=g["gate_d"])))


def _dt_path(nc, cp):
    g, es = cp["g"], cp["es"]
    s_wld, s_pe, s_ve, s_se, s_st = (
        cp[k] for k in ["s_wld", "s_pe", "s_ve", "s_se", "s_st"])
    dt = cp["m_dt"]
    acum = cp["m_acum"]
    t1, t2 = cp["m_t1"], cp["m_t2"]
    dtr = cp["m_dtraw"]

    def step(eng, sem, fn):
        es.wait(eng, sem)
        fin = inc(cp[{"vector": "s_ve", "scalar": "s_se"}[eng]])
        getattr(es, eng)(lambda E, fn=fn, fin=fin: fin(fn(E)))

    es.wait("vector", s_ve)
    fin = inc(s_ve)
    es.vector(lambda v, fin=fin: (
        v.memset(cp["m_zeros"][:], 0.0),
        fin(v.tensor_scalar(t1[:, :], dtr[:, XOFF:XOFF + TQ],
                            cp["m_dtb"][:, 0:1], None, ALU.add))))
    step("scalar", s_ve,
         lambda sc: sc.activation(t2[:, :], t1[:, :], AF.Abs))
    step("scalar", s_se,
         lambda sc: sc.activation(t2[:, :], t2[:, :], AF.Exp, scale=-1.0))
    step("scalar", s_se,
         lambda sc: sc.activation(t2[:, :], t2[:, :], AF.Ln, bias=1.0))
    step("vector", s_ve,
         lambda v: v.tensor_scalar(dt[:, :], t1[:, :], 0.0, None, ALU.max))
    es.wait("vector", s_se)
    step("vector", s_ve,
         lambda v: v.tensor_tensor(dt[:, :], dt[:, :], t2[:, :], ALU.add))
    step("vector", s_ve,
         lambda v: v.tensor_scalar(dt[:, :], dt[:, :], 100.0, None, ALU.min))
    es.wait("scalar", s_wld)
    step("scalar", s_se,
         lambda sc: sc.activation(cp["m_A"][:, :], cp["m_Alog"][:, :], AF.Exp))
    es.wait("vector", s_se)
    step("vector", s_ve,
         lambda v: v.tensor_scalar(cp["m_A"][:, :], cp["m_A"][:, :],
                                   -1.0, None, ALU.mult))
    step("vector", s_ve,
         lambda v: v.tensor_scalar(t1[:, :], dt[:, :], cp["m_A"][:, 0:1],
                                   None, ALU.mult))

    es.wait("vector", s_ve)
    fin = inc(s_ve)
    def scans(v, fin=fin):
        ins = None
        for c in range(NCHK):
            ins = v.tensor_tensor_scan(acum[:, c * MCH:(c + 1) * MCH],
                                       cp["m_zeros"][:, :],
                                       t1[:, c * MCH:(c + 1) * MCH],
                                       0.0, ALU.add, ALU.add)
        fin(ins)
    es.vector(scans)
    es.wait("vector", s_ve)
    fin = inc(s_ve)
    def lasts(v, fin=fin):
        ins = None
        for c in range(NCHK):
            ins = v.tensor_copy(cp["m_alast"][:, c:c + 1],
                                acum[:, (c + 1) * MCH - 1:(c + 1) * MCH])
        fin(ins)
    es.vector(lasts)
    es.wait("vector", s_ve)
    fin = inc(s_ve)
    es.vector(lambda v, fin=fin:
              fin(v.reduce_sum(cp["m_adsum"][:, :], cp["m_alast"][:, :],
                               axis=AX.X)))
    es.wait("vector", s_ve)
    fin = inc(s_ve)
    es.vector(lambda v, fin=fin:
              fin(v.tensor_tensor_scan(cp["m_pfx"][:, :], cp["m_zeros"][:, 0:NCHK],
                                       cp["m_alast"][:, :], 0.0, ALU.add, ALU.add)))
    es.wait("scalar", s_ve)
    fin = inc(s_se)
    es.scalar(lambda sc, fin=fin: (
        sc.activation(cp["m_pfx"][:, :], cp["m_pfx"][:, :], AF.Exp),
        fin(sc.activation(cp["m_alast"][:, :], cp["m_alast"][:, :], AF.Exp))))
    es.wait("sync", s_se)
    es.wait("sync", s_ve)
    for fn in [
        lambda sy, fin: fin(sy.dma_start(out=g["acum_d"], in_=acum[:, :])),
        lambda sy, fin: fin(sy.dma_start(
            out=g["alast_d"].rearrange("one (h c) -> (one h) c", h=MH),
            in_=cp["m_alast"][:, :])),
        lambda sy, fin: fin(sy.dma_start(out=g["adsum_in"], in_=cp["m_adsum"][:, :])),
        lambda sy, fin: fin(sy.dma_start(
            out=g["pfx_d"].rearrange("one (h c) -> (one h) c", h=MH),
            in_=cp["m_pfx"][:, :])),
    ]:
        fin = inc(s_st, 16)
        es.sync(lambda sy, fn=fn, fin=fin: fn(sy, fin))
    if "tap_dt" in DEBUG_TAPS:
        fin = inc(s_st, 16)
        es.sync(lambda sy, fin=fin: fin(sy.dma_start(out=g["tap_dt"], in_=dt[:, :])))
    if "tap_acum" in DEBUG_TAPS:
        fin = inc(s_st, 16)
        es.sync(lambda sy, fin=fin:
                fin(sy.dma_start(out=g["tap_acum"], in_=acum[:, :])))

    es.wait("gpsimd", s_st)
    fin = inc(s_wld, 16)
    es.gpsimd(lambda gp, fin=fin:
              fin(gp.dma_start(out=cp["m_alrow"][:, :], in_=g["alast_d"])))
    es.wait("tensor", s_wld)
    es.wait("tensor", s_ve)
    fin = inc(cp["s_pe"])
    es.tensor(lambda t, fin=fin:
              fin(t.matmul(cp["pk"][4][:, 0:MH * NCHK], cp["ones_row_f"][:, :],
                           cp["m_alrow"][0:1, :], start=True, stop=True)))
    es.wait("vector", cp["s_pe"])
    fin = inc(s_ve)
    es.vector(lambda v, fin=fin:
              fin(v.tensor_copy(cp["m_decbc"][:, :], cp["pk"][4][:, 0:MH * NCHK])))
    es.wait("gpsimd", s_ve)
    fin = inc(s_wld, 16)
    es.gpsimd(lambda gp, fin=fin:
              fin(gp.dma_start(out=cp["m_alrow"][:, :], in_=g["pfx_d"])))
    es.wait("tensor", s_wld)
    es.wait("tensor", s_ve)
    fin = inc(cp["s_pe"])
    es.tensor(lambda t, fin=fin:
              fin(t.matmul(cp["pk"][4][:, 0:MH * NCHK], cp["ones_row_f"][:, :],
                           cp["m_alrow"][0:1, :], start=True, stop=True)))
    es.wait("vector", cp["s_pe"])
    fin = inc(s_ve)
    es.vector(lambda v, fin=fin:
              fin(v.tensor_copy(cp["m_pfxbc"][:, :], cp["pk"][4][:, 0:MH * NCHK])))


def _ssd_local_states(nc, cp):
    g, es = cp["g"], cp["es"]
    pk, pb = cp["pk"], cp["pb"]
    s_wld, s_pe, s_ve, s_se, s_cc, s_st = (
        cp[k] for k in ["s_wld", "s_pe", "s_ve", "s_se", "s_cc", "s_st"])
    hbc = cp["m_hbc"]
    acum = cp["m_acum"]
    dt_sb = cp["m_dt"]

    for c in range(NCHK):
        es.wait("gpsimd", s_st)
        es.wait("gpsimd", s_pe)
        fin = inc(s_wld, 16)
        es.gpsimd(lambda gp, c=c, fin=fin: fin(gp.dma_start(
            out=hbc[:].rearrange("p (cc t) -> p cc t", cc=HBC_MT),
            in_=g["hbc_d"][:, c * MCH:(c + 1) * MCH].rearrange(
                "(cc p) t -> p cc t", p=128))))
        es.wait("vector", s_ve)
        es.wait("vector", s_pe)
        fin = inc(s_ve)
        es.vector(lambda v, c=c, fin=fin:
                  fin(v.tensor_scalar(cp["m_ds"][:, :],
                                      acum[:, c * MCH:(c + 1) * MCH],
                                      acum[:, (c + 1) * MCH - 1:(c + 1) * MCH],
                                      None, ALU.subtract)))
        es.wait("scalar", s_ve)
        fin = inc(s_se)
        es.scalar(lambda sc, fin=fin:
                  fin(sc.activation(cp["m_ds"][:, :], cp["m_ds"][:, :],
                                    AF.Exp, scale=-1.0)))
        es.wait("tensor", s_se)
        es.wait("tensor", s_wld)
        es.wait("tensor", s_ve)
        fin = inc(s_pe)
        es.tensor(lambda t, c=c, fin=fin: (
            t.transpose(pk[4][0:128, 0:MH], dt_sb[:, c * MCH: c * MCH + 128],
                        cp["ident_f"][0:MH, 0:MH]),
            t.transpose(pk[5][0:128, 0:MH], dt_sb[:, c * MCH + 128: c * MCH + 256],
                        cp["ident_f"][0:MH, 0:MH]),
            t.transpose(pk[4][0:128, MH:2 * MH], cp["m_ds"][:, 0:128],
                        cp["ident_f"][0:MH, 0:MH]),
            fin(t.transpose(pk[5][0:128, MH:2 * MH], cp["m_ds"][:, 128:256],
                            cp["ident_f"][0:MH, 0:MH]))))
        es.wait("vector", s_pe)
        fin = inc(s_ve)
        es.vector(lambda v, fin=fin: (
            v.tensor_copy(cp["m_dtT"][:, 0:MH], pk[4][:, 0:MH]),
            v.tensor_copy(cp["m_dtT"][:, MH:2 * MH], pk[5][:, 0:MH]),
            v.tensor_copy(cp["m_dsT"][:, 0:MH], pk[4][:, MH:2 * MH]),
            fin(v.tensor_copy(cp["m_dsT"][:, MH:2 * MH], pk[5][:, MH:2 * MH]))))
        fin = inc(s_pe)
        es.tensor(lambda t, fin=fin: (
            t.transpose(pb[:, 0:128], hbc[:, 32 * MCH:32 * MCH + 128],
                        cp["ident_bf"][:, :]),
            fin(t.transpose(pb[:, 128:256], hbc[:, 32 * MCH + 128:32 * MCH + 256],
                            cp["ident_bf"][:, :]))))
        es.wait("vector", s_pe)
        fin = inc(s_ve)
        es.vector(lambda v, fin=fin: (
            v.tensor_copy(cp["m_bT"][:, 0:128], pb[:, 0:128]),
            fin(v.tensor_copy(cp["m_bT"][:, 128:256], pb[:, 128:256]))))

        for h in range(MH):
            es.wait("tensor", s_ve)
            fin = inc(s_pe)
            es.tensor(lambda t, h=h, fin=fin: (
                t.transpose(pb[:, 256:384], hbc[:, h * MCH:h * MCH + 128],
                            cp["ident_bf"][:, :]),
                fin(t.transpose(pb[:, 384:512], hbc[:, h * MCH + 128:h * MCH + 256],
                                cp["ident_bf"][:, :]))))
            es.wait("vector", s_pe)
            fin = inc(s_ve)
            es.vector(lambda v, h=h, fin=fin: (
                v.tensor_scalar(cp["m_xdtT"][:, 0:MHD], pb[:, 256:384],
                                cp["m_dtT"][:, h:h + 1], None, ALU.mult),
                v.tensor_scalar(cp["m_xdtT"][:, MHD:2 * MHD], pb[:, 384:512],
                                cp["m_dtT"][:, MH + h:MH + h + 1], None, ALU.mult),
                v.tensor_scalar(cp["m_bdTh"][:, 0:MST], cp["m_bT"][:, 0:MST],
                                cp["m_dsT"][:, h:h + 1], None, ALU.mult),
                fin(v.tensor_scalar(cp["m_bdTh"][:, MST:2 * MST],
                                    cp["m_bT"][:, MST:2 * MST],
                                    cp["m_dsT"][:, MH + h:MH + h + 1],
                                    None, ALU.mult))))
            es.wait("tensor", s_ve)
            fin = inc(s_pe)
            es.tensor(lambda t, h=h, fin=fin: (
                t.matmul(pk[h % 2][:, 0:MHD], cp["m_bdTh"][:, 0:MST],
                         cp["m_xdtT"][:, 0:MHD], start=True, stop=False),
                fin(t.matmul(pk[h % 2][:, 0:MHD], cp["m_bdTh"][:, MST:2 * MST],
                             cp["m_xdtT"][:, MHD:2 * MHD], start=False, stop=True))))
            es.wait("vector", s_pe)
            fin = inc(s_ve)
            if c == 0:
                es.vector(lambda v, h=h, fin=fin: (
                    v.tensor_copy(cp["m_sstage"][:, h * MHD:(h + 1) * MHD],
                                  pk[h % 2][:, 0:MHD]),
                    fin(v.tensor_scalar(cp["m_srun"][:, h * MHD:(h + 1) * MHD],
                                        pk[h % 2][:, 0:MHD],
                                        cp["m_pfxbc"][:, h * NCHK:h * NCHK + 1],
                                        None, ALU.mult))))
            else:
                es.vector(lambda v, h=h, c=c, fin=fin: (
                    v.tensor_copy(cp["m_sstage"][:, h * MHD:(h + 1) * MHD],
                                  pk[h % 2][:, 0:MHD]),
                    fin(v.scalar_tensor_tensor(
                        cp["m_srun"][:, h * MHD:(h + 1) * MHD],
                        pk[h % 2][:, 0:MHD],
                        cp["m_pfxbc"][:, h * NCHK + c:h * NCHK + c + 1],
                        cp["m_srun"][:, h * MHD:(h + 1) * MHD],
                        ALU.mult, ALU.add))))
        es.wait("sync", s_ve)
        fin = inc(s_st, 16)
        es.sync(lambda sy, c=c, fin=fin: fin(sy.dma_start(
            out=g["sloc_d"][c * INTER:(c + 1) * INTER, :].rearrange(
                "(h n) p -> n h p", n=128),
            in_=cp["m_sstage"][:, :].rearrange("n (h p) -> n h p", h=MH))))

    es.wait("vector", s_ve)
    es.wait("vector", s_st)
    fin = inc(s_ve)
    es.vector(lambda v, fin=fin:
              fin(v.tensor_copy(cp["m_sstage"][:, :], cp["m_srun"][:, :])))
    es.wait("sync", s_ve)
    fin = inc(s_st, 16)
    es.sync(lambda sy, fin=fin: fin(sy.dma_start(
        out=g["stq_in"].rearrange("(h n) p -> n h p", n=128),
        in_=cp["m_sstage"][:, :].rearrange("n (h p) -> n h p", h=MH))))
    if "tap_sloc" in DEBUG_TAPS:
        fin = inc(s_st, 16)
        es.sync(lambda sy, fin=fin:
                fin(sy.dma_start(out=g["tap_sloc"], in_=g["sloc_d"])))
    if "tap_stq" in DEBUG_TAPS:
        es.wait("sync", s_st)
        fin = inc(s_st, 16)
        es.sync(lambda sy, fin=fin:
                fin(sy.dma_start(out=g["tap_stq"], in_=g["stq_in"])))
    if "tap_dec" in DEBUG_TAPS:
        fin = inc(s_st, 16)
        es.sync(lambda sy, fin=fin:
                fin(sy.dma_start(out=g["tap_dec"][:, 0:MH * NCHK],
                                 in_=cp["m_decbc"][:, :])))
        fin = inc(s_st, 16)
        es.sync(lambda sy, fin=fin:
                fin(sy.dma_start(out=g["tap_dec"][:, MH * NCHK:],
                                 in_=cp["m_pfxbc"][:, :])))
    es.wait("gpsimd", s_st)
    fin = inc(cp["s_wld"], 16)
    es.gpsimd(lambda gp, fin=fin: fin(gp.dma_start(
        out=g["lastloc_in"], in_=g["sloc_d"][7 * INTER:8 * INTER, :])))
    es.wait("gpsimd", cp["s_wld"])
    fin = inc(s_cc)
    es.gpsimd(lambda gp, fin=fin: fin(gp.collective_compute(
        "AllGather", ALU.bypass, replica_groups=GROUPS4,
        ins=[g["stq_in"]], outs=[g["stq_out"]])))
    fin = inc(s_cc)
    es.gpsimd(lambda gp, fin=fin: fin(gp.collective_compute(
        "AllGather", ALU.bypass, replica_groups=GROUPS4,
        ins=[g["adsum_in"]], outs=[g["adsum_out"]])))
    fin = inc(s_cc)
    es.gpsimd(lambda gp, fin=fin: fin(gp.collective_compute(
        "AllGather", ALU.bypass, replica_groups=GROUPS4,
        ins=[g["lastloc_in"]], outs=[g["lastloc_out"]])))


def _state_gather_combine(nc, cp):
    g, es = cp["g"], cp["es"]
    pk = cp["pk"]
    s_wld, s_pe, s_ve, s_se, s_cc, s_st = (
        cp[k] for k in ["s_wld", "s_pe", "s_ve", "s_se", "s_cc", "s_st"])

    es.wait("gpsimd", s_cc)
    fin = inc(s_wld, 16)
    es.gpsimd(lambda gp, fin=fin: fin(gp.dma_start(
        out=cp["m_adsq"][:, :],
        in_=g["adsum_out"].rearrange("(q h) one -> h (q one)", h=MH),
        allow_slow_non_contiguous=True)))
    for qq in range(4):
        fin = inc(s_wld, 16)
        es.gpsimd(lambda gp, qq=qq, fin=fin: fin(gp.dma_start(
            out=cp["m_stq"][:, qq * MH * MHD:(qq + 1) * MH * MHD].rearrange(
                "n (h p) -> n h p", h=MH),
            in_=g["stq_out"][qq * INTER:(qq + 1) * INTER, :].rearrange(
                "(h n) p -> n h p", n=128))))
    es.wait("vector", s_wld)
    for m in range(4):
        fin = inc(s_ve)
        es.vector(lambda v, m=m, fin=fin:
                  fin(v.tensor_scalar(cp["m_wexp"][:, m:m + 1],
                                      cp["m_adsq"][:, 0:1],
                                      cp["m_scmask"][:, 4 * m:4 * m + 1],
                                      None, ALU.mult)))
        for k in range(1, 4):
            es.wait("vector", s_ve)
            fin = inc(s_ve)
            es.vector(lambda v, m=m, k=k, fin=fin:
                      fin(v.scalar_tensor_tensor(
                          cp["m_wexp"][:, m:m + 1], cp["m_adsq"][:, k:k + 1],
                          cp["m_scmask"][:, 4 * m + k:4 * m + k + 1],
                          cp["m_wexp"][:, m:m + 1], ALU.mult, ALU.add)))
        es.wait("vector", s_ve)
    fin = inc(s_se)
    es.scalar(lambda sc, fin=fin:
              fin(sc.activation(cp["m_wexp"][:, :], cp["m_wexp"][:, :], AF.Exp)))
    es.wait("vector", s_se)
    fin = inc(s_ve)
    es.vector(lambda v, fin=fin:
              fin(v.tensor_tensor(cp["m_wexp"][:, :], cp["m_wexp"][:, :],
                                  cp["m_qmask"][:, :], ALU.mult)))
    es.wait("sync", s_ve)
    fin = inc(s_st, 16)
    es.sync(lambda sy, fin=fin: fin(sy.dma_start(
        out=g["wbase_d"].rearrange("one (h q) -> (one h) q", h=MH),
        in_=cp["m_wexp"][:, :])))
    es.wait("gpsimd", s_st)
    fin = inc(s_wld, 16)
    es.gpsimd(lambda gp, fin=fin:
              fin(gp.dma_start(out=cp["m_wrow"][:, :], in_=g["wbase_d"])))
    es.wait("tensor", s_wld)
    es.wait("tensor", s_ve)
    fin = inc(s_pe)
    es.tensor(lambda t, fin=fin:
              fin(t.matmul(pk[4][:, 0:4 * MH], cp["ones_row_f"][:, :],
                           cp["m_wrow"][0:1, :], start=True, stop=True)))
    es.wait("vector", s_pe)
    fin = inc(s_ve)
    es.vector(lambda v, fin=fin:
              fin(v.tensor_copy(cp["m_wbc"][:, :], pk[4][:, 0:4 * MH])))
    es.wait("vector", s_ve)
    def partials(dst, m, accumulate):
        fin = inc(s_ve)
        def body(v, dst=dst, m=m, accumulate=accumulate, fin=fin):
            ins = None
            for h in range(MH):
                if accumulate:
                    ins = v.scalar_tensor_tensor(
                        cp[dst][:, h * MHD:(h + 1) * MHD],
                        cp["m_stq"][:, (m * MH + h) * MHD:
                                    (m * MH + h + 1) * MHD],
                        cp["m_wbc"][:, h * 4 + m:h * 4 + m + 1],
                        cp[dst][:, h * MHD:(h + 1) * MHD], ALU.mult, ALU.add)
                else:
                    ins = v.tensor_scalar(
                        cp[dst][:, h * MHD:(h + 1) * MHD],
                        cp["m_stq"][:, (m * MH + h) * MHD:
                                    (m * MH + h + 1) * MHD],
                        cp["m_wbc"][:, h * 4 + m:h * 4 + m + 1], None, ALU.mult)
            fin(ins)
        es.vector(body)

    partials("m_srun", 0, False)
    partials("m_prev", 1, False)
    es.wait("vector", s_ve)
    partials("m_srun", 2, True)
    partials("m_prev", 3, True)
    es.wait("vector", s_ve)
    fin = inc(s_ve)
    es.vector(lambda v, fin=fin:
              fin(v.tensor_tensor(cp["m_srun"][:, :], cp["m_srun"][:, :],
                                  cp["m_prev"][:, :], ALU.add)))
    es.wait("vector", s_ve)

    es.wait("sync", s_ve)
    fin = inc(s_st, 16)
    es.sync(lambda sy, fin=fin: fin(sy.dma_start(
        out=g["pvrow_d"].rearrange("one (h q) -> (one h) q", h=MH),
        in_=cp["m_pmask"][:, :])))
    es.wait("gpsimd", s_st)
    fin = inc(s_wld, 16)
    es.gpsimd(lambda gp, fin=fin:
              fin(gp.dma_start(out=cp["m_wrow"][:, :], in_=g["pvrow_d"])))
    for qq in range(4):
        fin = inc(s_wld, 16)
        es.gpsimd(lambda gp, qq=qq, fin=fin: fin(gp.dma_start(
            out=cp["m_stq"][:, qq * MH * MHD:(qq + 1) * MH * MHD].rearrange(
                "n (h p) -> n h p", h=MH),
            in_=g["lastloc_out"][qq * INTER:(qq + 1) * INTER, :].rearrange(
                "(h n) p -> n h p", n=128))))
    es.wait("tensor", s_wld)
    es.wait("tensor", s_ve)
    fin = inc(s_pe)
    es.tensor(lambda t, fin=fin:
              fin(t.matmul(pk[4][:, 0:4 * MH], cp["ones_row_f"][:, :],
                           cp["m_wrow"][0:1, :], start=True, stop=True)))
    es.wait("vector", s_pe)
    fin = inc(s_ve)
    es.vector(lambda v, fin=fin:
              fin(v.tensor_copy(cp["m_wbc"][:, :], pk[4][:, 0:4 * MH])))
    es.wait("vector", s_ve)
    for m in range(4):
        if m == 0:
            fin = inc(s_ve)
            def p0(v, fin=fin):
                ins = None
                for h in range(MH):
                    ins = v.tensor_scalar(
                        cp["m_prev"][:, h * MHD:(h + 1) * MHD],
                        cp["m_stq"][:, (0 * MH + h) * MHD:
                                    (0 * MH + h + 1) * MHD],
                        cp["m_wbc"][:, h * 4 + 0:h * 4 + 1], None, ALU.mult)
                fin(ins)
            es.vector(p0)
        else:
            es.wait("vector", s_ve)
            fin = inc(s_ve)
            def pacc(v, m=m, fin=fin):
                ins = None
                for h in range(MH):
                    ins = v.scalar_tensor_tensor(
                        cp["m_prev"][:, h * MHD:(h + 1) * MHD],
                        cp["m_stq"][:, (m * MH + h) * MHD:
                                    (m * MH + h + 1) * MHD],
                        cp["m_wbc"][:, h * 4 + m:h * 4 + m + 1],
                        cp["m_prev"][:, h * MHD:(h + 1) * MHD],
                        ALU.mult, ALU.add)
                fin(ins)
            es.vector(pacc)
    es.wait("vector", s_ve)


def _state_backward(nc, cp):
    g, es = cp["g"], cp["es"]
    s_wld, s_ve, s_st = cp["s_wld"], cp["s_ve"], cp["s_st"]

    def load_sloc(c, dst):
        es.wait("gpsimd", s_ve)
        fin = inc(s_wld, 16)
        es.gpsimd(lambda gp, c=c, dst=dst, fin=fin: fin(gp.dma_start(
            out=cp[dst][:].rearrange("n (h p) -> n h p", h=MH),
            in_=g["sloc_d"][c * INTER:(c + 1) * INTER, :].rearrange(
                "(h n) p -> n h p", n=128))))

    def store_sin(c):
        es.wait("sync", s_ve)
        fin = inc(s_st, 16)
        es.sync(lambda sy, c=c, fin=fin: fin(sy.dma_start(
            out=g["sin_d"][c * INTER:(c + 1) * INTER, :].rearrange(
                "(h n) p -> n h p", n=128),
            in_=cp["m_sstage"][:, :].rearrange("n (h p) -> n h p", h=MH))))

    bufs = ["m_slocc", "m_slocp"]
    load_sloc(7, bufs[7 % 2])
    for c in range(7, -1, -1):
        loc = bufs[c % 2]
        es.wait("vector", s_wld)
        es.wait("vector", s_ve)
        if c < 7:
            fin = inc(s_ve)
            es.vector(lambda v, loc=loc, fin=fin:
                      fin(v.tensor_tensor(cp["m_sstage"][:, :], cp[loc][:, :],
                                          cp["m_srun"][:, :], ALU.add)))
            store_sin(c + 1)
        es.wait("vector", s_ve)
        fin = inc(s_ve)
        es.vector(lambda v, loc=loc, fin=fin:
                  fin(v.tensor_tensor(cp["m_srun"][:, :], cp[loc][:, :],
                                      cp["m_srun"][:, :], ALU.add)))
        es.wait("vector", s_ve)
        fin = inc(s_ve)
        def scale_h(v, c=c, fin=fin):
            ins = None
            for h in range(MH):
                ins = v.tensor_scalar(
                    cp["m_srun"][:, h * MHD:(h + 1) * MHD],
                    cp["m_srun"][:, h * MHD:(h + 1) * MHD],
                    cp["m_decbc"][:, h * NCHK + c:h * NCHK + c + 1],
                    None, ALU.mult)
            fin(ins)
        es.vector(scale_h)
        if c > 0:
            load_sloc(c - 1, bufs[(c - 1) % 2])
    es.wait("vector", s_ve)
    fin = inc(s_ve)
    es.vector(lambda v, fin=fin:
              fin(v.tensor_tensor(cp["m_sstage"][:, :], cp["m_prev"][:, :],
                                  cp["m_srun"][:, :], ALU.add)))
    store_sin(0)
    if "tap_sin" in DEBUG_TAPS:
        es.wait("sync", s_st)
        fin = inc(s_st, 16)
        es.sync(lambda sy, fin=fin:
                fin(sy.dma_start(out=g["tap_sin"], in_=g["sin_d"])))


def _ssd_main(nc, cp):
    g, es = cp["g"], cp["es"]
    pk, pb = cp["pk"], cp["pb"]
    s_wld, s_xld, s_pe, s_ve, s_se, s_ge, s_st = (
        cp[k] for k in ["s_wld", "s_xld", "s_pe", "s_ve", "s_se", "s_ge", "s_st"])
    hbc = cp["m_hbc"]
    dt_sb = cp["m_dt"]
    acum = cp["m_acum"]

    NBLK = TQ // 1024
    for blk_i in range(NBLK):
        for ci in range(4):
            c = blk_i * 4 + ci
            es.wait("gpsimd", s_pe)
            es.wait("gpsimd", s_ve)
            es.wait("gpsimd", s_se)
            fin = inc(s_wld, 16)
            es.gpsimd(lambda gp, c=c, fin=fin: fin(gp.dma_start(
                out=hbc[:].rearrange("p (cc t) -> p cc t", cc=HBC_MT),
                in_=g["hbc_d"][:, c * MCH:(c + 1) * MCH].rearrange(
                    "(cc p) t -> p cc t", p=128))))
            fin = inc(s_wld, 16)
            es.gpsimd(lambda gp, c=c, fin=fin: fin(gp.dma_start(
                out=cp["m_acrow"][:].rearrange("one (h t) -> one h t", h=MH // 2),
                in_=g["acum_d"].rearrange("(one h) t -> one h t", one=1)[
                    0:1, 0:MH // 2, c * MCH:(c + 1) * MCH])))
            es.wait("gpsimd", s_ve)
            fin = inc(s_wld, 16)
            es.gpsimd(lambda gp, c=c, fin=fin: fin(gp.dma_start(
                out=cp["m_sinc"][:].rearrange("n (h p) -> n h p", h=MH),
                in_=g["sin_d"][c * INTER:(c + 1) * INTER, :].rearrange(
                    "(h n) p -> n h p", n=128))))
            fin = inc(s_wld, 16)
            es.gpsimd(lambda gp, c=c, fin=fin: fin(gp.dma_start(
                out=cp["m_gatec"][:, :].rearrange("p (hh i) -> p hh i", hh=2),
                in_=g["gate_d"][c * MCH:(c + 1) * MCH, :].rearrange(
                    "(hh p) i -> p hh i", p=128))))

            es.wait("tensor", s_wld)
            es.wait("tensor", s_ve)
            fin = inc(s_pe)
            es.tensor(lambda t, c=c, fin=fin: (
                t.transpose(pk[4][0:128, 0:MH], dt_sb[:, c * MCH: c * MCH + 128],
                            cp["ident_f"][0:MH, 0:MH]),
                t.transpose(pk[5][0:128, 0:MH],
                            dt_sb[:, c * MCH + 128: c * MCH + 256],
                            cp["ident_f"][0:MH, 0:MH]),
                t.transpose(pk[4][0:128, MH:2 * MH],
                            acum[:, c * MCH: c * MCH + 128],
                            cp["ident_f"][0:MH, 0:MH]),
                fin(t.transpose(pk[5][0:128, MH:2 * MH],
                                acum[:, c * MCH + 128: c * MCH + 256],
                                cp["ident_f"][0:MH, 0:MH]))))
            es.wait("vector", s_pe)
            fin = inc(s_ve)
            es.vector(lambda v, fin=fin: (
                v.tensor_copy(cp["m_dtT"][:, 0:MH], pk[4][:, 0:MH]),
                v.tensor_copy(cp["m_dtT"][:, MH:2 * MH], pk[5][:, 0:MH]),
                v.tensor_copy(cp["m_acT"][:, 0:MH], pk[4][:, MH:2 * MH]),
                fin(v.tensor_copy(cp["m_acT"][:, MH:2 * MH], pk[5][:, MH:2 * MH]))))
            fin = inc(s_pe)
            es.tensor(lambda t, fin=fin: (
                t.transpose(pb[:, 0:128], hbc[:, 32 * MCH:32 * MCH + 128],
                            cp["ident_bf"][:, :]),
                fin(t.transpose(pb[:, 128:256], hbc[:, 32 * MCH + 128:
                                                    32 * MCH + 256],
                                cp["ident_bf"][:, :]))))
            es.wait("vector", s_pe)
            fin = inc(s_ve)
            es.vector(lambda v, fin=fin: (
                v.tensor_copy(cp["m_bT"][:, 0:128], pb[:, 0:128]),
                fin(v.tensor_copy(cp["m_bT"][:, 128:256], pb[:, 128:256]))))
            fin = inc(s_pe)
            es.tensor(lambda t, fin=fin: (
                t.matmul(pk[2][:, 0:MCH], hbc[:, 32 * MCH:32 * MCH + 128],
                         hbc[:, 33 * MCH:34 * MCH], start=True, stop=True),
                fin(t.matmul(pk[3][:, 0:MCH], hbc[:, 32 * MCH + 128:32 * MCH + 256],
                             hbc[:, 33 * MCH:34 * MCH], start=True, stop=True))))
            es.wait("vector", s_pe)
            fin = inc(s_ve)
            es.vector(lambda v, fin=fin: (
                v.tensor_tensor(cp["m_gm"][:, 0:MCH], pk[2][:, 0:MCH],
                                cp["ssd_mask"][:, 0:MCH], ALU.mult),
                fin(v.tensor_tensor(cp["m_gm"][:, MCH:2 * MCH], pk[3][:, 0:MCH],
                                    cp["ssd_mask"][:, MCH:2 * MCH], ALU.mult))))

            for h in range(MH):
                if h == MH // 2:
                    es.wait("gpsimd", s_pe)
                    fin = inc(s_wld, 16)
                    es.gpsimd(lambda gp, c=c, fin=fin: fin(gp.dma_start(
                        out=cp["m_acrow"][:].rearrange(
                            "one (h t) -> one h t", h=MH // 2),
                        in_=g["acum_d"].rearrange(
                            "(one h) t -> one h t", one=1)[
                            0:1, MH // 2:MH, c * MCH:(c + 1) * MCH])))
                    es.wait("tensor", s_wld)
                hh = h % (MH // 2)
                es.wait("tensor", s_ve)
                es.wait("tensor", s_se)
                fin = inc(s_pe)
                es.tensor(lambda t, h=h, hh=hh, fin=fin: (
                    t.transpose(pb[:, 256:384], hbc[:, h * MCH:h * MCH + 128],
                                cp["ident_bf"][:, :]),
                    t.transpose(pb[:, 384:512], hbc[:, h * MCH + 128:
                                                    h * MCH + 256],
                                cp["ident_bf"][:, :]),
                    fin(t.matmul(pk[4][:, 0:MCH], cp["ones_row_f"][:, :],
                                 cp["m_acrow"][0:1, hh * MCH:(hh + 1) * MCH],
                                 start=True, stop=True))))
                es.wait("vector", s_pe)
                fin = inc(s_ve)
                es.vector(lambda v, h=h, fin=fin: (
                    v.tensor_copy(cp["m_xrawT"][:, 0:MHD], pb[:, 256:384]),
                    v.tensor_copy(cp["m_xrawT"][:, MHD:2 * MHD], pb[:, 384:512]),
                    v.tensor_scalar(cp["m_xdtT"][:, 0:MHD], pb[:, 256:384],
                                    cp["m_dtT"][:, h:h + 1], None, ALU.mult),
                    v.tensor_scalar(cp["m_xdtT"][:, MHD:2 * MHD], pb[:, 384:512],
                                    cp["m_dtT"][:, MH + h:MH + h + 1],
                                    None, ALU.mult),
                    v.tensor_scalar(cp["m_eh"][:, 0:MCH], pk[4][:, 0:MCH],
                                    cp["m_acT"][:, h:h + 1], 0.0,
                                    ALU.subtract, ALU.min),
                    fin(v.tensor_scalar(cp["m_eh"][:, MCH:2 * MCH],
                                        pk[4][:, 0:MCH],
                                        cp["m_acT"][:, MH + h:MH + h + 1], 0.0,
                                        ALU.subtract, ALU.min))))
                es.wait("scalar", s_ve)
                fin = inc(s_se)
                es.scalar(lambda sc, h=h, fin=fin: (
                    sc.activation(cp["m_eh"][:, 0:2 * MCH], cp["m_eh"][:, 0:2 * MCH],
                                  AF.Exp),
                    fin(sc.activation(cp["m_sdbc"][:, 0:MCH], pk[4][:, 0:MCH],
                                      AF.Exp))))
                es.wait("vector", s_se)
                fin = inc(s_ve)
                es.vector(lambda v, h=h, fin=fin: (
                    v.tensor_tensor(cp["m_mh"][:, 0:2 * MCH], cp["m_gm"][:, 0:2 * MCH],
                                    cp["m_eh"][:, 0:2 * MCH], ALU.mult),
                    fin(v.tensor_tensor(cp["m_csc"][:, 0:MCH],
                                        hbc[:, 33 * MCH:34 * MCH],
                                        cp["m_sdbc"][:, 0:MCH], ALU.mult))))
                es.wait("tensor", s_ve)
                fin = inc(s_pe)
                def ymm(t, h=h, fin=fin):
                    ins = None
                    for lh in range(2):
                        bank = pk[lh]
                        t.matmul(bank[:, 0:MHD],
                                 cp["m_mh"][:, lh * 128:lh * 128 + 128],
                                 cp["m_xdtT"][:, 0:MHD], start=True, stop=False)
                        t.matmul(bank[:, 0:MHD],
                                 cp["m_mh"][:, MCH + lh * 128:MCH + lh * 128 + 128],
                                 cp["m_xdtT"][:, MHD:2 * MHD],
                                 start=False, stop=False)
                        ins = t.matmul(bank[:, 0:MHD],
                                       cp["m_csc"][:, lh * 128:lh * 128 + 128],
                                       cp["m_sinc"][:, h * MHD:(h + 1) * MHD],
                                       start=False, stop=True)
                    fin(ins)
                es.tensor(ymm)
                es.wait("vector", s_pe)
                fin = inc(s_ve)
                es.vector(lambda v, h=h, fin=fin: (
                    v.scalar_tensor_tensor(
                        cp["m_ych"][:, 0 * INTER + h * MHD:0 * INTER + (h + 1) * MHD],
                        cp["m_xrawT"][:, 0:MHD], cp["D_bc"][:, h:h + 1],
                        pk[0][:, 0:MHD], ALU.mult, ALU.add),
                    fin(v.scalar_tensor_tensor(
                        cp["m_ych"][:, 1 * INTER + h * MHD:1 * INTER + (h + 1) * MHD],
                        cp["m_xrawT"][:, MHD:2 * MHD], cp["D_bc"][:, h:h + 1],
                        pk[1][:, 0:MHD], ALU.mult, ALU.add))))
                es.wait("vector", s_ve)
                fin = inc(s_ve)
                es.vector(lambda v, h=h, fin=fin: (
                    v.tensor_tensor(
                        cp["m_ych"][:, 0 * INTER + h * MHD:0 * INTER + (h + 1) * MHD],
                        cp["m_ych"][:, 0 * INTER + h * MHD:0 * INTER + (h + 1) * MHD],
                        cp["m_gatec"][:, 0 * INTER + h * MHD:
                                      0 * INTER + (h + 1) * MHD], ALU.mult),
                    fin(v.tensor_tensor(
                        cp["m_ych"][:, 1 * INTER + h * MHD:1 * INTER + (h + 1) * MHD],
                        cp["m_ych"][:, 1 * INTER + h * MHD:1 * INTER + (h + 1) * MHD],
                        cp["m_gatec"][:, 1 * INTER + h * MHD:
                                      1 * INTER + (h + 1) * MHD], ALU.mult))))

            if "tap_yg" in DEBUG_TAPS:
                es.wait("sync", s_ve)
                for lh in range(2):
                    fin = inc(s_st, 16)
                    es.sync(lambda sy, c=c, lh=lh, fin=fin: fin(sy.dma_start(
                        out=g["tap_yg"][c * MCH + lh * 128:
                                        c * MCH + lh * 128 + 128, :],
                        in_=cp["m_ych"][:, lh * INTER:(lh + 1) * INTER])))
                es.wait("vector", s_st)
            for lh in range(2):
                es.wait("scalar", s_ve)
                fin = inc(s_se)
                es.scalar(lambda sc, lh=lh, fin=fin:
                          fin(sc.activation(cp["m_gatec"][:, lh * INTER:
                                                          (lh + 1) * INTER],
                                            cp["m_ych"][:, lh * INTER:
                                                        (lh + 1) * INTER],
                                            AF.Square)))
                es.wait("vector", s_se)
                fin = inc(s_ve)
                es.vector(lambda v, lh=lh, fin=fin:
                          fin(v.reduce_sum(cp["m_rsum"][:, :],
                                           cp["m_gatec"][:, lh * INTER:
                                                         (lh + 1) * INTER],
                                           axis=AX.X)))
                es.wait("scalar", s_ve)
                fin = inc(s_se)
                es.scalar(lambda sc, fin=fin:
                          fin(sc.activation(cp["m_rsum"][:, :], cp["m_rsum"][:, :],
                                            AF.Sqrt, scale=1.0 / INTER, bias=1e-5)))
                es.wait("vector", s_se)
                fin = inc(s_ve)
                es.vector(lambda v, fin=fin:
                          fin(v.reciprocal(cp["m_rstd"][:, :], cp["m_rsum"][:, :])))
                es.wait("vector", s_ve)
                fin = inc(s_ve)
                es.vector(lambda v, lh=lh, fin=fin:
                          fin(v.tensor_scalar(cp["m_ych"][:, lh * INTER:
                                                          (lh + 1) * INTER],
                                              cp["m_ych"][:, lh * INTER:
                                                          (lh + 1) * INTER],
                                              cp["m_rstd"][:, 0:1], None, ALU.mult)))
                es.wait("vector", s_ve)
                fin = inc(s_ve)
                es.vector(lambda v, lh=lh, fin=fin:
                          fin(v.tensor_tensor(cp["m_ych"][:, lh * INTER:
                                                          (lh + 1) * INTER],
                                              cp["m_ych"][:, lh * INTER:
                                                          (lh + 1) * INTER],
                                              cp["normw_bc"][:, :], ALU.mult)))
                tcol = ci * MCH + lh * 128
                for h4 in range(0, INTER // 128, 4):
                    es.wait("tensor", s_ve)
                    es.wait("tensor", s_se)
                    fin = inc(s_pe)
                    def tr4(t, lh=lh, h4=h4, fin=fin):
                        ins = None
                        for j in range(4):
                            ins = t.transpose(
                                pb[:, j * 128:(j + 1) * 128],
                                cp["m_ych"][:, lh * INTER + (h4 + j) * 128:
                                            lh * INTER + (h4 + j + 1) * 128],
                                cp["ident_bf"][:, :])
                        fin(ins)
                    es.tensor(tr4)
                    es.wait("scalar", s_pe)
                    fin = inc(s_se)
                    def ev4(sc, h4=h4, tcol=tcol, fin=fin):
                        ins = None
                        for j in range(4):
                            ins = sc.copy(
                                cp["m_ynf"][:, (h4 + j) * 1024 + tcol:
                                            (h4 + j) * 1024 + tcol + 128],
                                pb[:, j * 128:(j + 1) * 128])
                        fin(ins)
                    es.scalar(ev4)

        pe_after_mt = []
        for mt in range(FC):
            buf = mt % 2
            if mt >= 2:
                es.wait("gpsimd", s_pe, pe_after_mt[mt - 1])
            es.wait("gpsimd", s_ve)
            es.wait("gpsimd", s_se)
            fin = inc(s_wld, 16)
            es.gpsimd(lambda gp, mt=mt, buf=buf, fin=fin: fin(gp.dma_start(
                out=cp["m_ych"][:, buf * 4096:(buf + 1) * 4096].rearrange(
                    "p (kc m) -> p kc m", kc=INTER // 128),
                in_=g["w_outproj_T"][:, mt * 128:(mt + 1) * 128].rearrange(
                    "(kc p) m -> p kc m", p=128))))
            for n2 in range(2):
                tok0 = blk_i * 1024 + n2 * 512
                es.wait("sync", s_ve)
                fin = inc(s_xld, 16)
                es.sync(lambda sy, mt=mt, tok0=tok0, fin=fin: fin(sy.dma_start(
                    out=cp["m_eh"][:, 0:512],
                    in_=g["x_f"][mt * 128:(mt + 1) * 128,
                                 XOFF + tok0:XOFF + tok0 + 512])))
                es.wait("tensor", s_wld)
                es.wait("tensor", s_ve)
                es.wait("tensor", s_se)
                fin = inc(s_pe)
                def opmm(t, mt=mt, n2=n2, buf=buf, blk_i=blk_i, fin=fin):
                    bank = pk[2 + n2]
                    ins = None
                    for kc in range(INTER // 128):
                        ins = t.matmul(
                            bank[:, 0:512],
                            cp["m_ych"][:, buf * 4096 + kc * 128:
                                        buf * 4096 + (kc + 1) * 128],
                            cp["m_ynf"][:, kc * 1024 + n2 * 512:
                                        kc * 1024 + n2 * 512 + 512],
                            start=(kc == 0), stop=(kc == INTER // 128 - 1))
                    fin(ins)
                es.tensor(opmm)
                es.wait("vector", s_pe)
                es.wait("vector", s_xld)
                fin = inc(s_ve)
                es.vector(lambda v, n2=n2, fin=fin:
                          fin(v.tensor_tensor(cp["m_eh"][:, 0:512],
                                              cp["m_eh"][:, 0:512],
                                              pk[2 + n2][:, 0:512], ALU.add)))
                es.wait("sync", s_ve)
                fin = inc(s_st, 16)
                es.sync(lambda sy, mt=mt, tok0=tok0, fin=fin: fin(sy.dma_start(
                    out=g["x_res"][mt * 128:(mt + 1) * 128, tok0:tok0 + 512],
                    in_=cp["m_eh"][:, 0:512])))
                es.wait("vector", s_st)
            pe_after_mt.append(s_pe.n)


def stage_attn_block(nc, ctx, cp, kind, call):
    g, es, nc_ = cp["g"], cp["es"], nc
    pk, pb = cp["pk"], cp["pb"]
    s_wld, s_xld, s_pe, s_ve, s_se, s_ge, s_cc, s_st = (
        cp[k] for k in ["s_wld", "s_xld", "s_pe", "s_ve", "s_se", "s_ge",
                        "s_cc", "s_st"])
    tag = f"{kind}{call}"
    ffn = kind == "ffn"

    with ExitStack() as sctx:
        def SSB(name, shape, dtype=F32):
            t = sctx.enter_context(nc.sbuf_tensor(f"sb_{tag}_{name}",
                                                  list(shape), dtype))
            cp["a_" + name] = t
            return t

        if ffn:
            SSB("m0", [1, TQ], BF16); SSB("m1", [1, TQ], BF16)
            SSB("m2", [1, TQ], BF16)
        cc_chunk = [0, 0, 0, 0]

        if ffn:
            SSB("ln1s", [128, FC]); SSB("ln1b", [128, FC])
            SSB("ln2s", [128, FC]); SSB("ln2b", [128, FC])
            SSB("sbc", [128, FC])
            for nm, srcn in [("ln1s", "ln1_scale_c"), ("ln1b", "ln1_bias_c"),
                             ("ln2s", "ln2_scale_c"), ("ln2b", "ln2_bias_c"),
                             ("sbc", "simple_bias_c")]:
                fin = inc(s_wld, 16)
                es.gpsimd(lambda gp, nm=nm, srcn=srcn, fin=fin:
                          fin(gp.dma_start(out=cp["a_" + nm][:], in_=g[srcn])))

        with ExitStack() as c0:
            lnx = c0.enter_context(
                nc.sbuf_tensor(f"sb_{tag}_lnx", [128, FC * TQ], BF16))
            cp["lnsq"] = c0.enter_context(
                nc.sbuf_tensor(f"sb_{tag}_lnsq", [128, FC * TQ], BF16))
            if ffn:
                for nm, shape, dt_ in [("wr", [128, FC * 3], BF16),
                                       ("wdn", [128, FC * 32], BF16),
                                       ("wup", [32, D], BF16),
                                       ("pst0", [128, 512], BF16),
                                       ("pst1", [128, 512], BF16),
                                       ("z", [32, TQ], BF16),
                                       ("l0", [1, TQ], F32),
                                       ("l1", [1, TQ], F32),
                                       ("l2", [1, TQ], F32)]:
                    cp["a_" + nm] = c0.enter_context(nc.sbuf_tensor(
                        f"sb_{tag}0_{nm}", list(shape), dt_))
            es.wait("gpsimd", s_st)
            fin = inc(s_wld, 16)
            es.gpsimd(lambda gp, fin=fin: fin(gp.dma_start(
                out=lnx[:].rearrange("p (cc t) -> p cc t", cc=FC),
                in_=g["x_res"].rearrange("(cc p) t -> p cc t", p=128))))
            es.flush(f"{tag}_lnload")
            emit_ln(nc, cp, lnx, TQ, lnx, cp["lnsq"])
            if ffn:
                es.wait("sync", s_ve)
                fin = inc(s_st, 16)
                es.sync(lambda sy, fin=fin: fin(sy.dma_start(
                    out=g["xn_loc"].rearrange("(cc p) t -> p cc t", p=128),
                    in_=lnx[:].rearrange("p (cc t) -> p cc t", cc=FC))))
                _router_lowrank(nc, cp, lnx)
                emit_ln(nc, cp, lnx, TQ, lnx, cp["lnsq"],
                        gamma=cp["a_ln1s"], beta=cp["a_ln1b"])
            es.wait("sync", s_ve)
            for tch in range(4):
                fin = inc(s_st, 16)
                es.sync(lambda sy, tch=tch, fin=fin: fin(sy.dma_start(
                    out=g[f"xn_ag_{tch}"].rearrange("(cc p) t -> p cc t", p=128),
                    in_=lnx[:].rearrange("p (cc t) -> p cc t", cc=FC)[
                        :, :, tch * 512:(tch + 1) * 512])))
            es.wait("gpsimd", s_st)
            for tch in range(4):
                fin = inc(s_cc)
                es.gpsimd(lambda gp, tch=tch, fin=fin: fin(gp.collective_compute(
                    "AllGather", ALU.bypass, replica_groups=GROUPS4,
                    ins=[g[f"xn_ag_{tch}"]], outs=[g[f"ag_out_{tch}"]])))
                cc_chunk[tch] = s_cc.n
            es.flush(f"{tag}_ln_ag")

        with ExitStack() as c2:
            def S2(name, shape, dtype=F32):
                t = c2.enter_context(nc.sbuf_tensor(f"sb_{tag}2_{name}",
                                                    list(shape), dtype))
                cp["a_" + name] = t
                return t
            S2("wslab", [128, FC * 512], BF16)
            S2("agb", [128, FC * 512], BF16)
            S2("pstage0", [128, 512], BF16)
            S2("pstage1", [128, 512], BF16)
            if not ffn:
                S2("bqv", [1, 512], BF16)
                S2("bqkv", [128, 12])
                fin = inc(s_wld, 16)
                es.gpsimd(lambda gp, fin=fin:
                          fin(gp.dma_start(out=cp["a_bqv"][:], in_=g["bqv_row"])))
                fin = inc(s_wld, 16)
                es.gpsimd(lambda gp, fin=fin:
                          fin(gp.dma_start(out=cp["a_bqkv"][:], in_=g["bqkv_hg"])))
            _qkv_proj(nc, cp, kind, cc_chunk)
            es.flush(f"{tag}_qkv")

        with ExitStack() as c3:
            def S3(name, shape, dtype=F32):
                t = c3.enter_context(nc.sbuf_tensor(f"sb_{tag}3_{name}",
                                                    list(shape), dtype))
                cp["a_" + name] = t
                return t
            S3("mask01", [128, 8 * 1024], BF16)
            fin = inc(s_wld, 16)
            es.gpsimd(lambda gp, fin=fin:
                      fin(gp.dma_start(out=cp["a_mask01"][:], in_=g["mask01"])))
            for b in range(2):
                S3(f"kh{b}", [128, S], BF16)
                S3(f"vh{b}", [128, S], BF16)
                S3(f"qh{b}", [128, S], BF16)
            S3("pt0", [128, 1024], BF16)
            S3("pt1", [128, 1024], BF16)
            S3("accf", [128, 1024])
            S3("osb", [128, 1024])
            S3("ctxs", [128, 1024], BF16)
            _flash(nc, cp)
            es.flush(f"{tag}_flash")

        with ExitStack() as c4:
            def S4(name, shape, dtype=F32):
                t = c4.enter_context(nc.sbuf_tensor(f"sb_{tag}4_{name}",
                                                    list(shape), dtype))
                cp["a_" + name] = t
                return t
            S4("woslab", [128, 4 * D], BF16)
            S4("ctx4", [128, 4 * 512], BF16)
            S4("dstage0", [128, 512], BF16)
            S4("dstage1", [128, 512], BF16)
            _wo_partial(nc, cp, kind)
            fin = inc(s_cc)
            es.wait("gpsimd", s_st)
            es.gpsimd(lambda gp, fin=fin: fin(gp.collective_compute(
                "ReduceScatter", ALU.add, replica_groups=GROUPS4,
                ins=[g["rs_in2"]], outs=[g["rs_out2"]])))
            es.flush(f"{tag}_wo")

        if not ffn:
            _attnb_epilogue(nc, cp)
            es.flush(f"{tag}_epi")
        else:
            with ExitStack() as c5:
                lnx2 = c5.enter_context(
                    nc.sbuf_tensor(f"sb_{tag}5_lnx", [128, FC * TQ], BF16))
                with ExitStack() as c5b:
                    cp["lnsq"] = c5b.enter_context(
                        nc.sbuf_tensor(f"sb_{tag}5_lnsq", [128, FC * TQ], BF16))
                    _ffn_xtl(nc, cp, lnx2)
                    emit_ln(nc, cp, lnx2, TQ, lnx2, cp["lnsq"],
                            gamma=cp["a_ln2s"], beta=cp["a_ln2b"])
                    es.flush(f"{tag}_xtl")
                def S5(name, shape, dtype=F32):
                    t = c5.enter_context(nc.sbuf_tensor(f"sb_{tag}5_{name}",
                                                        list(shape), dtype))
                    cp["a_" + name] = t
                    return t
                S5("hsb", [128, (FF // 128) * 512], BF16)
                S5("fslab0", [128, FC * 128], BF16)
                S5("fslab1", [128, FC * 128], BF16)
                S5("f2slab", [128, (FF // 128) * 128], BF16)
                S5("cmb0", [128, 512], BF16)
                S5("cmb1", [128, 512], BF16)
                S5("cmb2", [128, 512], BF16)
                S5("cx", [128, 512])
                S5("ct0", [128, 512], BF16)
                S5("ct1", [128, 512], BF16)
                S5("ct2", [128, 512], BF16)
                S5("ct0b", [128, 512])
                S5("ct1b", [128, 512])
                _ffn_mlp_combine(nc, cp, lnx2, call)
                es.flush(f"{tag}_mlp")


def _router_lowrank(nc, cp, xnbuf):
    g, es = cp["g"], cp["es"]
    pk = cp["pk"]
    s_wld, s_pe, s_ve, s_se, s_st = (
        cp[k] for k in ["s_wld", "s_pe", "s_ve", "s_se", "s_st"])
    z = cp["a_z"]

    for dst, srcn, kd in [(cp["a_wr"], "w_router_T", 3),
                          (cp["a_wdn"], "w_down_T", 32)]:
        fin = inc(s_wld, 16)
        es.gpsimd(lambda gp, dst=dst, srcn=srcn, kd=kd, fin=fin:
                  fin(gp.dma_start(
                      out=dst[:].rearrange("p (kc m) -> p kc m", kc=FC),
                      in_=g[srcn].rearrange("(kc p) m -> p kc m", p=128))))
    fin = inc(s_wld, 16)
    es.gpsimd(lambda gp, fin=fin:
              fin(gp.dma_start(out=cp["a_wup"][:], in_=g["w_up_T"])))

    es.wait("tensor", s_wld)
    for li in range(3):
        for nt in range(4):
            if not (li == 0 and nt == 0):
                es.wait("tensor", s_ve)
            fin = inc(s_pe)
            def lmm(t, li=li, nt=nt, fin=fin):
                ins = None
                for kc in range(FC):
                    ins = t.matmul(pk[4][0:1, 0:512],
                                   cp["a_wr"][:, kc * 3 + li:kc * 3 + li + 1],
                                   xnbuf[:, kc * TQ + nt * 512:
                                         kc * TQ + nt * 512 + 512],
                                   start=(kc == 0), stop=(kc == FC - 1))
                fin(ins)
            es.tensor(lmm)
            es.wait("vector", s_pe)
            fin = inc(s_ve)
            es.vector(lambda v, li=li, nt=nt, fin=fin:
                      fin(v.tensor_copy(cp[f"a_l{li}"][0:1, nt * 512:
                                                       nt * 512 + 512],
                                        pk[4][0:1, 0:512])))
    def vstep(fn):
        es.wait("vector", cp["s_ve"])
        fin = inc(cp["s_ve"])
        es.vector(lambda v, fn=fn, fin=fin: fin(fn(v)))
    vstep(lambda v: v.tensor_tensor(cp["a_m0"][0:1, :], cp["a_l0"][0:1, :],
                                    cp["a_l1"][0:1, :], ALU.is_ge))
    vstep(lambda v: v.tensor_tensor(cp["a_m1"][0:1, :], cp["a_l0"][0:1, :],
                                    cp["a_l2"][0:1, :], ALU.is_ge))
    vstep(lambda v: v.tensor_tensor(cp["a_m2"][0:1, :], cp["a_l1"][0:1, :],
                                    cp["a_l2"][0:1, :], ALU.is_ge))
    vstep(lambda v: v.tensor_tensor(cp["a_l0"][0:1, :], cp["a_m0"][0:1, :],
                                    cp["a_m1"][0:1, :], ALU.mult))
    vstep(lambda v: v.tensor_scalar(cp["a_l1"][0:1, :], cp["a_m0"][0:1, :],
                                    -1.0, 1.0, ALU.mult, ALU.add))
    vstep(lambda v: v.tensor_tensor(cp["a_l1"][0:1, :], cp["a_l1"][0:1, :],
                                    cp["a_m2"][0:1, :], ALU.mult))
    vstep(lambda v: v.tensor_copy(cp["a_m0"][0:1, :], cp["a_l0"][0:1, :]))
    vstep(lambda v: v.tensor_copy(cp["a_m1"][0:1, :], cp["a_l1"][0:1, :]))
    vstep(lambda v: v.tensor_tensor(cp["a_m2"][0:1, :], cp["a_m0"][0:1, :],
                                    cp["a_m1"][0:1, :], ALU.add))
    vstep(lambda v: v.tensor_scalar(cp["a_m2"][0:1, :], cp["a_m2"][0:1, :],
                                    -1.0, 1.0, ALU.mult, ALU.add))
    for nt in range(4):
        if nt > 0:
            es.wait("tensor", s_ve)
        fin = inc(s_pe)
        def zmm(t, nt=nt, fin=fin):
            ins = None
            for kc in range(FC):
                ins = t.matmul(pk[5][0:32, 0:512],
                               cp["a_wdn"][:, kc * 32:(kc + 1) * 32],
                               xnbuf[:, kc * TQ + nt * 512:
                                     kc * TQ + nt * 512 + 512],
                               start=(kc == 0), stop=(kc == FC - 1))
            fin(ins)
        es.tensor(zmm)
        es.wait("vector", s_pe)
        fin = inc(s_ve)
        es.vector(lambda v, nt=nt, fin=fin:
                  fin(v.tensor_copy(z[0:32, nt * 512:nt * 512 + 512],
                                    pk[5][0:32, 0:512])))
    es.wait("tensor", s_ve)
    st_h = {}
    for mt in range(FC):
        for nt in range(4):
            it = mt * 4 + nt
            if it >= 2:
                es.wait("tensor", s_se, st_h.get(("se", it - 2), 0))
            fin = inc(s_pe)
            es.tensor(lambda t, mt=mt, nt=nt, fin=fin:
                      fin(t.matmul(pk[nt % 2][:, 0:512],
                                   cp["a_wup"][0:32, mt * 128:(mt + 1) * 128],
                                   z[0:32, nt * 512:nt * 512 + 512],
                                   start=True, stop=True)))
            es.wait("scalar", s_pe)
            if it >= 2:
                es.wait("scalar", s_st, st_h.get(("st", it - 2), 0))
            fin = inc(s_se)
            es.scalar(lambda sc, nt=nt, fin=fin:
                      fin(sc.copy(cp[f"a_pst{nt % 2}"][:, 0:512],
                                  pk[nt % 2][:, 0:512])))
            st_h[("se", it)] = s_se.n
            es.wait("sync", s_se)
            fin = inc(s_st, 16)
            es.sync(lambda sy, mt=mt, nt=nt, fin=fin: fin(sy.dma_start(
                out=g["y1_d"][mt * 128:(mt + 1) * 128,
                              nt * 512:nt * 512 + 512],
                in_=cp[f"a_pst{nt % 2}"][:, 0:512])))
            st_h[("st", it)] = s_st.n


def _qkv_proj(nc, cp, kind, cc_chunk):
    g, es = cp["g"], cp["es"]
    pk = cp["pk"]
    s_wld, s_pe, s_ve, s_se, s_cc, s_st = (
        cp[k] for k in ["s_wld", "s_pe", "s_ve", "s_se", "s_cc", "s_st"])
    ffn = kind == "ffn"

    def ag_tile(gt, kc, w0, w1):
        r, tch = gt // 4, gt % 4
        return g[f"ag_out_{tch}"][r * D + kc * 128:r * D + kc * 128 + 128, w0:w1]

    def slab_load(srcn, col0):
        es.wait("gpsimd", s_pe)
        fin = inc(s_wld, 16)
        es.gpsimd(lambda gp, srcn=srcn, col0=col0, fin=fin: fin(gp.dma_start(
            out=cp["a_wslab"][:].rearrange("p (kc m) -> p kc m", kc=FC),
            in_=g[srcn][:, col0:col0 + 512].rearrange(
                "(kc p) m -> p kc m", p=128))))

    hist = {}
    itc = [0]
    agb = cp["a_agb"]

    def fmaj_part(name, srcn, col0, dst, bias_col):
        slab_load(srcn, col0)
        for tch in range(4):
            for r in range(4):
                gt = r * 4 + tch
                es.wait("gpsimd", s_cc, cc_chunk[tch])
                es.wait("gpsimd", s_pe)
                fin = inc(s_wld, 16)
                es.gpsimd(lambda gp, gt=gt, fin=fin: fin(gp.dma_start(
                    out=agb[:].rearrange("p (kc t) -> p kc t", kc=FC),
                    in_=g[f"ag_out_{gt % 4}"].rearrange(
                        "(r kc p) t -> p r kc t", r=4, p=128)[
                        :, gt // 4, :, :])))
                es.wait("tensor", s_wld)
                for mt in range(4):
                    it = itc[0]
                    if ("se", it - 2) in hist:
                        hv = hist[("se", it - 2)]
                        if isinstance(hv, tuple):
                            es.wait("tensor", s_ve, hv[1])
                        else:
                            es.wait("tensor", s_se, hv)
                    fin = inc(s_pe)
                    def mm(t, mt=mt, it=it, fin=fin):
                        ins = None
                        for kc in range(FC):
                            ins = t.matmul(
                                pk[it % 2][:, 0:512],
                                cp["a_wslab"][:, kc * 512 + mt * 128:
                                              kc * 512 + mt * 128 + 128],
                                agb[:, kc * 512:(kc + 1) * 512],
                                start=(kc == 0), stop=(kc == FC - 1))
                        fin(ins)
                    es.tensor(mm)
                    if bias_col is None:
                        es.wait("scalar", s_pe)
                        if ("st", it - 2) in hist:
                            es.wait("scalar", s_st, hist[("st", it - 2)])
                        fin = inc(s_se)
                        es.scalar(lambda sc, it=it, fin=fin:
                                  fin(sc.copy(cp[f"a_pstage{it % 2}"][:, :],
                                              pk[it % 2][:, 0:512])))
                        hist[("se", it)] = s_se.n
                        es.wait("sync", s_se)
                    else:
                        es.wait("vector", s_pe)
                        if ("st", it - 2) in hist:
                            es.wait("vector", s_st, hist[("st", it - 2)])
                        fin = inc(s_ve)
                        es.vector(lambda v, it=it, mt=mt, fin=fin:
                                  fin(v.tensor_scalar(
                                      cp[f"a_pstage{it % 2}"][:, :],
                                      pk[it % 2][:, 0:512],
                                      g_bias(cp, bias_col, mt), None, ALU.add)))
                        hist[("se", it)] = ("ve", s_ve.n)
                        es.wait("sync", s_ve)
                    fin = inc(s_st, 16)
                    es.sync(lambda sy, gt=gt, mt=mt, it=it, fin=fin:
                            fin(sy.dma_start(
                                out=g[dst][mt * 128:(mt + 1) * 128,
                                           gt * 512:(gt + 1) * 512],
                                in_=cp[f"a_pstage{it % 2}"][:, :])))
                    hist[("st", it)] = s_st.n
                    itc[0] += 1

    def g_bias(cp, qkv_i, mt):
        return cp["a_bqkv"][:, qkv_i * 4 + mt:qkv_i * 4 + mt + 1]

    if ffn:
        fmaj_part("K", "wk_hg", 0, "K_d", None)
        fmaj_part("Q", "wq_hg", 0, "Q_d", None)
        vsrc, vcol = "wv_hg", 0
    else:
        fmaj_part("K", "wqkv_hg", 512, "K_d", 1)
        fmaj_part("Q", "wqkv_hg", 0, "Q_d", 0)
        vsrc, vcol = "wqkv_hg", 1024

    slab_load(vsrc, vcol)
    es.wait("gpsimd", s_cc, cc_chunk[3])
    for vt in range(64):
        gt = vt // 4
        if vt % 4 == 0:
            es.wait("gpsimd", s_pe)
            fin = inc(s_wld, 16)
            es.gpsimd(lambda gp, gt=gt, fin=fin: fin(gp.dma_start(
                out=agb[:].rearrange("p (kc t) -> p kc t", kc=FC),
                in_=g[f"ag_out_{gt % 4}"].rearrange(
                    "(r kc p) t -> p r kc t", r=4, p=128)[:, gt // 4, :, :])))
            es.wait("tensor", s_wld)
        it = itc[0]
        itc[0] += 1
        if ("se", it - 2) in hist:
            hv = hist[("se", it - 2)]
            if isinstance(hv, tuple):
                es.wait("tensor", s_ve, hv[1])
            else:
                es.wait("tensor", s_se, hv)
        fin = inc(s_pe)
        def vmm(t, vt=vt, it=it, fin=fin):
            sub = vt % 4
            withb = kind == "attnb"
            ins = None
            for kc in range(FC):
                ins = t.matmul(
                    pk[it % 2][:, 0:512],
                    agb[:, kc * 512 + sub * 128:kc * 512 + sub * 128 + 128],
                    cp["a_wslab"][:, kc * 512:(kc + 1) * 512],
                    start=(kc == 0), stop=(kc == FC - 1 and not withb))
            if withb:
                ins = t.matmul(pk[it % 2][:, 0:512], cp["ones_row"][:, 0:128],
                               cp["a_bqv"][:, :], start=False, stop=True)
            fin(ins)
        es.tensor(vmm)
        es.wait("scalar", s_pe)
        if ("st", it - 2) in hist:
            es.wait("scalar", s_st, hist[("st", it - 2)])
        fin = inc(s_se)
        es.scalar(lambda sc, it=it, fin=fin:
                  fin(sc.copy(cp[f"a_pstage{it % 2}"][:, :], pk[it % 2][:, 0:512])))
        hist[("se", it)] = s_se.n
        es.wait("sync", s_se)
        fin = inc(s_st, 16)
        es.sync(lambda sy, vt=vt, it=it, fin=fin: fin(sy.dma_start(
            out=g["V_d"][vt * 128:(vt + 1) * 128, :],
            in_=cp[f"a_pstage{it % 2}"][:, :])))
        hist[("st", it)] = s_st.n


def _flash(nc, cp):
    g, es = cp["g"], cp["es"]
    pk = cp["pk"]
    s_wld, s_xld, s_pe, s_ve, s_se, s_ge, s_st = (
        cp[k] for k in ["s_wld", "s_xld", "s_pe", "s_ve", "s_se", "s_ge", "s_st"])

    def load_head(h):
        b = h % 2
        es.wait("sync", s_pe)
        for dst, src in [(cp[f"a_kh{b}"], g["K_d"][h * 128:(h + 1) * 128, :]),
                         (cp[f"a_qh{b}"], g["Q_d"][h * 128:(h + 1) * 128, :])]:
            fin = inc(s_xld, 16)
            es.sync(lambda sy, dst=dst, src=src, fin=fin:
                    fin(sy.dma_start(out=dst[:], in_=src)))
        fin = inc(s_xld, 16)
        es.sync(lambda sy, h=h, b=b, fin=fin: fin(sy.dma_start(
            out=cp[f"a_vh{b}"][:].rearrange("p (kt d) -> p kt d", kt=64),
            in_=g["V_d"].rearrange("(kt p) d -> p kt d", p=128)[
                :, :, h * 128:(h + 1) * 128])))

    ld_done = {}
    load_head(0)
    ld_done[0] = s_xld.n
    for h in range(HG):
        b = h % 2
        if h + 1 < HG:
            load_head(h + 1)
            ld_done[h + 1] = s_xld.n
        es.wait("tensor", s_xld, ld_done[h])
        kh, vh, qh = cp[f"a_kh{b}"], cp[f"a_vh{b}"], cp[f"a_qh{b}"]
        for qm in range(8):
            nkv = (qm + 1) * 8
            exp_h, msk_h = {}, {}
            es.wait("tensor", s_se)
            es.wait("tensor", s_ve)
            for kvt in range(nkv + 1):
                if kvt < nkv:
                    sb0, sb1 = (2, 3) if kvt % 2 == 0 else (4, 5)
                    if kvt - 2 in exp_h:
                        es.wait("tensor", s_se, exp_h[kvt - 2])
                    fin = inc(s_pe)
                    es.tensor(lambda t, kvt=kvt, qm=qm, sb0=sb0, sb1=sb1,
                              kh=kh, qh=qh, fin=fin: (
                        t.matmul(pk[sb0][:, :], kh[:, kvt * 128:(kvt + 1) * 128],
                                 qh[:, qm * 1024:qm * 1024 + 512],
                                 start=True, stop=True),
                        fin(t.matmul(pk[sb1][:, :],
                                     kh[:, kvt * 128:(kvt + 1) * 128],
                                     qh[:, qm * 1024 + 512:qm * 1024 + 1024],
                                     start=True, stop=True))))
                av = kvt - 1
                if av >= 0:
                    es.wait("tensor", s_se, exp_h[av])
                    if av in msk_h:
                        es.wait("tensor", s_ge, msk_h[av])
                    es.wait("tensor", s_ve)
                    fin = inc(s_pe)
                    es.tensor(lambda t, av=av, vh=vh, nkv=nkv, fin=fin: (
                        t.matmul(pk[0][:, :], vh[:, av * 128:(av + 1) * 128],
                                 cp[f"a_pt{av % 2}"][:, 0:512],
                                 start=(av == 0), stop=(av == nkv - 1),
                                 skip_group_check=True),
                        fin(t.matmul(pk[1][:, :], vh[:, av * 128:(av + 1) * 128],
                                     cp[f"a_pt{av % 2}"][:, 512:1024],
                                     start=(av == 0), stop=(av == nkv - 1),
                                     skip_group_check=True))))
                if kvt < nkv:
                    es.wait("scalar", s_pe)
                    fin = inc(s_se)
                    es.scalar(lambda sc, kvt=kvt, sb0=sb0, sb1=sb1, fin=fin: (
                        sc.activation(cp[f"a_pt{kvt % 2}"][:, 0:512],
                                      pk[sb0][:, :], AF.Exp,
                                      scale=RV, bias=-EXP_SHIFT),
                        fin(sc.activation(cp[f"a_pt{kvt % 2}"][:, 512:1024],
                                          pk[sb1][:, :], AF.Exp,
                                          scale=RV, bias=-EXP_SHIFT))))
                    exp_h[kvt] = s_se.n
                    if kvt >= qm * 8:
                        rel = kvt - qm * 8
                        es.wait("gpsimd", s_se)
                        fin = inc(s_ge)
                        es.gpsimd(lambda gp, kvt=kvt, rel=rel, fin=fin:
                                  fin(gp.tensor_tensor(
                                      cp[f"a_pt{kvt % 2}"][:, :],
                                      cp[f"a_pt{kvt % 2}"][:, :],
                                      cp["a_mask01"][:, rel * 1024:
                                                     (rel + 1) * 1024],
                                      ALU.mult)))
                        msk_h[kvt] = s_ge.n
                    es.wait("vector", s_se)
                    if kvt in msk_h:
                        es.wait("vector", s_ge)
                    es.wait("vector", s_ve)
                    if kvt == 0:
                        es.wait("vector", s_pe)
                    fin = inc(s_ve)
                    if kvt == 0:
                        es.vector(lambda v, kvt=kvt, fin=fin:
                                  fin(v.tensor_copy(cp["a_accf"][:, :],
                                                    cp[f"a_pt{kvt % 2}"][:, :])))
                    else:
                        es.vector(lambda v, kvt=kvt, fin=fin:
                                  fin(v.tensor_tensor(cp["a_accf"][:, :],
                                                      cp["a_accf"][:, :],
                                                      cp[f"a_pt{kvt % 2}"][:, :],
                                                      ALU.add)))
            es.wait("scalar", s_pe)
            fin = inc(s_se)
            es.scalar(lambda sc, fin=fin: (
                sc.copy(cp["a_osb"][:, 0:512], pk[0][:, :]),
                fin(sc.copy(cp["a_osb"][:, 512:1024], pk[1][:, :]))))
            es.wait("tensor", s_ve)
            es.wait("tensor", s_se)
            fin = inc(s_pe)
            es.tensor(lambda t, fin=fin: (
                t.matmul(pk[2][0:1, :], cp["ones_col_f"][:, :],
                         cp["a_accf"][:, 0:512], start=True, stop=True),
                fin(t.matmul(pk[3][0:1, :], cp["ones_col_f"][:, :],
                             cp["a_accf"][:, 512:1024], start=True, stop=True))))
            es.wait("vector", s_pe)
            fin = inc(s_ve)
            es.vector(lambda v, fin=fin: (
                v.reciprocal(cp["r_mean"][0:1, 0:512], pk[2][0:1, :]),
                fin(v.reciprocal(cp["r_ex2"][0:1, 0:512], pk[3][0:1, :]))))
            es.wait("tensor", s_ve)
            fin = inc(s_pe)
            es.tensor(lambda t, fin=fin: (
                t.matmul(pk[4][:, :], cp["ones_row_f"][:, :],
                         cp["r_mean"][0:1, 0:512], start=True, stop=True),
                fin(t.matmul(pk[5][:, :], cp["ones_row_f"][:, :],
                             cp["r_ex2"][0:1, 0:512], start=True, stop=True))))
            es.wait("vector", s_pe)
            es.wait("vector", s_st)
            fin = inc(s_ve)
            es.vector(lambda v, fin=fin: (
                v.tensor_tensor(cp["a_ctxs"][:, 0:512], cp["a_osb"][:, 0:512],
                                pk[4][:, :], ALU.mult),
                fin(v.tensor_tensor(cp["a_ctxs"][:, 512:1024],
                                    cp["a_osb"][:, 512:1024],
                                    pk[5][:, :], ALU.mult))))
            es.wait("sync", s_ve)
            fin = inc(s_st, 16)
            es.sync(lambda sy, h=h, qm=qm, fin=fin: fin(sy.dma_start(
                out=g["ctx_d"][h * 128:(h + 1) * 128,
                               qm * 1024:(qm + 1) * 1024],
                in_=cp["a_ctxs"][:, :])))


def _wo_partial(nc, cp, kind):
    g, es = cp["g"], cp["es"]
    pk = cp["pk"]
    s_wld, s_xld, s_pe, s_se, s_st = (
        cp[k] for k in ["s_wld", "s_xld", "s_pe", "s_se", "s_st"])
    wsrc = "wo_f_hg" if kind == "ffn" else "wo_b_hg"

    fin = inc(s_wld, 16)
    es.gpsimd(lambda gp, fin=fin: fin(gp.dma_start(
        out=cp["a_woslab"][:].rearrange("p (kc m) -> p kc m", kc=4),
        in_=g[wsrc].rearrange("(kc p) m -> p kc m", p=128))))
    es.wait("sync", s_st)
    hist = {}
    for gt in range(16):
        es.wait("sync", s_pe)
        fin = inc(s_xld, 16)
        es.sync(lambda sy, gt=gt, fin=fin: fin(sy.dma_start(
            out=cp["a_ctx4"][:].rearrange("p (kc t) -> p kc t", kc=4),
            in_=g["ctx_d"][:, gt * 512:(gt + 1) * 512].rearrange(
                "(kc p) t -> p kc t", p=128))))
        es.wait("tensor", s_xld)
        es.wait("tensor", s_wld)
        for ot in range(16):
            it = gt * 16 + ot
            if ("se", it - 2) in hist:
                es.wait("tensor", s_se, hist[("se", it - 2)])
            fin = inc(s_pe)
            def mm(t, ot=ot, it=it, fin=fin):
                ins = None
                for kc in range(4):
                    ins = t.matmul(pk[it % 2][:, :],
                                   cp["a_woslab"][:, kc * D + ot * 128:
                                                  kc * D + ot * 128 + 128],
                                   cp["a_ctx4"][:, kc * 512:(kc + 1) * 512],
                                   start=(kc == 0), stop=(kc == 3))
                fin(ins)
            es.tensor(mm)
            es.wait("scalar", s_pe)
            if ("st", it - 2) in hist:
                es.wait("scalar", s_st, hist[("st", it - 2)])
            fin = inc(s_se)
            es.scalar(lambda sc, it=it, fin=fin:
                      fin(sc.copy(cp[f"a_dstage{it % 2}"][:, :],
                                  pk[it % 2][:, :])))
            hist[("se", it)] = s_se.n
            es.wait("sync", s_se)
            r = gt // 4
            fin = inc(s_st, 16)
            es.sync(lambda sy, gt=gt, ot=ot, it=it, r=r, fin=fin:
                    fin(sy.dma_start(
                        out=g["rs_in2"][r * D + ot * 128:r * D + ot * 128 + 128,
                                        (gt % 4) * 512:(gt % 4) * 512 + 512],
                        in_=cp[f"a_dstage{it % 2}"][:, :])))
            hist[("st", it)] = s_st.n


def _attnb_epilogue(nc, cp):
    g, es, nc_ = cp["g"], cp["es"], nc
    s_xld, s_ve, s_st, s_cc = (cp[k] for k in ["s_xld", "s_ve", "s_st", "s_cc"])
    with ExitStack() as ec:
        cx = ec.enter_context(nc.sbuf_tensor("sb_ab_cx", [128, 512], F32))
        dd = ec.enter_context(nc.sbuf_tensor("sb_ab_dd", [128, 512], BF16))
        bo = ec.enter_context(nc.sbuf_tensor("sb_ab_bo", [128, FC], F32))
        fin = inc(cp["s_wld"], 16)
        es.gpsimd(lambda gp, fin=fin:
                  fin(gp.dma_start(out=bo[:], in_=g["bo_c"])))
        es.wait("sync", cp["s_cc"])
        es.wait("sync", cp["s_wld"])
        for mt in range(FC):
            for nt in range(4):
                es.wait("sync", s_st)
                es.wait("sync", s_ve)
                fin = inc(s_xld, 16)
                es.sync(lambda sy, mt=mt, nt=nt, fin=fin: fin(sy.dma_start(
                    out=cx[:, :], in_=g["x_res"][mt * 128:(mt + 1) * 128,
                                                 nt * 512:nt * 512 + 512])))
                fin = inc(s_xld, 16)
                es.sync(lambda sy, mt=mt, nt=nt, fin=fin: fin(sy.dma_start(
                    out=dd[:, :], in_=g["rs_out2"][mt * 128:(mt + 1) * 128,
                                                   nt * 512:nt * 512 + 512])))
                es.wait("vector", s_xld)
                fin = inc(s_ve)
                es.vector(lambda v, fin=fin:
                          fin(v.tensor_tensor(cx[:, :], cx[:, :], dd[:, :],
                                              ALU.add)))
                es.wait("vector", s_ve)
                fin = inc(s_ve)
                es.vector(lambda v, mt=mt, fin=fin:
                          fin(v.tensor_scalar(cx[:, :], cx[:, :],
                                              bo[:, mt:mt + 1], None, ALU.add)))
                es.wait("sync", s_ve)
                fin = inc(s_st, 16)
                es.sync(lambda sy, mt=mt, nt=nt, fin=fin: fin(sy.dma_start(
                    out=g["x_res"][mt * 128:(mt + 1) * 128,
                                   nt * 512:nt * 512 + 512],
                    in_=cx[:, :])))


_FXN = [0]


def _ffn_xtl(nc, cp, lnx2):
    g, es = cp["g"], cp["es"]
    s_xld, s_ve, s_st = (cp[k] for k in ["s_xld", "s_ve", "s_st"])
    _FXN[0] += 1
    with ExitStack() as ec:
        ta = ec.enter_context(nc.sbuf_tensor(f"sb_fx_ta{_FXN[0]}", [128, 512], BF16))
        tb = ec.enter_context(nc.sbuf_tensor(f"sb_fx_tb{_FXN[0]}", [128, 512], BF16))
        es.wait("sync", cp["s_cc"])
        for cc in range(FC):
            for nt in range(4):
                es.wait("sync", s_ve)
                fin = inc(s_xld, 16)
                es.sync(lambda sy, cc=cc, nt=nt, fin=fin: fin(sy.dma_start(
                    out=ta[:, :], in_=g["xn_loc"][cc * 128:(cc + 1) * 128,
                                                  nt * 512:nt * 512 + 512])))
                fin = inc(s_xld, 16)
                es.sync(lambda sy, cc=cc, nt=nt, fin=fin: fin(sy.dma_start(
                    out=tb[:, :], in_=g["rs_out2"][cc * 128:(cc + 1) * 128,
                                                   nt * 512:nt * 512 + 512])))
                es.wait("vector", s_xld)
                fin = inc(s_ve)
                es.vector(lambda v, cc=cc, nt=nt, fin=fin:
                          fin(v.tensor_tensor(
                              lnx2[:, cc * TQ + nt * 512:cc * TQ + nt * 512 + 512],
                              ta[:, :], tb[:, :], ALU.add)))
        es.wait("sync", s_ve)
        fin = inc(s_st, 16)
        es.sync(lambda sy, fin=fin: fin(sy.dma_start(
            out=g["xtl_d"].rearrange("(cc p) t -> p cc t", p=128),
            in_=lnx2[:].rearrange("p (cc t) -> p cc t", cc=FC))))


def _ffn_mlp_combine(nc, cp, xn2, call):
    g, es = cp["g"], cp["es"]
    pk = cp["pk"]
    s_wld, s_xld, s_pe, s_ve, s_se, s_ge, s_st = (
        cp[k] for k in ["s_wld", "s_xld", "s_pe", "s_ve", "s_se", "s_ge", "s_st"])
    NMT = FF // 128

    def vstep(fn, waits=()):
        for w in waits:
            es.wait("vector", cp[w])
        es.wait("vector", s_ve)
        fin = inc(s_ve)
        es.vector(lambda v, fn=fn, fin=fin: fin(fn(v)))

    for nt in range(4):
        es.wait("tensor", s_ve)
        es.wait("tensor", s_se)
        fin = inc(s_pe)
        es.tensor(lambda t, nt=nt, fin=fin: (
            t.matmul(pk[4][:, :], cp["ones_row"][:, :],
                     cp["a_m0"][0:1, nt * 512:nt * 512 + 512],
                     start=True, stop=True),
            fin(t.matmul(pk[5][:, :], cp["ones_row"][:, :],
                         cp["a_m1"][0:1, nt * 512:nt * 512 + 512],
                         start=True, stop=True))))
        es.wait("vector", s_pe)
        fin = inc(s_ve)
        es.vector(lambda v, fin=fin: (
            v.tensor_copy(cp["a_cmb0"][:, :], pk[4][:, :]),
            fin(v.tensor_copy(cp["a_cmb1"][:, :], pk[5][:, :]))))
        vstep(lambda v: v.tensor_tensor(cp["a_cmb2"][:, :], cp["a_cmb0"][:, :],
                                        cp["a_cmb1"][:, :], ALU.add))
        vstep(lambda v: v.tensor_scalar(cp["a_cmb2"][:, :], cp["a_cmb2"][:, :],
                                        -1.0, 1.0, ALU.mult, ALU.add))
        se_h = {}
        for mt in range(NMT):
            buf = mt % 2
            if mt >= 2:
                es.wait("gpsimd", s_pe)
            if call == 0 and nt == 0:
                fin = inc(s_wld, 16)
                es.gpsimd(lambda gp, mt=mt, buf=buf, fin=fin: fin(gp.dma_start(
                    out=cp[f"a_fslab{buf}"][:].rearrange(
                        "p (kc m) -> p kc m", kc=FC),
                    in_=g["w_ff1_T"][:, mt * 128:(mt + 1) * 128].rearrange(
                        "(kc p) m -> p kc m", p=128))))
                es.wait("sync", s_wld)
                fin = inc(s_st, 16)
                es.sync(lambda sy, mt=mt, buf=buf, fin=fin: fin(sy.dma_start(
                    out=g["wff1_bf"][:, mt * 128:(mt + 1) * 128].rearrange(
                        "(kc p) m -> p kc m", p=128),
                    in_=cp[f"a_fslab{buf}"][:].rearrange(
                        "p (kc m) -> p kc m", kc=FC))))
            else:
                fin = inc(s_wld, 16)
                es.gpsimd(lambda gp, mt=mt, buf=buf, fin=fin: fin(gp.dma_start(
                    out=cp[f"a_fslab{buf}"][:].rearrange(
                        "p (kc m) -> p kc m", kc=FC),
                    in_=g["wff1_bf"][:, mt * 128:(mt + 1) * 128].rearrange(
                        "(kc p) m -> p kc m", p=128))))
            es.wait("tensor", s_wld)
            if mt >= 2:
                es.wait("tensor", s_se, se_h[mt - 2])
            fin = inc(s_pe)
            def f1mm(t, mt=mt, nt=nt, buf=buf, fin=fin):
                ins = None
                for kc in range(FC):
                    ins = t.matmul(pk[mt % 2][:, :],
                                   cp[f"a_fslab{buf}"][:, kc * 128:(kc + 1) * 128],
                                   xn2[:, kc * TQ + nt * 512:
                                       kc * TQ + nt * 512 + 512],
                                   start=(kc == 0), stop=(kc == FC - 1))
                fin(ins)
            es.tensor(f1mm)
            es.wait("scalar", s_pe)
            fin = inc(s_se)
            es.scalar(lambda sc, mt=mt, fin=fin:
                      fin(sc.activation(cp["a_hsb"][:, mt * 512:(mt + 1) * 512],
                                        pk[mt % 2][:, :], AF.Gelu_apprx_tanh)))
            se_h[mt] = s_se.n
        for ot in range(FC):
            es.wait("gpsimd", s_pe)
            if call == 0 and nt == 0:
                fin = inc(s_wld, 16)
                es.gpsimd(lambda gp, ot=ot, fin=fin: fin(gp.dma_start(
                    out=cp["a_f2slab"][:].rearrange("p (kc m) -> p kc m",
                                                    kc=FF // 128),
                    in_=g["w_ff2_T"][:, ot * 128:(ot + 1) * 128].rearrange(
                        "(kc p) m -> p kc m", p=128))))
                es.wait("sync", s_wld)
                fin = inc(s_st, 16)
                es.sync(lambda sy, ot=ot, fin=fin: fin(sy.dma_start(
                    out=g["wff2_bf"][:, ot * 128:(ot + 1) * 128].rearrange(
                        "(kc p) m -> p kc m", p=128),
                    in_=cp["a_f2slab"][:].rearrange("p (kc m) -> p kc m",
                                                    kc=FF // 128))))
            else:
                fin = inc(s_wld, 16)
                es.gpsimd(lambda gp, ot=ot, fin=fin: fin(gp.dma_start(
                    out=cp["a_f2slab"][:].rearrange("p (kc m) -> p kc m",
                                                    kc=FF // 128),
                    in_=g["wff2_bf"][:, ot * 128:(ot + 1) * 128].rearrange(
                        "(kc p) m -> p kc m", p=128))))
            es.wait("tensor", s_wld)
            es.wait("tensor", s_se)
            es.wait("tensor", s_ve)
            fin = inc(s_pe)
            def f2mm(t, ot=ot, fin=fin):
                ins = None
                for kc in range(FF // 128):
                    ins = t.matmul(pk[2 + ot % 2][:, :],
                                   cp["a_f2slab"][:, kc * 128:(kc + 1) * 128],
                                   cp["a_hsb"][:, kc * 512:(kc + 1) * 512],
                                   start=(kc == 0), stop=(kc == FF // 128 - 1))
                fin(ins)
            es.tensor(f2mm)
            es.wait("sync", s_ve)
            for dst, srcn in [("a_ct0", "xn_loc"), ("a_ct1", "y1_d"),
                              ("a_ct2", "xtl_d")]:
                fin = inc(s_xld, 16)
                es.sync(lambda sy, dst=dst, srcn=srcn, ot=ot, nt=nt, fin=fin:
                        fin(sy.dma_start(
                            out=cp[dst][:, :],
                            in_=g[srcn][ot * 128:(ot + 1) * 128,
                                        nt * 512:nt * 512 + 512])))
            fin = inc(s_xld, 16)
            es.sync(lambda sy, ot=ot, nt=nt, fin=fin: fin(sy.dma_start(
                out=cp["a_cx"][:, :],
                in_=g["x_res"][ot * 128:(ot + 1) * 128,
                               nt * 512:nt * 512 + 512])))
            es.wait("vector", s_pe)
            es.wait("vector", s_xld)
            vstep(lambda v, ot=ot: v.tensor_tensor(
                cp["a_ct0b"][:, :], pk[2 + ot % 2][:, :], cp["a_ct2"][:, :],
                ALU.add))
            vstep(lambda v: v.tensor_tensor(cp["a_ct0b"][:, :], cp["a_ct0b"][:, :],
                                            cp["a_cmb2"][:, :], ALU.mult))
            vstep(lambda v, ot=ot: v.tensor_scalar(
                cp["a_ct1b"][:, :], cp["a_ct0"][:, :],
                cp["a_sbc"][:, ot:ot + 1], None, ALU.add))
            vstep(lambda v: v.tensor_tensor(cp["a_ct1b"][:, :], cp["a_ct1b"][:, :],
                                            cp["a_cmb0"][:, :], ALU.mult))
            vstep(lambda v: v.tensor_tensor(cp["a_ct0b"][:, :], cp["a_ct0b"][:, :],
                                            cp["a_ct1b"][:, :], ALU.add))
            vstep(lambda v: v.tensor_tensor(cp["a_ct1b"][:, :], cp["a_ct0"][:, :],
                                            cp["a_ct1"][:, :], ALU.add))
            vstep(lambda v: v.tensor_tensor(cp["a_ct1b"][:, :], cp["a_ct1b"][:, :],
                                            cp["a_cmb1"][:, :], ALU.mult))
            vstep(lambda v: v.tensor_tensor(cp["a_ct0b"][:, :], cp["a_ct0b"][:, :],
                                            cp["a_ct1b"][:, :], ALU.add))
            vstep(lambda v: v.tensor_tensor(cp["a_cx"][:, :], cp["a_cx"][:, :],
                                            cp["a_ct0b"][:, :], ALU.add))
            es.wait("sync", s_ve)
            fin = inc(s_st, 16)
            es.sync(lambda sy, ot=ot, nt=nt, fin=fin: fin(sy.dma_start(
                out=g["x_res"][ot * 128:(ot + 1) * 128,
                               nt * 512:nt * 512 + 512],
                in_=cp["a_cx"][:, :])))
            es.wait("vector", s_st)


def kernel(**inputs):
    maps = stage_inputs(inputs)
    nc = build_program()
    res = run_bass_kernel_spmd(nc, maps, core_ids=list(range(NCORE)))
    out = np.zeros((B, S, D), np.float32)
    for c in range(NCORE):
        b, q = divmod(c, 4)
        out[b, q * TQ:(q + 1) * TQ, :] = np.asarray(res.results[c]["out_f"]).T
    return out


# version 02
# speedup vs baseline: 1.2045x; 1.2045x over previous
import math
import numpy as np
import ml_dtypes
from contextlib import ExitStack

import concourse.bass as bass
import concourse.mybir as mybir
from concourse.bass_utils import run_bass_kernel_spmd

F32 = mybir.dt.float32
BF16 = mybir.dt.bfloat16
AF = mybir.ActivationFunctionType
ALU = mybir.AluOpType
AX = mybir.AxisListType
BF16NP = ml_dtypes.bfloat16

B, S, D = 2, 8192, 2048
TQ = 2048
NCORE = 8
FC = D // 128
INTER = 4096
MH, MHD, MST, MCH = 32, 128, 128, 256
CONV_DIM = INTER + 2 * MST
PROJ_M = INTER + CONV_DIM + MH
HBC_MT = CONV_DIM // 128
HALO = 3
XOFF = 4
TQH = TQ + XOFF
NCHK = TQ // MCH
AH, AHD = 16, 128
HG = 4
FF = 8192
EXP_SHIFT = 12.0
NEG = -1.0e9
RV = 1.0 / math.sqrt(AHD)
GROUPS4 = [[0, 1, 2, 3], [4, 5, 6, 7]]

STAGES = ("mamba", "ffn0", "attnb", "ffn1")
M_PARTS = ("ln", "inprojA", "inprojB", "dt", "sloc", "gather", "sbk", "ssd")
DEBUG_TAPS = ()


def _act_consts():
    return [1e-5, 1.0, -1.0, RV, -EXP_SHIFT, 0.5, 1.0 / D, 1.0 / INTER,
            math.sqrt(2.0 / math.pi), 1.0 / MCH, 100.0]


ENGS = ("tensor", "vector", "scalar", "gpsimd", "sync")


class Sem:
    def __init__(self, nc, ctx, name):
        self.h = ctx.enter_context(nc.semaphore(name))
        self.n = 0


class ES:

    def __init__(self, nc):
        self.nc = nc
        self.q = {e: [] for e in ENGS}

    def _emit(self, eng, fn):
        self.q[eng].append(fn)

    def __getattr__(self, eng):
        if eng in ENGS:
            return lambda fn: self._emit(eng, fn)
        raise AttributeError(eng)

    def wait(self, eng, sem, val=None):
        n = sem.n if val is None else val
        assert n <= sem.n, "waiting on a future semaphore value"
        if n > 0:
            self._emit(eng, lambda E: E.wait_ge(sem.h, n))

    def flush(self, name):
        q = self.q
        self.q = {e: [] for e in ENGS}
        with self.nc.Block(name) as blk:
            for eng in ENGS:
                fns = q[eng]
                if fns:
                    getattr(blk, eng)(lambda E, fns=fns: [f(E) for f in fns])


def inc(sem, k=1):
    sem.n += k
    return lambda inst: inst.then_inc(sem.h, k)


def _fmajor(a):
    return np.ascontiguousarray(np.asarray(a).T)


def stage_inputs(inp):
    x = np.asarray(inp["L0_attn_a_hidden_states"])

    shared = {}
    shared["ident_bf"] = np.eye(128, dtype=BF16NP)
    shared["ident_f"] = np.eye(128, dtype=np.float32)
    shared["ones_col_bf"] = np.ones((128, 1), BF16NP)
    shared["ones_row_bf"] = np.ones((1, 128), BF16NP)
    shared["ones_row_f"] = np.ones((1, 128), np.float32)
    shared["ones_col_f"] = np.ones((128, 1), np.float32)
    m01 = np.zeros((128, 8 * 1024), BF16NP)
    for t in range(8):
        kv = t * 128 + np.arange(128)[:, None]
        qq = np.arange(1024)[None, :]
        m01[:, t * 1024:(t + 1) * 1024] = (kv <= qq).astype(BF16NP)
    shared["mask01"] = m01
    sm = np.zeros((128, 2 * MCH), BF16NP)
    for sh in range(2):
        ss = sh * 128 + np.arange(128)[:, None]
        ll = np.arange(MCH)[None, :]
        sm[:, sh * MCH:(sh + 1) * MCH] = (ss <= ll).astype(BF16NP)
    shared["ssd_mask"] = sm
    shared["consts"] = np.repeat(np.asarray(_act_consts(), np.float32)[None, :],
                                 128, axis=0)

    shared["w_inproj_T"] = _fmajor(inp["L0_attn_a_in_proj_w"])
    shared["conv_w_c"] = np.ascontiguousarray(
        np.asarray(inp["L0_attn_a_conv_w"])[:, 0, :]
        .reshape(HBC_MT, 128, 4).transpose(1, 0, 2).reshape(128, -1))
    shared["conv_b_c"] = np.ascontiguousarray(
        np.asarray(inp["L0_attn_a_conv_b"]).reshape(HBC_MT, 128).T)
    shared["A_log"] = np.asarray(inp["L0_attn_a_A_log"]).reshape(MH, 1)
    shared["dt_bias"] = np.asarray(inp["L0_attn_a_dt_bias"]).reshape(MH, 1)
    shared["D_row"] = np.ascontiguousarray(
        np.asarray(inp["L0_attn_a_D"]).reshape(1, MH))
    shared["norm_w_row"] = np.ascontiguousarray(
        np.asarray(inp["L0_attn_a_norm_w"]).reshape(1, INTER))
    shared["w_outproj_T"] = _fmajor(inp["L0_attn_a_out_proj_w"])

    Wqkv = np.asarray(inp["L0_attn_b_Wqkv"])
    bqkv = np.asarray(inp["L0_attn_b_bqkv"])
    Wo = np.asarray(inp["L0_attn_b_Wo"])
    shared["bo_c"] = np.ascontiguousarray(
        np.asarray(inp["L0_attn_b_bo"]).reshape(FC, 128).T)

    shared["w_router_T"] = _fmajor(inp["L0_ffn_w_router"])
    shared["w_down_T"] = _fmajor(inp["L0_ffn_w_down"])
    shared["w_up_T"] = _fmajor(inp["L0_ffn_w_up"])
    shared["w_ff1_T"] = _fmajor(inp["L0_ffn_w_ff1"])
    shared["w_ff2_T"] = _fmajor(inp["L0_ffn_w_ff2"])
    for nm in ["ln1_scale", "ln1_bias", "ln2_scale", "ln2_bias", "simple_bias"]:
        shared[nm + "_c"] = np.ascontiguousarray(
            np.asarray(inp["L0_ffn_" + nm]).reshape(FC, 128).T)
    wq_T = _fmajor(inp["L0_ffn_wq"])
    wk_T = _fmajor(inp["L0_ffn_wk"])
    wv_T = _fmajor(inp["L0_ffn_wv"])
    wo_T = _fmajor(inp["L0_ffn_wo"])

    maps = []
    for c in range(NCORE):
        b, q = divmod(c, 4)
        m = dict(shared)
        xt = np.ascontiguousarray(x[b].T)
        sl = np.zeros((D, TQH), np.float32)
        t0 = q * TQ
        sl[:, XOFF:] = xt[:, t0:t0 + TQ]
        if t0 >= HALO:
            sl[:, XOFF - HALO:XOFF] = xt[:, t0 - HALO:t0]
        m["x_f"] = sl
        qmask = np.zeros((MH, 4), np.float32)
        scmask = np.zeros((MH, 16), np.float32)
        pmask = np.zeros((MH, 4), np.float32)
        for mm in range(4):
            if mm > q:
                qmask[:, mm] = 1.0
                for k in range(q + 1, mm):
                    scmask[:, mm * 4 + k] = 1.0
            if mm == q - 1:
                pmask[:, mm] = 1.0
        m["qmask"] = qmask
        m["scmask"] = scmask
        m["pmask"] = pmask
        h0 = q * HG * AHD
        m["wqkv_hg"] = np.ascontiguousarray(np.concatenate(
            [Wqkv[:, h0:h0 + 512], Wqkv[:, 2048 + h0:2048 + h0 + 512],
             Wqkv[:, 4096 + h0:4096 + h0 + 512]], axis=1))
        bq = np.concatenate([bqkv[h0:h0 + 512], bqkv[2048 + h0:2048 + h0 + 512],
                             bqkv[4096 + h0:4096 + h0 + 512]])
        m["bqkv_hg"] = np.ascontiguousarray(bq.reshape(12, 128).T)
        m["bqv_row"] = np.ascontiguousarray(bqkv[4096 + h0:4096 + h0 + 512]
                                            .reshape(1, 512))
        m["wo_b_hg"] = np.ascontiguousarray(Wo[h0:h0 + 512, :])
        m["wq_hg"] = np.ascontiguousarray(wq_T[:, h0:h0 + 512])
        m["wk_hg"] = np.ascontiguousarray(wk_T[:, h0:h0 + 512])
        m["wv_hg"] = np.ascontiguousarray(wv_T[:, h0:h0 + 512])
        m["wo_f_hg"] = np.ascontiguousarray(wo_T[h0:h0 + 512, :])
        maps.append(m)
    return maps


_TAPS = {
    "tap_xn": ([128, FC * TQH], BF16),
    "tap_hbc": ([CONV_DIM, TQ], BF16),
    "tap_gate": ([TQ, INTER], BF16),
    "tap_dt": ([MH, TQ], F32),
    "tap_acum": ([MH, TQ], F32),
    "tap_sloc": ([NCHK * INTER, MHD], BF16),
    "tap_sin": ([NCHK * INTER, MHD], BF16),
    "tap_stq": ([INTER, MHD], BF16),
    "tap_dec": ([128, 2 * MH * NCHK], F32),
    "tap_yg": ([TQ, INTER], BF16),
    "tap_ybase": ([128, MH * MHD], F32),
    "tap_kd": ([HG * AHD, S], BF16),
    "tap_vd": ([S, HG * AHD], BF16),
    "tap_ctx": ([HG * AHD, S], BF16),
    "tap_xtl": ([D, TQ], BF16),
    "tap_logits": ([3, TQ], F32),
}


def build_program():
    nc = bass.Bass("TRN2", target_bir_lowering=False, debug=False)
    g = {}

    def IN(name, shape, dtype=F32):
        g[name] = nc.dram_tensor(name, list(shape), dtype, kind="ExternalInput").ap()

    def DRI(name, shape, dtype):
        g[name] = nc.dram_tensor(name, list(shape), dtype, kind="Internal").ap()

    IN("x_f", [D, TQH])
    IN("ident_bf", [128, 128], BF16); IN("ident_f", [128, 128])
    IN("ones_col_bf", [128, 1], BF16)
    IN("ones_row_bf", [1, 128], BF16); IN("ones_row_f", [1, 128])
    IN("ones_col_f", [128, 1])
    IN("mask01", [128, 8 * 1024], BF16); IN("ssd_mask", [128, 2 * MCH], BF16)
    IN("consts", [128, len(_act_consts())])
    IN("w_inproj_T", [D, PROJ_M])
    IN("conv_w_c", [128, HBC_MT * 4]); IN("conv_b_c", [128, HBC_MT])
    IN("A_log", [MH, 1]); IN("dt_bias", [MH, 1])
    IN("D_row", [1, MH]); IN("norm_w_row", [1, INTER])
    IN("w_outproj_T", [INTER, D])
    IN("qmask", [MH, 4]); IN("scmask", [MH, 16]); IN("pmask", [MH, 4])
    IN("wqkv_hg", [D, 3 * 512]); IN("bqkv_hg", [128, 12]); IN("bqv_row", [1, 512])
    IN("wo_b_hg", [512, D]); IN("bo_c", [128, FC])
    IN("w_router_T", [D, 3]); IN("w_down_T", [D, 32]); IN("w_up_T", [32, D])
    IN("w_ff1_T", [D, FF]); IN("w_ff2_T", [FF, D])
    for nm in ["ln1_scale", "ln1_bias", "ln2_scale", "ln2_bias", "simple_bias"]:
        IN(nm + "_c", [128, FC])
    IN("wq_hg", [D, 512]); IN("wk_hg", [D, 512]); IN("wv_hg", [D, 512])
    IN("wo_f_hg", [512, D])

    g["out_f"] = nc.dram_tensor("out_f", [D, TQ], F32, kind="ExternalOutput").ap()

    DRI("x_res", [D, TQ], F32)
    DRI("gate_d", [TQ, INTER], BF16)
    DRI("hbc_d", [CONV_DIM, TQ], BF16)
    DRI("acum_d", [MH, TQ], F32)
    DRI("alast_d", [1, MH * NCHK], F32)
    DRI("pfx_d", [1, MH * NCHK], F32)
    DRI("pvrow_d", [1, 4 * MH], F32)
    DRI("sin_d", [NCHK * INTER, MHD], BF16)
    DRI("lastloc_in", [INTER, MHD], BF16)
    DRI("lastloc_out", [4 * INTER, MHD], BF16)
    DRI("wbase_d", [1, 4 * MH], F32)
    DRI("sloc_d", [NCHK * INTER, MHD], BF16)
    DRI("stq_in", [INTER, MHD], BF16)
    DRI("stq_out", [4 * INTER, MHD], BF16)
    DRI("adsum_in", [MH, 1], F32)
    DRI("adsum_out", [4 * MH, 1], F32)
    for tch in range(4):
        DRI(f"xn_ag_{tch}", [D, 512], BF16)
        DRI(f"ag_out_{tch}", [4 * D, 512], BF16)
    DRI("xn_loc", [D, TQ], BF16)
    DRI("y1_d", [D, TQ], BF16)
    DRI("xtl_d", [D, TQ], BF16)
    DRI("K_d", [HG * AHD, S], BF16)
    DRI("Q_d", [HG * AHD, S], BF16)
    DRI("V_d", [S, HG * AHD], BF16)
    DRI("ctx_d", [HG * AHD, S], BF16)
    DRI("rs_in2", [4 * D, TQ], BF16)
    DRI("rs_out2", [D, TQ], BF16)
    DRI("wff1_bf", [D, FF], BF16)
    DRI("wff2_bf", [FF, D], BF16)

    for nm in DEBUG_TAPS:
        shp, dt = _TAPS[nm]
        g[nm] = nc.dram_tensor(nm, list(shp), dt, kind="ExternalOutput").ap()

    with ExitStack() as ctx:
        _build(nc, ctx, g)
    return nc


def _build(nc, ctx, g):
    cp = {"g": g, "nc": nc, "es": ES(nc)}
    for nm in ["s_wld", "s_xld", "s_pe", "s_ve", "s_se", "s_ge", "s_cc", "s_st"]:
        cp[nm] = Sem(nc, ctx, nm)

    cvals = _act_consts()
    c_tile = ctx.enter_context(nc.sbuf_tensor("c_consts", [128, len(cvals)], F32))
    for i, v in enumerate(cvals):
        nc.const_aps.register(F32, float(v), c_tile[:, i:i + 1])
    cp["c_tile"] = c_tile

    def PSB(name, shape, dtype=F32):
        t = ctx.enter_context(nc.sbuf_tensor("sb_" + name, list(shape), dtype))
        cp[name] = t
        return t

    PSB("ident_bf", [128, 128], BF16)
    PSB("ident_f", [128, 128])
    PSB("ones_col", [128, 1], BF16)
    PSB("ones_row", [1, 128], BF16)
    PSB("ones_row_f", [1, 128])
    PSB("ones_col_f", [128, 1])
    PSB("ssd_mask", [128, 2 * MCH], BF16)
    for nm in ["r_mean", "r_ex2", "r_m2", "r_var", "r_std", "r_rstd", "r_b"]:
        PSB(nm, [1, 512])
    PSB("ln_a", [128, 512])
    PSB("ln_b", [128, 512])

    cp["pk"] = [ctx.enter_context(nc.psum_tensor(f"pk{i}", [128, 512], F32))
                for i in range(6)]
    cp["pb"] = ctx.enter_context(nc.psum_tensor("pb", [128, 1024], BF16))

    es = cp["es"]
    s_wld = cp["s_wld"]
    for dst, src in [(c_tile, g["consts"]), (cp["ident_bf"], g["ident_bf"]),
                     (cp["ident_f"], g["ident_f"]), (cp["ones_col"], g["ones_col_bf"]),
                     (cp["ones_row"], g["ones_row_bf"]),
                     (cp["ones_row_f"], g["ones_row_f"]),
                     (cp["ones_col_f"], g["ones_col_f"]),
                     (cp["ssd_mask"], g["ssd_mask"])]:
        fin = inc(s_wld, 16)
        es.gpsimd(lambda gp, dst=dst, src=src, fin=fin:
                  fin(gp.dma_start(out=dst[:], in_=src)))
    es.flush("preload")

    if "mamba" in STAGES:
        stage_mamba(nc, ctx, cp)
    if "ffn0" in STAGES:
        stage_attn_block(nc, ctx, cp, kind="ffn", call=0)
    if "attnb" in STAGES:
        stage_attn_block(nc, ctx, cp, kind="attnb", call=0)
    if "ffn1" in STAGES:
        stage_attn_block(nc, ctx, cp, kind="ffn", call=1)

    _finalize(nc, ctx, cp)


def _finalize(nc, ctx, cp):
    g, es = cp["g"], cp["es"]
    s_st = cp["s_st"]
    for s in ["s_pe", "s_ve", "s_se", "s_ge", "s_st", "s_wld", "s_xld", "s_cc"]:
        es.wait("sync", cp[s])
    fin = inc(s_st, 16)
    es.sync(lambda sy, fin=fin: fin(sy.dma_start(out=g["out_f"], in_=g["x_res"])))
    es.wait("sync", s_st)
    es.flush("finalize")


def emit_ln(nc, cp, src_sb, ntok, dst_sb, sq_sb, gamma=None, beta=None):
    es = cp["es"]
    pk = cp["pk"]
    s_pe, s_ve, s_se = cp["s_pe"], cp["s_ve"], cp["s_se"]

    es.wait("scalar", cp["s_wld"])
    es.wait("scalar", s_ve)
    es.wait("vector", cp["s_wld"])
    fin = inc(s_se)
    es.scalar(lambda sc, fin=fin:
              fin(sc.activation(sq_sb[:, 0:FC * ntok], src_sb[:, 0:FC * ntok],
                                AF.Square)))
    tiles = []
    t0 = 0
    while t0 < ntok:
        w = min(512, ntok - t0)
        tiles.append((t0, w))
        t0 += w

    es.wait("tensor", s_se)
    for (t0_, w) in tiles:
        fin = inc(s_pe)
        def stats(t, t0_=t0_, w=w, fin=fin):
            for cc in range(FC):
                t.matmul(pk[0][0:1, 0:w], cp["ones_col"][:, :],
                         src_sb[:, cc * ntok + t0_: cc * ntok + t0_ + w],
                         start=(cc == 0), stop=(cc == FC - 1))
            ins = None
            for cc in range(FC):
                ins = t.matmul(pk[1][0:1, 0:w], cp["ones_col"][:, :],
                               sq_sb[:, cc * ntok + t0_: cc * ntok + t0_ + w],
                               start=(cc == 0), stop=(cc == FC - 1))
            fin(ins)
        es.tensor(stats)

        es.wait("vector", s_pe)
        fin = inc(s_ve)
        es.vector(lambda v, w=w, fin=fin: (
            v.tensor_scalar(cp["r_mean"][0:1, 0:w], pk[0][0:1, 0:w],
                            1.0 / D, None, ALU.mult),
            fin(v.tensor_scalar(cp["r_ex2"][0:1, 0:w], pk[1][0:1, 0:w],
                                1.0 / D, None, ALU.mult))))
        es.wait("scalar", s_ve)
        fin = inc(s_se)
        es.scalar(lambda sc, w=w, fin=fin:
                  fin(sc.activation(cp["r_m2"][0:1, 0:w], cp["r_mean"][0:1, 0:w],
                                    AF.Square)))
        es.wait("vector", s_se)
        fin = inc(s_ve)
        es.vector(lambda v, w=w, fin=fin:
                  fin(v.tensor_tensor(cp["r_var"][0:1, 0:w], cp["r_ex2"][0:1, 0:w],
                                      cp["r_m2"][0:1, 0:w], ALU.subtract)))
        es.wait("scalar", s_ve)
        fin = inc(s_se)
        es.scalar(lambda sc, w=w, fin=fin:
                  fin(sc.activation(cp["r_std"][0:1, 0:w], cp["r_var"][0:1, 0:w],
                                    AF.Sqrt, bias=1e-5)))
        es.wait("vector", s_se)
        fin = inc(s_ve)
        es.vector(lambda v, w=w, fin=fin:
                  fin(v.reciprocal(cp["r_rstd"][0:1, 0:w], cp["r_std"][0:1, 0:w])))
        es.wait("vector", s_ve)
        fin = inc(s_ve)
        es.vector(lambda v, w=w, fin=fin:
                  fin(v.scalar_tensor_tensor(cp["r_b"][0:1, 0:w],
                                             cp["r_mean"][0:1, 0:w], -1.0,
                                             cp["r_rstd"][0:1, 0:w],
                                             ALU.mult, ALU.mult)))
        es.wait("tensor", s_ve)
        fin = inc(s_pe)
        es.tensor(lambda t, w=w, fin=fin: (
            t.matmul(pk[2][:, 0:w], cp["ones_row_f"][:, :], cp["r_rstd"][0:1, 0:w],
                     start=True, stop=True),
            fin(t.matmul(pk[3][:, 0:w], cp["ones_row_f"][:, :], cp["r_b"][0:1, 0:w],
                         start=True, stop=True))))
        es.wait("vector", s_pe)
        fin = inc(s_ve)
        es.vector(lambda v, w=w, fin=fin: (
            v.tensor_copy(cp["ln_a"][:, 0:w], pk[2][:, 0:w]),
            fin(v.tensor_copy(cp["ln_b"][:, 0:w], pk[3][:, 0:w]))))
        es.wait("vector", s_ve)
        fin = inc(s_ve)
        def apply1(v, t0_=t0_, w=w, fin=fin):
            ins = None
            for cc in range(FC):
                ins = v.tensor_tensor(
                    sq_sb[:, cc * ntok + t0_: cc * ntok + t0_ + w],
                    src_sb[:, cc * ntok + t0_: cc * ntok + t0_ + w],
                    cp["ln_a"][:, 0:w], ALU.mult)
            fin(ins)
        es.vector(apply1)
        es.wait("vector", s_ve)
        fin = inc(s_ve)
        def apply2(v, t0_=t0_, w=w, fin=fin):
            ins = None
            for cc in range(FC):
                ins = v.tensor_tensor(
                    dst_sb[:, cc * ntok + t0_: cc * ntok + t0_ + w],
                    sq_sb[:, cc * ntok + t0_: cc * ntok + t0_ + w],
                    cp["ln_b"][:, 0:w], ALU.add)
            fin(ins)
        es.vector(apply2)
        if gamma is not None:
            es.wait("vector", s_ve)
            fin = inc(s_ve)
            def apply3(v, t0_=t0_, w=w, fin=fin):
                ins = None
                for cc in range(FC):
                    ins = v.tensor_scalar(
                        dst_sb[:, cc * ntok + t0_: cc * ntok + t0_ + w],
                        dst_sb[:, cc * ntok + t0_: cc * ntok + t0_ + w],
                        gamma[:, cc:cc + 1], beta[:, cc:cc + 1],
                        ALU.mult, ALU.add)
                fin(ins)
            es.vector(apply3)
        es.wait("tensor", s_ve)


def stage_mamba(nc, ctx, cp):
    g, es = cp["g"], cp["es"]

    with ExitStack() as octx:
        def OSB(name, shape, dtype=F32):
            t = octx.enter_context(nc.sbuf_tensor("sb_" + name, list(shape), dtype))
            cp[name] = t
            return t

        OSB("normw_bc", [128, INTER], BF16)
        OSB("D_bc", [128, MH])
        OSB("row_D", [1, MH])
        OSB("m_dt", [MH, TQ])
        OSB("m_acum", [MH, TQ])
        OSB("m_decbc", [128, MH * NCHK])
        OSB("m_pfxbc", [128, MH * NCHK])
        OSB("m_alrow", [1, MH * NCHK])
        OSB("m_convw", [128, HBC_MT * 4])
        OSB("m_convb", [128, HBC_MT])
        OSB("m_Alog", [MH, 1]); OSB("m_dtb", [MH, 1]); OSB("m_A", [MH, 1])
        OSB("m_alast", [MH, NCHK]); OSB("m_adsum", [MH, 1])
        OSB("m_pfx", [MH, NCHK])
        OSB("m_qmask", [MH, 4]); OSB("m_scmask", [MH, 16])
        OSB("m_pmask", [MH, 4])
        OSB("m_dtraw", [MH, TQH])

        with ExitStack() as cprep:
            cp["row_nw"] = cprep.enter_context(
                nc.sbuf_tensor("sb_row_nw", [1, INTER], F32))
            _mamba_prep(nc, cp)

        with ExitStack() as cx:
            xb = cx.enter_context(nc.sbuf_tensor("sb_m_xb", [128, FC * TQH], BF16))
            cp["m_xb"] = xb
            with ExitStack() as c1:
                cp["lnsq"] = c1.enter_context(
                    nc.sbuf_tensor("sb_lnsq", [128, FC * TQH], BF16))
                fin = inc(cp["s_wld"], 16)
                es.gpsimd(lambda gp, fin=fin: fin(gp.dma_start(
                    out=xb[:].rearrange("p (cc t) -> p cc t", cc=FC),
                    in_=g["x_f"].rearrange("(cc p) t -> p cc t", p=128))))
                es.flush("m_loadx")
                if "ln" in M_PARTS:
                    emit_ln(nc, cp, xb, TQH, xb, cp["lnsq"])
                if "tap_xn" in DEBUG_TAPS:
                    es.wait("sync", cp["s_ve"])
                    fin = inc(cp["s_st"], 16)
                    es.sync(lambda sy, fin=fin:
                            fin(sy.dma_start(out=g["tap_xn"], in_=xb[:, :])))
                es.flush("m_ln")

            with ExitStack() as c2:
                def SB2(name, shape, dtype=F32):
                    t = c2.enter_context(
                        nc.sbuf_tensor("sb_" + name, list(shape), dtype))
                    cp[name] = t
                    return t
                SB2("m_wslab0", [128, FC * 128], BF16)
                SB2("m_wslab1", [128, FC * 128], BF16)
                SB2("m_craw0", [128, TQH], BF16)
                SB2("m_craw1", [128, TQH], BF16)
                SB2("m_ctmp0", [128, TQ], BF16)
                SB2("m_ctmp1", [128, TQ], BF16)
                SB2("m_hstage0", [128, TQ], BF16)
                SB2("m_hstage1", [128, TQ], BF16)
                if "inprojA" in M_PARTS:
                    _inproj_hbc_dt(nc, cp)
                es.flush("m_inprojA")
            with ExitStack() as c2:
                def SB2(name, shape, dtype=F32):
                    t = c2.enter_context(
                        nc.sbuf_tensor("sb_" + name, list(shape), dtype))
                    cp[name] = t
                    return t
                SB2("m_gslab0", [128, FC * 512], BF16)
                SB2("m_gslab1", [128, FC * 512], BF16)
                SB2("m_gstage0", [128, 512], BF16)
                SB2("m_gstage1", [128, 512], BF16)
                if "inprojB" in M_PARTS:
                    _inproj_gate(nc, cp)
                es.flush("m_inprojB")
        with ExitStack() as c2:
            def SB2(name, shape, dtype=F32):
                t = c2.enter_context(nc.sbuf_tensor("sb_" + name, list(shape), dtype))
                cp[name] = t
                return t
            SB2("m_t1", [MH, TQ]); SB2("m_t2", [MH, TQ]); SB2("m_zeros", [MH, MCH])
            if "dt" in M_PARTS:
                _dt_path(nc, cp)
            es.flush("m_dt")

        with ExitStack() as c3:
            def SB3(name, shape, dtype=F32):
                t = c3.enter_context(nc.sbuf_tensor("sb_" + name, list(shape), dtype))
                cp[name] = t
                return t
            SB3("m_hbc", [128, HBC_MT * MCH], BF16)
            SB3("m_dtT", [128, 2 * MH])
            SB3("m_acT", [128, 2 * MH])
            SB3("m_dsT", [128, 2 * MH])
            SB3("m_ds", [MH, MCH])
            SB3("m_xdtT", [128, 2 * MHD], BF16)
            SB3("m_xrawT", [128, 2 * MHD], BF16)
            SB3("m_bT", [128, 2 * MST], BF16)
            SB3("m_bdTh", [128, 2 * MST], BF16)
            SB3("m_sinc", [128, MH * MHD], BF16)
            with ExitStack() as c4:
                def SB4(name, shape, dtype=F32):
                    t = c4.enter_context(
                        nc.sbuf_tensor("sb_" + name, list(shape), dtype))
                    cp[name] = t
                    return t
                SB4("m_srun", [128, MH * MHD])
                SB4("m_prev", [128, MH * MHD])
                SB4("m_slocp", [128, MH * MHD], BF16)
                SB4("m_slocc", [128, MH * MHD], BF16)
                SB4("m_sstage", [128, MH * MHD], BF16)
                if "sloc" in M_PARTS:
                    _ssd_local_states(nc, cp)
                es.flush("m_sloc")
                with ExitStack() as c5:
                    def SB5(name, shape, dtype=F32):
                        t = c5.enter_context(
                            nc.sbuf_tensor("sb_" + name, list(shape), dtype))
                        cp[name] = t
                        return t
                    SB5("m_adsq", [MH, 4]); SB5("m_wexp", [MH, 4])
                    SB5("m_stq", [128, 4 * MH * MHD], BF16)
                    SB5("m_wrow", [1, 4 * MH]); SB5("m_wbc", [128, 4 * MH])
                    if "gather" in M_PARTS:
                        _state_gather_combine(nc, cp)
                    es.flush("m_gather")
                if "sbk" in M_PARTS:
                    _state_backward(nc, cp)
                es.flush("m_sbk")
            SB3("m_acrow", [1, (MH // 2) * MCH])
            SB3("m_gm", [128, 2 * MCH])
            SB3("m_eh", [128, 2 * MCH])
            SB3("m_mh", [128, 2 * MCH], BF16)
            SB3("m_csc", [128, MCH], BF16)
            SB3("m_sdbc", [128, MCH])
            SB3("m_ych", [128, 2 * INTER], BF16)
            SB3("m_rsum", [128, 1]); SB3("m_rstd", [128, 1])
            SB3("m_gatec", [128, 2 * INTER], BF16)
            SB3("m_ynf", [128, (INTER // 128) * 1024], BF16)
            if "ssd" in M_PARTS:
                _ssd_main(nc, cp)
            es.flush("m_ssd")


def _mamba_prep(nc, cp):
    g, es = cp["g"], cp["es"]
    pk = cp["pk"]
    s_wld, s_pe, s_se = cp["s_wld"], cp["s_pe"], cp["s_se"]

    for dst, src in [(cp["row_nw"], g["norm_w_row"]), (cp["row_D"], g["D_row"]),
                     (cp["m_convw"], g["conv_w_c"]), (cp["m_convb"], g["conv_b_c"]),
                     (cp["m_Alog"], g["A_log"]), (cp["m_dtb"], g["dt_bias"]),
                     (cp["m_qmask"], g["qmask"]), (cp["m_scmask"], g["scmask"]),
                     (cp["m_pmask"], g["pmask"])]:
        fin = inc(s_wld, 16)
        es.gpsimd(lambda gp, dst=dst, src=src, fin=fin:
                  fin(gp.dma_start(out=dst[:], in_=src)))
    es.wait("tensor", s_wld)
    nseg = INTER // 512
    for j4 in range(0, nseg, 4):
        fin = inc(s_pe)
        def bc4(t, j4=j4, fin=fin):
            ins = None
            for j in range(j4, min(j4 + 4, nseg)):
                ins = t.matmul(pk[j % 4][:, :], cp["ones_row_f"][:, :],
                               cp["row_nw"][0:1, j * 512:(j + 1) * 512],
                               start=True, stop=True)
            fin(ins)
        es.tensor(bc4)
        es.wait("scalar", s_pe)
        fin = inc(s_se)
        def ev4(sc, j4=j4, fin=fin):
            ins = None
            for j in range(j4, min(j4 + 4, nseg)):
                ins = sc.copy(cp["normw_bc"][:, j * 512:(j + 1) * 512],
                              pk[j % 4][:, :])
            fin(ins)
        es.scalar(ev4)
        es.wait("tensor", s_se)
    fin = inc(s_pe)
    es.tensor(lambda t, fin=fin:
              fin(t.matmul(pk[0][:, 0:MH], cp["ones_row_f"][:, :],
                           cp["row_D"][0:1, :], start=True, stop=True)))
    es.wait("scalar", s_pe)
    fin = inc(s_se)
    es.scalar(lambda sc, fin=fin: fin(sc.copy(cp["D_bc"][:, :], pk[0][:, 0:MH])))
    es.wait("tensor", s_se)
    es.flush("m_prep")


def _inproj_hbc_dt(nc, cp):
    g, es = cp["g"], cp["es"]
    pk = cp["pk"]
    s_wld, s_pe, s_ve, s_se, s_st = (
        cp[k] for k in ["s_wld", "s_pe", "s_ve", "s_se", "s_st"])
    xn = cp["m_xb"]

    ntiles = [(t0, min(512, TQH - t0)) for t0 in range(0, TQH, 512)]
    bank_guard = {0: 0, 1: 0}
    NMT = HBC_MT + 1
    pe_after_m = []
    cons_after_m = []

    for m in range(NMT):
        mw = 128 if m < HBC_MT else MH
        m0 = INTER + m * 128
        buf = m % 2
        if m >= 2:
            es.wait("gpsimd", s_pe, pe_after_m[m - 1])
        fin = inc(s_wld, 16)
        es.gpsimd(lambda gp, m0=m0, mw=mw, buf=buf, fin=fin: fin(gp.dma_start(
            out=cp[f"m_wslab{buf}"][:, 0:FC * mw].rearrange(
                "p (kc m) -> p kc m", kc=FC),
            in_=g["w_inproj_T"][:, m0:m0 + mw].rearrange(
                "(kc p) m -> p kc m", p=128))))
        es.wait("tensor", s_wld)
        if m >= 2:
            es.wait("tensor", s_se, cons_after_m[m - 2][1])
            es.wait("tensor", s_ve, cons_after_m[m - 2][0])
        for ni, (t0_, w) in enumerate(ntiles):
            if bank_guard[ni % 2]:
                es.wait("tensor", s_se, bank_guard[ni % 2])
            fin = inc(s_pe)
            def mms(t, mw=mw, buf=buf, ni=ni, t0_=t0_, w=w, fin=fin):
                bank = pk[ni % 2]
                ins = None
                for kc in range(FC):
                    ins = t.matmul(bank[0:mw, 0:w],
                                   cp[f"m_wslab{buf}"][:, kc * mw:kc * mw + mw],
                                   xn[:, kc * TQH + t0_: kc * TQH + t0_ + w],
                                   start=(kc == 0), stop=(kc == FC - 1))
                fin(ins)
            es.tensor(mms)
            es.wait("scalar", s_pe)
            fin = inc(s_se)
            es.scalar(lambda sc, mw=mw, buf=buf, ni=ni, t0_=t0_, w=w, fin=fin:
                      fin(sc.copy(cp[f"m_craw{buf}"][0:mw, t0_:t0_ + w],
                                  pk[ni % 2][0:mw, 0:w])))
            bank_guard[ni % 2] = s_se.n
        if m < HBC_MT:
            cw = cp["m_convw"]
            es.wait("vector", s_se)
            fin = inc(s_ve)
            es.vector(lambda v, m=m, buf=buf, fin=fin:
                      fin(v.tensor_scalar(cp[f"m_ctmp{buf}"][:, 0:TQ],
                                          cp[f"m_craw{buf}"][:, 1:1 + TQ],
                                          cw[:, 4 * m:4 * m + 1], None, ALU.mult)))
            for j in range(1, 4):
                es.wait("vector", s_ve)
                fin = inc(s_ve)
                es.vector(lambda v, m=m, j=j, buf=buf, fin=fin:
                          fin(v.scalar_tensor_tensor(
                              cp[f"m_ctmp{buf}"][:, 0:TQ],
                              cp[f"m_craw{buf}"][:, 1 + j:1 + j + TQ],
                              cw[:, 4 * m + j:4 * m + j + 1],
                              cp[f"m_ctmp{buf}"][:, 0:TQ], ALU.mult, ALU.add)))
            es.wait("scalar", s_ve)
            fin = inc(s_se)
            es.scalar(lambda sc, m=m, buf=buf, fin=fin:
                      fin(sc.activation(cp[f"m_hstage{buf}"][:, 0:TQ],
                                        cp[f"m_ctmp{buf}"][:, 0:TQ], AF.Silu,
                                        bias=cp["m_convb"][:, m:m + 1])))
            es.wait("sync", s_se)
            fin = inc(s_st, 16)
            es.sync(lambda sy, m=m, buf=buf, fin=fin:
                    fin(sy.dma_start(out=g["hbc_d"][m * 128:(m + 1) * 128, :],
                                     in_=cp[f"m_hstage{buf}"][:, 0:TQ])))
        else:
            es.wait("vector", s_se)
            fin = inc(s_ve)
            es.vector(lambda v, buf=buf, fin=fin:
                      fin(v.tensor_copy(cp["m_dtraw"][:, 0:TQH],
                                        cp[f"m_craw{buf}"][0:MH, 0:TQH])))
        pe_after_m.append(s_pe.n)
        cons_after_m.append((s_ve.n, s_se.n))

    if "tap_hbc" in DEBUG_TAPS:
        es.wait("sync", s_st)
        fin = inc(s_st, 16)
        es.sync(lambda sy, fin=fin:
                fin(sy.dma_start(out=g["tap_hbc"], in_=g["hbc_d"])))


def _inproj_gate(nc, cp):
    g, es = cp["g"], cp["es"]
    pk = cp["pk"]
    s_wld, s_pe, s_se, s_st = (cp[k] for k in ["s_wld", "s_pe", "s_se", "s_st"])
    xn = cp["m_xb"]

    NMB = INTER // 512
    NTT = TQ // 128
    pe_after_b = []
    se_at = {}
    st_hist = {}

    for mb in range(NMB):
        buf = mb % 2
        if mb >= 2:
            es.wait("gpsimd", s_pe, pe_after_b[mb - 1])
        fin = inc(s_wld, 16)
        es.gpsimd(lambda gp, mb=mb, buf=buf, fin=fin: fin(gp.dma_start(
            out=cp[f"m_gslab{buf}"][:].rearrange("p (kc m) -> p kc m", kc=FC),
            in_=g["w_inproj_T"][:, mb * 512:(mb + 1) * 512].rearrange(
                "(kc p) m -> p kc m", p=128))))
        es.wait("tensor", s_wld)
        for tt in range(NTT):
            it = mb * NTT + tt
            if it >= 2:
                es.wait("tensor", s_se, se_at[it - 2])
            fin = inc(s_pe)
            def mms(t, tt=tt, buf=buf, fin=fin):
                bank = pk[tt % 2]
                ins = None
                for kc in range(FC):
                    ins = t.matmul(
                        bank[:, 0:512],
                        xn[:, kc * TQH + XOFF + tt * 128:
                           kc * TQH + XOFF + tt * 128 + 128],
                        cp[f"m_gslab{buf}"][:, kc * 512:(kc + 1) * 512],
                        start=(kc == 0), stop=(kc == FC - 1))
                fin(ins)
            es.tensor(mms)
            es.wait("scalar", s_pe)
            if it >= 2:
                es.wait("scalar", s_st, st_hist[it - 2])
            fin = inc(s_se)
            es.scalar(lambda sc, tt=tt, fin=fin:
                      fin(sc.activation(cp[f"m_gstage{tt % 2}"][:, 0:512],
                                        pk[tt % 2][:, 0:512], AF.Silu)))
            se_at[it] = s_se.n
            es.wait("sync", s_se)
            fin = inc(s_st, 16)
            es.sync(lambda sy, mb=mb, tt=tt, fin=fin:
                    fin(sy.dma_start(
                        out=g["gate_d"][tt * 128:(tt + 1) * 128,
                                        mb * 512:(mb + 1) * 512],
                        in_=cp[f"m_gstage{tt % 2}"][:, 0:512])))
            st_hist[it] = s_st.n
        pe_after_b.append(s_pe.n)

    if "tap_gate" in DEBUG_TAPS:
        es.wait("sync", s_st)
        fin = inc(s_st, 16)
        es.sync(lambda sy, fin=fin:
                fin(sy.dma_start(out=g["tap_gate"], in_=g["gate_d"])))


def _dt_path(nc, cp):
    g, es = cp["g"], cp["es"]
    s_wld, s_pe, s_ve, s_se, s_st = (
        cp[k] for k in ["s_wld", "s_pe", "s_ve", "s_se", "s_st"])
    dt = cp["m_dt"]
    acum = cp["m_acum"]
    t1, t2 = cp["m_t1"], cp["m_t2"]
    dtr = cp["m_dtraw"]

    def step(eng, sem, fn):
        es.wait(eng, sem)
        fin = inc(cp[{"vector": "s_ve", "scalar": "s_se"}[eng]])
        getattr(es, eng)(lambda E, fn=fn, fin=fin: fin(fn(E)))

    es.wait("vector", s_ve)
    fin = inc(s_ve)
    es.vector(lambda v, fin=fin: (
        v.memset(cp["m_zeros"][:], 0.0),
        fin(v.tensor_scalar(t1[:, :], dtr[:, XOFF:XOFF + TQ],
                            cp["m_dtb"][:, 0:1], None, ALU.add))))
    step("scalar", s_ve,
         lambda sc: sc.activation(t2[:, :], t1[:, :], AF.Abs))
    step("scalar", s_se,
         lambda sc: sc.activation(t2[:, :], t2[:, :], AF.Exp, scale=-1.0))
    step("scalar", s_se,
         lambda sc: sc.activation(t2[:, :], t2[:, :], AF.Ln, bias=1.0))
    step("vector", s_ve,
         lambda v: v.tensor_scalar(dt[:, :], t1[:, :], 0.0, None, ALU.max))
    es.wait("vector", s_se)
    step("vector", s_ve,
         lambda v: v.tensor_tensor(dt[:, :], dt[:, :], t2[:, :], ALU.add))
    step("vector", s_ve,
         lambda v: v.tensor_scalar(dt[:, :], dt[:, :], 100.0, None, ALU.min))
    es.wait("scalar", s_wld)
    step("scalar", s_se,
         lambda sc: sc.activation(cp["m_A"][:, :], cp["m_Alog"][:, :], AF.Exp))
    es.wait("vector", s_se)
    step("vector", s_ve,
         lambda v: v.tensor_scalar(cp["m_A"][:, :], cp["m_A"][:, :],
                                   -1.0, None, ALU.mult))
    step("vector", s_ve,
         lambda v: v.tensor_scalar(t1[:, :], dt[:, :], cp["m_A"][:, 0:1],
                                   None, ALU.mult))

    es.wait("vector", s_ve)
    fin = inc(s_ve)
    def scans(v, fin=fin):
        ins = None
        for c in range(NCHK):
            ins = v.tensor_tensor_scan(acum[:, c * MCH:(c + 1) * MCH],
                                       cp["m_zeros"][:, :],
                                       t1[:, c * MCH:(c + 1) * MCH],
                                       0.0, ALU.add, ALU.add)
        fin(ins)
    es.vector(scans)
    es.wait("vector", s_ve)
    fin = inc(s_ve)
    def lasts(v, fin=fin):
        ins = None
        for c in range(NCHK):
            ins = v.tensor_copy(cp["m_alast"][:, c:c + 1],
                                acum[:, (c + 1) * MCH - 1:(c + 1) * MCH])
        fin(ins)
    es.vector(lasts)
    es.wait("vector", s_ve)
    fin = inc(s_ve)
    es.vector(lambda v, fin=fin:
              fin(v.reduce_sum(cp["m_adsum"][:, :], cp["m_alast"][:, :],
                               axis=AX.X)))
    es.wait("vector", s_ve)
    fin = inc(s_ve)
    es.vector(lambda v, fin=fin:
              fin(v.tensor_tensor_scan(cp["m_pfx"][:, :], cp["m_zeros"][:, 0:NCHK],
                                       cp["m_alast"][:, :], 0.0, ALU.add, ALU.add)))
    es.wait("scalar", s_ve)
    fin = inc(s_se)
    es.scalar(lambda sc, fin=fin: (
        sc.activation(cp["m_pfx"][:, :], cp["m_pfx"][:, :], AF.Exp),
        fin(sc.activation(cp["m_alast"][:, :], cp["m_alast"][:, :], AF.Exp))))
    es.wait("sync", s_se)
    es.wait("sync", s_ve)
    for fn in [
        lambda sy, fin: fin(sy.dma_start(out=g["acum_d"], in_=acum[:, :])),
        lambda sy, fin: fin(sy.dma_start(
            out=g["alast_d"].rearrange("one (h c) -> (one h) c", h=MH),
            in_=cp["m_alast"][:, :])),
        lambda sy, fin: fin(sy.dma_start(out=g["adsum_in"], in_=cp["m_adsum"][:, :])),
        lambda sy, fin: fin(sy.dma_start(
            out=g["pfx_d"].rearrange("one (h c) -> (one h) c", h=MH),
            in_=cp["m_pfx"][:, :])),
    ]:
        fin = inc(s_st, 16)
        es.sync(lambda sy, fn=fn, fin=fin: fn(sy, fin))
    if "tap_dt" in DEBUG_TAPS:
        fin = inc(s_st, 16)
        es.sync(lambda sy, fin=fin: fin(sy.dma_start(out=g["tap_dt"], in_=dt[:, :])))
    if "tap_acum" in DEBUG_TAPS:
        fin = inc(s_st, 16)
        es.sync(lambda sy, fin=fin:
                fin(sy.dma_start(out=g["tap_acum"], in_=acum[:, :])))

    es.wait("gpsimd", s_st)
    fin = inc(s_wld, 16)
    es.gpsimd(lambda gp, fin=fin:
              fin(gp.dma_start(out=cp["m_alrow"][:, :], in_=g["alast_d"])))
    es.wait("tensor", s_wld)
    es.wait("tensor", s_ve)
    fin = inc(cp["s_pe"])
    es.tensor(lambda t, fin=fin:
              fin(t.matmul(cp["pk"][4][:, 0:MH * NCHK], cp["ones_row_f"][:, :],
                           cp["m_alrow"][0:1, :], start=True, stop=True)))
    es.wait("vector", cp["s_pe"])
    fin = inc(s_ve)
    es.vector(lambda v, fin=fin:
              fin(v.tensor_copy(cp["m_decbc"][:, :], cp["pk"][4][:, 0:MH * NCHK])))
    es.wait("gpsimd", s_ve)
    fin = inc(s_wld, 16)
    es.gpsimd(lambda gp, fin=fin:
              fin(gp.dma_start(out=cp["m_alrow"][:, :], in_=g["pfx_d"])))
    es.wait("tensor", s_wld)
    es.wait("tensor", s_ve)
    fin = inc(cp["s_pe"])
    es.tensor(lambda t, fin=fin:
              fin(t.matmul(cp["pk"][4][:, 0:MH * NCHK], cp["ones_row_f"][:, :],
                           cp["m_alrow"][0:1, :], start=True, stop=True)))
    es.wait("vector", cp["s_pe"])
    fin = inc(s_ve)
    es.vector(lambda v, fin=fin:
              fin(v.tensor_copy(cp["m_pfxbc"][:, :], cp["pk"][4][:, 0:MH * NCHK])))


def _ssd_local_states(nc, cp):
    g, es = cp["g"], cp["es"]
    pk, pb = cp["pk"], cp["pb"]
    s_wld, s_pe, s_ve, s_se, s_cc, s_st = (
        cp[k] for k in ["s_wld", "s_pe", "s_ve", "s_se", "s_cc", "s_st"])
    hbc = cp["m_hbc"]
    acum = cp["m_acum"]
    dt_sb = cp["m_dt"]

    for c in range(NCHK):
        es.wait("gpsimd", s_st)
        es.wait("gpsimd", s_pe)
        fin = inc(s_wld, 16)
        es.gpsimd(lambda gp, c=c, fin=fin: fin(gp.dma_start(
            out=hbc[:].rearrange("p (cc t) -> p cc t", cc=HBC_MT),
            in_=g["hbc_d"][:, c * MCH:(c + 1) * MCH].rearrange(
                "(cc p) t -> p cc t", p=128))))
        es.wait("vector", s_ve)
        es.wait("vector", s_pe)
        fin = inc(s_ve)
        es.vector(lambda v, c=c, fin=fin:
                  fin(v.tensor_scalar(cp["m_ds"][:, :],
                                      acum[:, c * MCH:(c + 1) * MCH],
                                      acum[:, (c + 1) * MCH - 1:(c + 1) * MCH],
                                      None, ALU.subtract)))
        es.wait("scalar", s_ve)
        fin = inc(s_se)
        es.scalar(lambda sc, fin=fin:
                  fin(sc.activation(cp["m_ds"][:, :], cp["m_ds"][:, :],
                                    AF.Exp, scale=-1.0)))
        es.wait("tensor", s_se)
        es.wait("tensor", s_wld)
        es.wait("tensor", s_ve)
        fin = inc(s_pe)
        es.tensor(lambda t, c=c, fin=fin: (
            t.transpose(pk[4][0:128, 0:MH], dt_sb[:, c * MCH: c * MCH + 128],
                        cp["ident_f"][0:MH, 0:MH]),
            t.transpose(pk[5][0:128, 0:MH], dt_sb[:, c * MCH + 128: c * MCH + 256],
                        cp["ident_f"][0:MH, 0:MH]),
            t.transpose(pk[4][0:128, MH:2 * MH], cp["m_ds"][:, 0:128],
                        cp["ident_f"][0:MH, 0:MH]),
            fin(t.transpose(pk[5][0:128, MH:2 * MH], cp["m_ds"][:, 128:256],
                            cp["ident_f"][0:MH, 0:MH]))))
        es.wait("vector", s_pe)
        fin = inc(s_ve)
        es.vector(lambda v, fin=fin: (
            v.tensor_copy(cp["m_dtT"][:, 0:MH], pk[4][:, 0:MH]),
            v.tensor_copy(cp["m_dtT"][:, MH:2 * MH], pk[5][:, 0:MH]),
            v.tensor_copy(cp["m_dsT"][:, 0:MH], pk[4][:, MH:2 * MH]),
            fin(v.tensor_copy(cp["m_dsT"][:, MH:2 * MH], pk[5][:, MH:2 * MH]))))
        fin = inc(s_pe)
        es.tensor(lambda t, fin=fin: (
            t.transpose(pb[:, 0:128], hbc[:, 32 * MCH:32 * MCH + 128],
                        cp["ident_bf"][:, :]),
            fin(t.transpose(pb[:, 128:256], hbc[:, 32 * MCH + 128:32 * MCH + 256],
                            cp["ident_bf"][:, :]))))
        es.wait("vector", s_pe)
        fin = inc(s_ve)
        es.vector(lambda v, fin=fin: (
            v.tensor_copy(cp["m_bT"][:, 0:128], pb[:, 0:128]),
            fin(v.tensor_copy(cp["m_bT"][:, 128:256], pb[:, 128:256]))))

        for h in range(MH):
            es.wait("tensor", s_ve)
            fin = inc(s_pe)
            es.tensor(lambda t, h=h, fin=fin: (
                t.transpose(pb[:, 256:384], hbc[:, h * MCH:h * MCH + 128],
                            cp["ident_bf"][:, :]),
                fin(t.transpose(pb[:, 384:512], hbc[:, h * MCH + 128:h * MCH + 256],
                                cp["ident_bf"][:, :]))))
            es.wait("vector", s_pe)
            fin = inc(s_ve)
            es.vector(lambda v, h=h, fin=fin: (
                v.tensor_scalar(cp["m_xdtT"][:, 0:MHD], pb[:, 256:384],
                                cp["m_dtT"][:, h:h + 1], None, ALU.mult),
                v.tensor_scalar(cp["m_xdtT"][:, MHD:2 * MHD], pb[:, 384:512],
                                cp["m_dtT"][:, MH + h:MH + h + 1], None, ALU.mult),
                v.tensor_scalar(cp["m_bdTh"][:, 0:MST], cp["m_bT"][:, 0:MST],
                                cp["m_dsT"][:, h:h + 1], None, ALU.mult),
                fin(v.tensor_scalar(cp["m_bdTh"][:, MST:2 * MST],
                                    cp["m_bT"][:, MST:2 * MST],
                                    cp["m_dsT"][:, MH + h:MH + h + 1],
                                    None, ALU.mult))))
            es.wait("tensor", s_ve)
            fin = inc(s_pe)
            es.tensor(lambda t, h=h, fin=fin: (
                t.matmul(pk[h % 2][:, 0:MHD], cp["m_bdTh"][:, 0:MST],
                         cp["m_xdtT"][:, 0:MHD], start=True, stop=False),
                fin(t.matmul(pk[h % 2][:, 0:MHD], cp["m_bdTh"][:, MST:2 * MST],
                             cp["m_xdtT"][:, MHD:2 * MHD], start=False, stop=True))))
            es.wait("vector", s_pe)
            fin = inc(s_ve)
            if c == 0:
                es.vector(lambda v, h=h, fin=fin: (
                    v.tensor_copy(cp["m_sstage"][:, h * MHD:(h + 1) * MHD],
                                  pk[h % 2][:, 0:MHD]),
                    fin(v.tensor_scalar(cp["m_srun"][:, h * MHD:(h + 1) * MHD],
                                        pk[h % 2][:, 0:MHD],
                                        cp["m_pfxbc"][:, h * NCHK:h * NCHK + 1],
                                        None, ALU.mult))))
            else:
                es.vector(lambda v, h=h, c=c, fin=fin: (
                    v.tensor_copy(cp["m_sstage"][:, h * MHD:(h + 1) * MHD],
                                  pk[h % 2][:, 0:MHD]),
                    fin(v.scalar_tensor_tensor(
                        cp["m_srun"][:, h * MHD:(h + 1) * MHD],
                        pk[h % 2][:, 0:MHD],
                        cp["m_pfxbc"][:, h * NCHK + c:h * NCHK + c + 1],
                        cp["m_srun"][:, h * MHD:(h + 1) * MHD],
                        ALU.mult, ALU.add))))
        es.wait("sync", s_ve)
        fin = inc(s_st, 16)
        es.sync(lambda sy, c=c, fin=fin: fin(sy.dma_start(
            out=g["sloc_d"][c * INTER:(c + 1) * INTER, :].rearrange(
                "(h n) p -> n h p", n=128),
            in_=cp["m_sstage"][:, :].rearrange("n (h p) -> n h p", h=MH))))

    es.wait("vector", s_ve)
    es.wait("vector", s_st)
    fin = inc(s_ve)
    es.vector(lambda v, fin=fin:
              fin(v.tensor_copy(cp["m_sstage"][:, :], cp["m_srun"][:, :])))
    es.wait("sync", s_ve)
    fin = inc(s_st, 16)
    es.sync(lambda sy, fin=fin: fin(sy.dma_start(
        out=g["stq_in"].rearrange("(h n) p -> n h p", n=128),
        in_=cp["m_sstage"][:, :].rearrange("n (h p) -> n h p", h=MH))))
    if "tap_sloc" in DEBUG_TAPS:
        fin = inc(s_st, 16)
        es.sync(lambda sy, fin=fin:
                fin(sy.dma_start(out=g["tap_sloc"], in_=g["sloc_d"])))
    if "tap_stq" in DEBUG_TAPS:
        es.wait("sync", s_st)
        fin = inc(s_st, 16)
        es.sync(lambda sy, fin=fin:
                fin(sy.dma_start(out=g["tap_stq"], in_=g["stq_in"])))
    if "tap_dec" in DEBUG_TAPS:
        fin = inc(s_st, 16)
        es.sync(lambda sy, fin=fin:
                fin(sy.dma_start(out=g["tap_dec"][:, 0:MH * NCHK],
                                 in_=cp["m_decbc"][:, :])))
        fin = inc(s_st, 16)
        es.sync(lambda sy, fin=fin:
                fin(sy.dma_start(out=g["tap_dec"][:, MH * NCHK:],
                                 in_=cp["m_pfxbc"][:, :])))
    es.wait("gpsimd", s_st)
    fin = inc(cp["s_wld"], 16)
    es.gpsimd(lambda gp, fin=fin: fin(gp.dma_start(
        out=g["lastloc_in"], in_=g["sloc_d"][7 * INTER:8 * INTER, :])))
    es.wait("gpsimd", cp["s_wld"])
    fin = inc(s_cc)
    es.gpsimd(lambda gp, fin=fin: fin(gp.collective_compute(
        "AllGather", ALU.bypass, replica_groups=GROUPS4,
        ins=[g["stq_in"]], outs=[g["stq_out"]])))
    fin = inc(s_cc)
    es.gpsimd(lambda gp, fin=fin: fin(gp.collective_compute(
        "AllGather", ALU.bypass, replica_groups=GROUPS4,
        ins=[g["adsum_in"]], outs=[g["adsum_out"]])))
    fin = inc(s_cc)
    es.gpsimd(lambda gp, fin=fin: fin(gp.collective_compute(
        "AllGather", ALU.bypass, replica_groups=GROUPS4,
        ins=[g["lastloc_in"]], outs=[g["lastloc_out"]])))


def _state_gather_combine(nc, cp):
    g, es = cp["g"], cp["es"]
    pk = cp["pk"]
    s_wld, s_pe, s_ve, s_se, s_cc, s_st = (
        cp[k] for k in ["s_wld", "s_pe", "s_ve", "s_se", "s_cc", "s_st"])

    es.wait("gpsimd", s_cc)
    fin = inc(s_wld, 16)
    es.gpsimd(lambda gp, fin=fin: fin(gp.dma_start(
        out=cp["m_adsq"][:, :],
        in_=g["adsum_out"].rearrange("(q h) one -> h (q one)", h=MH),
        allow_slow_non_contiguous=True)))
    for qq in range(4):
        fin = inc(s_wld, 16)
        es.gpsimd(lambda gp, qq=qq, fin=fin: fin(gp.dma_start(
            out=cp["m_stq"][:, qq * MH * MHD:(qq + 1) * MH * MHD].rearrange(
                "n (h p) -> n h p", h=MH),
            in_=g["stq_out"][qq * INTER:(qq + 1) * INTER, :].rearrange(
                "(h n) p -> n h p", n=128))))
    es.wait("vector", s_wld)
    for m in range(4):
        fin = inc(s_ve)
        es.vector(lambda v, m=m, fin=fin:
                  fin(v.tensor_scalar(cp["m_wexp"][:, m:m + 1],
                                      cp["m_adsq"][:, 0:1],
                                      cp["m_scmask"][:, 4 * m:4 * m + 1],
                                      None, ALU.mult)))
        for k in range(1, 4):
            es.wait("vector", s_ve)
            fin = inc(s_ve)
            es.vector(lambda v, m=m, k=k, fin=fin:
                      fin(v.scalar_tensor_tensor(
                          cp["m_wexp"][:, m:m + 1], cp["m_adsq"][:, k:k + 1],
                          cp["m_scmask"][:, 4 * m + k:4 * m + k + 1],
                          cp["m_wexp"][:, m:m + 1], ALU.mult, ALU.add)))
        es.wait("vector", s_ve)
    fin = inc(s_se)
    es.scalar(lambda sc, fin=fin:
              fin(sc.activation(cp["m_wexp"][:, :], cp["m_wexp"][:, :], AF.Exp)))
    es.wait("vector", s_se)
    fin = inc(s_ve)
    es.vector(lambda v, fin=fin:
              fin(v.tensor_tensor(cp["m_wexp"][:, :], cp["m_wexp"][:, :],
                                  cp["m_qmask"][:, :], ALU.mult)))
    es.wait("sync", s_ve)
    fin = inc(s_st, 16)
    es.sync(lambda sy, fin=fin: fin(sy.dma_start(
        out=g["wbase_d"].rearrange("one (h q) -> (one h) q", h=MH),
        in_=cp["m_wexp"][:, :])))
    es.wait("gpsimd", s_st)
    fin = inc(s_wld, 16)
    es.gpsimd(lambda gp, fin=fin:
              fin(gp.dma_start(out=cp["m_wrow"][:, :], in_=g["wbase_d"])))
    es.wait("tensor", s_wld)
    es.wait("tensor", s_ve)
    fin = inc(s_pe)
    es.tensor(lambda t, fin=fin:
              fin(t.matmul(pk[4][:, 0:4 * MH], cp["ones_row_f"][:, :],
                           cp["m_wrow"][0:1, :], start=True, stop=True)))
    es.wait("vector", s_pe)
    fin = inc(s_ve)
    es.vector(lambda v, fin=fin:
              fin(v.tensor_copy(cp["m_wbc"][:, :], pk[4][:, 0:4 * MH])))
    es.wait("vector", s_ve)
    def partials(dst, m, accumulate):
        fin = inc(s_ve)
        def body(v, dst=dst, m=m, accumulate=accumulate, fin=fin):
            ins = None
            for h in range(MH):
                if accumulate:
                    ins = v.scalar_tensor_tensor(
                        cp[dst][:, h * MHD:(h + 1) * MHD],
                        cp["m_stq"][:, (m * MH + h) * MHD:
                                    (m * MH + h + 1) * MHD],
                        cp["m_wbc"][:, h * 4 + m:h * 4 + m + 1],
                        cp[dst][:, h * MHD:(h + 1) * MHD], ALU.mult, ALU.add)
                else:
                    ins = v.tensor_scalar(
                        cp[dst][:, h * MHD:(h + 1) * MHD],
                        cp["m_stq"][:, (m * MH + h) * MHD:
                                    (m * MH + h + 1) * MHD],
                        cp["m_wbc"][:, h * 4 + m:h * 4 + m + 1], None, ALU.mult)
            fin(ins)
        es.vector(body)

    partials("m_srun", 0, False)
    partials("m_prev", 1, False)
    es.wait("vector", s_ve)
    partials("m_srun", 2, True)
    partials("m_prev", 3, True)
    es.wait("vector", s_ve)
    fin = inc(s_ve)
    es.vector(lambda v, fin=fin:
              fin(v.tensor_tensor(cp["m_srun"][:, :], cp["m_srun"][:, :],
                                  cp["m_prev"][:, :], ALU.add)))
    es.wait("vector", s_ve)

    es.wait("sync", s_ve)
    fin = inc(s_st, 16)
    es.sync(lambda sy, fin=fin: fin(sy.dma_start(
        out=g["pvrow_d"].rearrange("one (h q) -> (one h) q", h=MH),
        in_=cp["m_pmask"][:, :])))
    es.wait("gpsimd", s_st)
    fin = inc(s_wld, 16)
    es.gpsimd(lambda gp, fin=fin:
              fin(gp.dma_start(out=cp["m_wrow"][:, :], in_=g["pvrow_d"])))
    for qq in range(4):
        fin = inc(s_wld, 16)
        es.gpsimd(lambda gp, qq=qq, fin=fin: fin(gp.dma_start(
            out=cp["m_stq"][:, qq * MH * MHD:(qq + 1) * MH * MHD].rearrange(
                "n (h p) -> n h p", h=MH),
            in_=g["lastloc_out"][qq * INTER:(qq + 1) * INTER, :].rearrange(
                "(h n) p -> n h p", n=128))))
    es.wait("tensor", s_wld)
    es.wait("tensor", s_ve)
    fin = inc(s_pe)
    es.tensor(lambda t, fin=fin:
              fin(t.matmul(pk[4][:, 0:4 * MH], cp["ones_row_f"][:, :],
                           cp["m_wrow"][0:1, :], start=True, stop=True)))
    es.wait("vector", s_pe)
    fin = inc(s_ve)
    es.vector(lambda v, fin=fin:
              fin(v.tensor_copy(cp["m_wbc"][:, :], pk[4][:, 0:4 * MH])))
    es.wait("vector", s_ve)
    for m in range(4):
        if m == 0:
            fin = inc(s_ve)
            def p0(v, fin=fin):
                ins = None
                for h in range(MH):
                    ins = v.tensor_scalar(
                        cp["m_prev"][:, h * MHD:(h + 1) * MHD],
                        cp["m_stq"][:, (0 * MH + h) * MHD:
                                    (0 * MH + h + 1) * MHD],
                        cp["m_wbc"][:, h * 4 + 0:h * 4 + 1], None, ALU.mult)
                fin(ins)
            es.vector(p0)
        else:
            es.wait("vector", s_ve)
            fin = inc(s_ve)
            def pacc(v, m=m, fin=fin):
                ins = None
                for h in range(MH):
                    ins = v.scalar_tensor_tensor(
                        cp["m_prev"][:, h * MHD:(h + 1) * MHD],
                        cp["m_stq"][:, (m * MH + h) * MHD:
                                    (m * MH + h + 1) * MHD],
                        cp["m_wbc"][:, h * 4 + m:h * 4 + m + 1],
                        cp["m_prev"][:, h * MHD:(h + 1) * MHD],
                        ALU.mult, ALU.add)
                fin(ins)
            es.vector(pacc)
    es.wait("vector", s_ve)


def _state_backward(nc, cp):
    g, es = cp["g"], cp["es"]
    s_wld, s_ve, s_st = cp["s_wld"], cp["s_ve"], cp["s_st"]

    def load_sloc(c, dst):
        es.wait("gpsimd", s_ve)
        fin = inc(s_wld, 16)
        es.gpsimd(lambda gp, c=c, dst=dst, fin=fin: fin(gp.dma_start(
            out=cp[dst][:].rearrange("n (h p) -> n h p", h=MH),
            in_=g["sloc_d"][c * INTER:(c + 1) * INTER, :].rearrange(
                "(h n) p -> n h p", n=128))))

    def store_sin(c):
        es.wait("sync", s_ve)
        fin = inc(s_st, 16)
        es.sync(lambda sy, c=c, fin=fin: fin(sy.dma_start(
            out=g["sin_d"][c * INTER:(c + 1) * INTER, :].rearrange(
                "(h n) p -> n h p", n=128),
            in_=cp["m_sstage"][:, :].rearrange("n (h p) -> n h p", h=MH))))

    bufs = ["m_slocc", "m_slocp"]
    load_sloc(7, bufs[7 % 2])
    for c in range(7, -1, -1):
        loc = bufs[c % 2]
        es.wait("vector", s_wld)
        es.wait("vector", s_ve)
        if c < 7:
            fin = inc(s_ve)
            es.vector(lambda v, loc=loc, fin=fin:
                      fin(v.tensor_tensor(cp["m_sstage"][:, :], cp[loc][:, :],
                                          cp["m_srun"][:, :], ALU.add)))
            store_sin(c + 1)
        es.wait("vector", s_ve)
        fin = inc(s_ve)
        es.vector(lambda v, loc=loc, fin=fin:
                  fin(v.tensor_tensor(cp["m_srun"][:, :], cp[loc][:, :],
                                      cp["m_srun"][:, :], ALU.add)))
        es.wait("vector", s_ve)
        fin = inc(s_ve)
        def scale_h(v, c=c, fin=fin):
            ins = None
            for h in range(MH):
                ins = v.tensor_scalar(
                    cp["m_srun"][:, h * MHD:(h + 1) * MHD],
                    cp["m_srun"][:, h * MHD:(h + 1) * MHD],
                    cp["m_decbc"][:, h * NCHK + c:h * NCHK + c + 1],
                    None, ALU.mult)
            fin(ins)
        es.vector(scale_h)
        if c > 0:
            load_sloc(c - 1, bufs[(c - 1) % 2])
    es.wait("vector", s_ve)
    fin = inc(s_ve)
    es.vector(lambda v, fin=fin:
              fin(v.tensor_tensor(cp["m_sstage"][:, :], cp["m_prev"][:, :],
                                  cp["m_srun"][:, :], ALU.add)))
    store_sin(0)
    if "tap_sin" in DEBUG_TAPS:
        es.wait("sync", s_st)
        fin = inc(s_st, 16)
        es.sync(lambda sy, fin=fin:
                fin(sy.dma_start(out=g["tap_sin"], in_=g["sin_d"])))


def _ssd_main(nc, cp):
    g, es = cp["g"], cp["es"]
    pk, pb = cp["pk"], cp["pb"]
    s_wld, s_xld, s_pe, s_ve, s_se, s_ge, s_st = (
        cp[k] for k in ["s_wld", "s_xld", "s_pe", "s_ve", "s_se", "s_ge", "s_st"])
    hbc = cp["m_hbc"]
    dt_sb = cp["m_dt"]
    acum = cp["m_acum"]

    NBLK = TQ // 1024
    for blk_i in range(NBLK):
        for ci in range(4):
            c = blk_i * 4 + ci
            es.wait("gpsimd", s_pe)
            es.wait("gpsimd", s_ve)
            es.wait("gpsimd", s_se)
            fin = inc(s_wld, 16)
            es.gpsimd(lambda gp, c=c, fin=fin: fin(gp.dma_start(
                out=hbc[:].rearrange("p (cc t) -> p cc t", cc=HBC_MT),
                in_=g["hbc_d"][:, c * MCH:(c + 1) * MCH].rearrange(
                    "(cc p) t -> p cc t", p=128))))
            fin = inc(s_wld, 16)
            es.gpsimd(lambda gp, c=c, fin=fin: fin(gp.dma_start(
                out=cp["m_acrow"][:].rearrange("one (h t) -> one h t", h=MH // 2),
                in_=g["acum_d"].rearrange("(one h) t -> one h t", one=1)[
                    0:1, 0:MH // 2, c * MCH:(c + 1) * MCH])))
            es.wait("gpsimd", s_ve)
            fin = inc(s_wld, 16)
            es.gpsimd(lambda gp, c=c, fin=fin: fin(gp.dma_start(
                out=cp["m_sinc"][:].rearrange("n (h p) -> n h p", h=MH),
                in_=g["sin_d"][c * INTER:(c + 1) * INTER, :].rearrange(
                    "(h n) p -> n h p", n=128))))
            fin = inc(s_wld, 16)
            es.gpsimd(lambda gp, c=c, fin=fin: fin(gp.dma_start(
                out=cp["m_gatec"][:, :].rearrange("p (hh i) -> p hh i", hh=2),
                in_=g["gate_d"][c * MCH:(c + 1) * MCH, :].rearrange(
                    "(hh p) i -> p hh i", p=128))))

            es.wait("tensor", s_wld)
            es.wait("tensor", s_ve)
            fin = inc(s_pe)
            es.tensor(lambda t, c=c, fin=fin: (
                t.transpose(pk[4][0:128, 0:MH], dt_sb[:, c * MCH: c * MCH + 128],
                            cp["ident_f"][0:MH, 0:MH]),
                t.transpose(pk[5][0:128, 0:MH],
                            dt_sb[:, c * MCH + 128: c * MCH + 256],
                            cp["ident_f"][0:MH, 0:MH]),
                t.transpose(pk[4][0:128, MH:2 * MH],
                            acum[:, c * MCH: c * MCH + 128],
                            cp["ident_f"][0:MH, 0:MH]),
                fin(t.transpose(pk[5][0:128, MH:2 * MH],
                                acum[:, c * MCH + 128: c * MCH + 256],
                                cp["ident_f"][0:MH, 0:MH]))))
            es.wait("vector", s_pe)
            fin = inc(s_ve)
            es.vector(lambda v, fin=fin: (
                v.tensor_copy(cp["m_dtT"][:, 0:MH], pk[4][:, 0:MH]),
                v.tensor_copy(cp["m_dtT"][:, MH:2 * MH], pk[5][:, 0:MH]),
                v.tensor_copy(cp["m_acT"][:, 0:MH], pk[4][:, MH:2 * MH]),
                fin(v.tensor_copy(cp["m_acT"][:, MH:2 * MH], pk[5][:, MH:2 * MH]))))
            fin = inc(s_pe)
            es.tensor(lambda t, fin=fin: (
                t.transpose(pb[:, 0:128], hbc[:, 32 * MCH:32 * MCH + 128],
                            cp["ident_bf"][:, :]),
                fin(t.transpose(pb[:, 128:256], hbc[:, 32 * MCH + 128:
                                                    32 * MCH + 256],
                                cp["ident_bf"][:, :]))))
            es.wait("vector", s_pe)
            fin = inc(s_ve)
            es.vector(lambda v, fin=fin: (
                v.tensor_copy(cp["m_bT"][:, 0:128], pb[:, 0:128]),
                fin(v.tensor_copy(cp["m_bT"][:, 128:256], pb[:, 128:256]))))
            fin = inc(s_pe)
            es.tensor(lambda t, fin=fin: (
                t.matmul(pk[2][:, 0:MCH], hbc[:, 32 * MCH:32 * MCH + 128],
                         hbc[:, 33 * MCH:34 * MCH], start=True, stop=True),
                fin(t.matmul(pk[3][:, 0:MCH], hbc[:, 32 * MCH + 128:32 * MCH + 256],
                             hbc[:, 33 * MCH:34 * MCH], start=True, stop=True))))
            es.wait("vector", s_pe)
            fin = inc(s_ve)
            es.vector(lambda v, fin=fin: (
                v.tensor_tensor(cp["m_gm"][:, 0:MCH], pk[2][:, 0:MCH],
                                cp["ssd_mask"][:, 0:MCH], ALU.mult),
                fin(v.tensor_tensor(cp["m_gm"][:, MCH:2 * MCH], pk[3][:, 0:MCH],
                                    cp["ssd_mask"][:, MCH:2 * MCH], ALU.mult))))

            for h in range(MH):
                if h == MH // 2:
                    es.wait("gpsimd", s_pe)
                    fin = inc(s_wld, 16)
                    es.gpsimd(lambda gp, c=c, fin=fin: fin(gp.dma_start(
                        out=cp["m_acrow"][:].rearrange(
                            "one (h t) -> one h t", h=MH // 2),
                        in_=g["acum_d"].rearrange(
                            "(one h) t -> one h t", one=1)[
                            0:1, MH // 2:MH, c * MCH:(c + 1) * MCH])))
                    es.wait("tensor", s_wld)
                hh = h % (MH // 2)
                es.wait("tensor", s_ve)
                es.wait("tensor", s_se)
                fin = inc(s_pe)
                es.tensor(lambda t, h=h, hh=hh, fin=fin: (
                    t.transpose(pb[:, 256:384], hbc[:, h * MCH:h * MCH + 128],
                                cp["ident_bf"][:, :]),
                    t.transpose(pb[:, 384:512], hbc[:, h * MCH + 128:
                                                    h * MCH + 256],
                                cp["ident_bf"][:, :]),
                    fin(t.matmul(pk[4][:, 0:MCH], cp["ones_row_f"][:, :],
                                 cp["m_acrow"][0:1, hh * MCH:(hh + 1) * MCH],
                                 start=True, stop=True))))
                es.wait("vector", s_pe)
                fin = inc(s_ve)
                es.vector(lambda v, h=h, fin=fin: (
                    v.tensor_copy(cp["m_xrawT"][:, 0:MHD], pb[:, 256:384]),
                    v.tensor_copy(cp["m_xrawT"][:, MHD:2 * MHD], pb[:, 384:512]),
                    v.tensor_scalar(cp["m_xdtT"][:, 0:MHD], pb[:, 256:384],
                                    cp["m_dtT"][:, h:h + 1], None, ALU.mult),
                    v.tensor_scalar(cp["m_xdtT"][:, MHD:2 * MHD], pb[:, 384:512],
                                    cp["m_dtT"][:, MH + h:MH + h + 1],
                                    None, ALU.mult),
                    v.tensor_scalar(cp["m_eh"][:, 0:MCH], pk[4][:, 0:MCH],
                                    cp["m_acT"][:, h:h + 1], 0.0,
                                    ALU.subtract, ALU.min),
                    fin(v.tensor_scalar(cp["m_eh"][:, MCH:2 * MCH],
                                        pk[4][:, 0:MCH],
                                        cp["m_acT"][:, MH + h:MH + h + 1], 0.0,
                                        ALU.subtract, ALU.min))))
                es.wait("scalar", s_ve)
                fin = inc(s_se)
                es.scalar(lambda sc, h=h, fin=fin: (
                    sc.activation(cp["m_eh"][:, 0:2 * MCH], cp["m_eh"][:, 0:2 * MCH],
                                  AF.Exp),
                    fin(sc.activation(cp["m_sdbc"][:, 0:MCH], pk[4][:, 0:MCH],
                                      AF.Exp))))
                es.wait("vector", s_se)
                fin = inc(s_ve)
                es.vector(lambda v, h=h, fin=fin: (
                    v.tensor_tensor(cp["m_mh"][:, 0:2 * MCH], cp["m_gm"][:, 0:2 * MCH],
                                    cp["m_eh"][:, 0:2 * MCH], ALU.mult),
                    fin(v.tensor_tensor(cp["m_csc"][:, 0:MCH],
                                        hbc[:, 33 * MCH:34 * MCH],
                                        cp["m_sdbc"][:, 0:MCH], ALU.mult))))
                es.wait("tensor", s_ve)
                fin = inc(s_pe)
                def ymm(t, h=h, fin=fin):
                    ins = None
                    for lh in range(2):
                        bank = pk[lh]
                        t.matmul(bank[:, 0:MHD],
                                 cp["m_mh"][:, lh * 128:lh * 128 + 128],
                                 cp["m_xdtT"][:, 0:MHD], start=True, stop=False)
                        t.matmul(bank[:, 0:MHD],
                                 cp["m_mh"][:, MCH + lh * 128:MCH + lh * 128 + 128],
                                 cp["m_xdtT"][:, MHD:2 * MHD],
                                 start=False, stop=False)
                        ins = t.matmul(bank[:, 0:MHD],
                                       cp["m_csc"][:, lh * 128:lh * 128 + 128],
                                       cp["m_sinc"][:, h * MHD:(h + 1) * MHD],
                                       start=False, stop=True)
                    fin(ins)
                es.tensor(ymm)
                es.wait("vector", s_pe)
                fin = inc(s_ve)
                es.vector(lambda v, h=h, fin=fin: (
                    v.scalar_tensor_tensor(
                        cp["m_ych"][:, 0 * INTER + h * MHD:0 * INTER + (h + 1) * MHD],
                        cp["m_xrawT"][:, 0:MHD], cp["D_bc"][:, h:h + 1],
                        pk[0][:, 0:MHD], ALU.mult, ALU.add),
                    fin(v.scalar_tensor_tensor(
                        cp["m_ych"][:, 1 * INTER + h * MHD:1 * INTER + (h + 1) * MHD],
                        cp["m_xrawT"][:, MHD:2 * MHD], cp["D_bc"][:, h:h + 1],
                        pk[1][:, 0:MHD], ALU.mult, ALU.add))))
                es.wait("vector", s_ve)
                fin = inc(s_ve)
                es.vector(lambda v, h=h, fin=fin: (
                    v.tensor_tensor(
                        cp["m_ych"][:, 0 * INTER + h * MHD:0 * INTER + (h + 1) * MHD],
                        cp["m_ych"][:, 0 * INTER + h * MHD:0 * INTER + (h + 1) * MHD],
                        cp["m_gatec"][:, 0 * INTER + h * MHD:
                                      0 * INTER + (h + 1) * MHD], ALU.mult),
                    fin(v.tensor_tensor(
                        cp["m_ych"][:, 1 * INTER + h * MHD:1 * INTER + (h + 1) * MHD],
                        cp["m_ych"][:, 1 * INTER + h * MHD:1 * INTER + (h + 1) * MHD],
                        cp["m_gatec"][:, 1 * INTER + h * MHD:
                                      1 * INTER + (h + 1) * MHD], ALU.mult))))

            if "tap_yg" in DEBUG_TAPS:
                es.wait("sync", s_ve)
                for lh in range(2):
                    fin = inc(s_st, 16)
                    es.sync(lambda sy, c=c, lh=lh, fin=fin: fin(sy.dma_start(
                        out=g["tap_yg"][c * MCH + lh * 128:
                                        c * MCH + lh * 128 + 128, :],
                        in_=cp["m_ych"][:, lh * INTER:(lh + 1) * INTER])))
                es.wait("vector", s_st)
            for lh in range(2):
                es.wait("scalar", s_ve)
                fin = inc(s_se)
                es.scalar(lambda sc, lh=lh, fin=fin:
                          fin(sc.activation(cp["m_gatec"][:, lh * INTER:
                                                          (lh + 1) * INTER],
                                            cp["m_ych"][:, lh * INTER:
                                                        (lh + 1) * INTER],
                                            AF.Square)))
                es.wait("vector", s_se)
                fin = inc(s_ve)
                es.vector(lambda v, lh=lh, fin=fin:
                          fin(v.reduce_sum(cp["m_rsum"][:, :],
                                           cp["m_gatec"][:, lh * INTER:
                                                         (lh + 1) * INTER],
                                           axis=AX.X)))
                es.wait("scalar", s_ve)
                fin = inc(s_se)
                es.scalar(lambda sc, fin=fin:
                          fin(sc.activation(cp["m_rsum"][:, :], cp["m_rsum"][:, :],
                                            AF.Sqrt, scale=1.0 / INTER, bias=1e-5)))
                es.wait("vector", s_se)
                fin = inc(s_ve)
                es.vector(lambda v, fin=fin:
                          fin(v.reciprocal(cp["m_rstd"][:, :], cp["m_rsum"][:, :])))
                es.wait("vector", s_ve)
                fin = inc(s_ve)
                es.vector(lambda v, lh=lh, fin=fin:
                          fin(v.tensor_scalar(cp["m_ych"][:, lh * INTER:
                                                          (lh + 1) * INTER],
                                              cp["m_ych"][:, lh * INTER:
                                                          (lh + 1) * INTER],
                                              cp["m_rstd"][:, 0:1], None, ALU.mult)))
                es.wait("vector", s_ve)
                fin = inc(s_ve)
                es.vector(lambda v, lh=lh, fin=fin:
                          fin(v.tensor_tensor(cp["m_ych"][:, lh * INTER:
                                                          (lh + 1) * INTER],
                                              cp["m_ych"][:, lh * INTER:
                                                          (lh + 1) * INTER],
                                              cp["normw_bc"][:, :], ALU.mult)))
                tcol = ci * MCH + lh * 128
                for h4 in range(0, INTER // 128, 4):
                    es.wait("tensor", s_ve)
                    es.wait("tensor", s_se)
                    fin = inc(s_pe)
                    def tr4(t, lh=lh, h4=h4, fin=fin):
                        ins = None
                        for j in range(4):
                            ins = t.transpose(
                                pb[:, j * 128:(j + 1) * 128],
                                cp["m_ych"][:, lh * INTER + (h4 + j) * 128:
                                            lh * INTER + (h4 + j + 1) * 128],
                                cp["ident_bf"][:, :])
                        fin(ins)
                    es.tensor(tr4)
                    es.wait("scalar", s_pe)
                    fin = inc(s_se)
                    def ev4(sc, h4=h4, tcol=tcol, fin=fin):
                        ins = None
                        for j in range(4):
                            ins = sc.copy(
                                cp["m_ynf"][:, (h4 + j) * 1024 + tcol:
                                            (h4 + j) * 1024 + tcol + 128],
                                pb[:, j * 128:(j + 1) * 128])
                        fin(ins)
                    es.scalar(ev4)

        pe_after_mt = []
        for mt in range(FC):
            buf = mt % 2
            if mt >= 2:
                es.wait("gpsimd", s_pe, pe_after_mt[mt - 1])
            es.wait("gpsimd", s_ve)
            es.wait("gpsimd", s_se)
            fin = inc(s_wld, 16)
            es.gpsimd(lambda gp, mt=mt, buf=buf, fin=fin: fin(gp.dma_start(
                out=cp["m_ych"][:, buf * 4096:(buf + 1) * 4096].rearrange(
                    "p (kc m) -> p kc m", kc=INTER // 128),
                in_=g["w_outproj_T"][:, mt * 128:(mt + 1) * 128].rearrange(
                    "(kc p) m -> p kc m", p=128))))
            for n2 in range(2):
                tok0 = blk_i * 1024 + n2 * 512
                es.wait("sync", s_ve)
                fin = inc(s_xld, 16)
                es.sync(lambda sy, mt=mt, tok0=tok0, fin=fin: fin(sy.dma_start(
                    out=cp["m_eh"][:, 0:512],
                    in_=g["x_f"][mt * 128:(mt + 1) * 128,
                                 XOFF + tok0:XOFF + tok0 + 512])))
                es.wait("tensor", s_wld)
                es.wait("tensor", s_ve)
                es.wait("tensor", s_se)
                fin = inc(s_pe)
                def opmm(t, mt=mt, n2=n2, buf=buf, blk_i=blk_i, fin=fin):
                    bank = pk[2 + n2]
                    ins = None
                    for kc in range(INTER // 128):
                        ins = t.matmul(
                            bank[:, 0:512],
                            cp["m_ych"][:, buf * 4096 + kc * 128:
                                        buf * 4096 + (kc + 1) * 128],
                            cp["m_ynf"][:, kc * 1024 + n2 * 512:
                                        kc * 1024 + n2 * 512 + 512],
                            start=(kc == 0), stop=(kc == INTER // 128 - 1))
                    fin(ins)
                es.tensor(opmm)
                es.wait("vector", s_pe)
                es.wait("vector", s_xld)
                fin = inc(s_ve)
                es.vector(lambda v, n2=n2, fin=fin:
                          fin(v.tensor_tensor(cp["m_eh"][:, 0:512],
                                              cp["m_eh"][:, 0:512],
                                              pk[2 + n2][:, 0:512], ALU.add)))
                es.wait("sync", s_ve)
                fin = inc(s_st, 16)
                es.sync(lambda sy, mt=mt, tok0=tok0, fin=fin: fin(sy.dma_start(
                    out=g["x_res"][mt * 128:(mt + 1) * 128, tok0:tok0 + 512],
                    in_=cp["m_eh"][:, 0:512])))
                es.wait("vector", s_st)
            pe_after_mt.append(s_pe.n)


def stage_attn_block(nc, ctx, cp, kind, call):
    g, es, nc_ = cp["g"], cp["es"], nc
    pk, pb = cp["pk"], cp["pb"]
    s_wld, s_xld, s_pe, s_ve, s_se, s_ge, s_cc, s_st = (
        cp[k] for k in ["s_wld", "s_xld", "s_pe", "s_ve", "s_se", "s_ge",
                        "s_cc", "s_st"])
    tag = f"{kind}{call}"
    ffn = kind == "ffn"

    with ExitStack() as sctx:
        def SSB(name, shape, dtype=F32):
            t = sctx.enter_context(nc.sbuf_tensor(f"sb_{tag}_{name}",
                                                  list(shape), dtype))
            cp["a_" + name] = t
            return t

        if ffn:
            SSB("m0", [1, TQ], BF16); SSB("m1", [1, TQ], BF16)
            SSB("m2", [1, TQ], BF16)
        cc_chunk = [0, 0, 0, 0]

        if ffn:
            SSB("ln1s", [128, FC]); SSB("ln1b", [128, FC])
            SSB("ln2s", [128, FC]); SSB("ln2b", [128, FC])
            SSB("sbc", [128, FC])
            for nm, srcn in [("ln1s", "ln1_scale_c"), ("ln1b", "ln1_bias_c"),
                             ("ln2s", "ln2_scale_c"), ("ln2b", "ln2_bias_c"),
                             ("sbc", "simple_bias_c")]:
                fin = inc(s_wld, 16)
                es.gpsimd(lambda gp, nm=nm, srcn=srcn, fin=fin:
                          fin(gp.dma_start(out=cp["a_" + nm][:], in_=g[srcn])))

        with ExitStack() as c0:
            lnx = c0.enter_context(
                nc.sbuf_tensor(f"sb_{tag}_lnx", [128, FC * TQ], BF16))
            cp["lnsq"] = c0.enter_context(
                nc.sbuf_tensor(f"sb_{tag}_lnsq", [128, FC * TQ], BF16))
            if ffn:
                for nm, shape, dt_ in [("wr", [128, FC * 3], BF16),
                                       ("wdn", [128, FC * 32], BF16),
                                       ("wup", [32, D], BF16),
                                       ("pst0", [128, 512], BF16),
                                       ("pst1", [128, 512], BF16),
                                       ("z", [32, TQ], BF16),
                                       ("l0", [1, TQ], F32),
                                       ("l1", [1, TQ], F32),
                                       ("l2", [1, TQ], F32)]:
                    cp["a_" + nm] = c0.enter_context(nc.sbuf_tensor(
                        f"sb_{tag}0_{nm}", list(shape), dt_))
            es.wait("gpsimd", s_st)
            fin = inc(s_wld, 16)
            es.gpsimd(lambda gp, fin=fin: fin(gp.dma_start(
                out=lnx[:].rearrange("p (cc t) -> p cc t", cc=FC),
                in_=g["x_res"].rearrange("(cc p) t -> p cc t", p=128))))
            es.flush(f"{tag}_lnload")
            emit_ln(nc, cp, lnx, TQ, lnx, cp["lnsq"])
            if ffn:
                es.wait("sync", s_ve)
                fin = inc(s_st, 16)
                es.sync(lambda sy, fin=fin: fin(sy.dma_start(
                    out=g["xn_loc"].rearrange("(cc p) t -> p cc t", p=128),
                    in_=lnx[:].rearrange("p (cc t) -> p cc t", cc=FC))))
                _router_lowrank(nc, cp, lnx)
                emit_ln(nc, cp, lnx, TQ, lnx, cp["lnsq"],
                        gamma=cp["a_ln1s"], beta=cp["a_ln1b"])
            es.wait("sync", s_ve)
            for tch in range(4):
                fin = inc(s_st, 16)
                es.sync(lambda sy, tch=tch, fin=fin: fin(sy.dma_start(
                    out=g[f"xn_ag_{tch}"].rearrange("(cc p) t -> p cc t", p=128),
                    in_=lnx[:].rearrange("p (cc t) -> p cc t", cc=FC)[
                        :, :, tch * 512:(tch + 1) * 512])))
            es.wait("gpsimd", s_st)
            for tch in range(4):
                fin = inc(s_cc)
                es.gpsimd(lambda gp, tch=tch, fin=fin: fin(gp.collective_compute(
                    "AllGather", ALU.bypass, replica_groups=GROUPS4,
                    ins=[g[f"xn_ag_{tch}"]], outs=[g[f"ag_out_{tch}"]])))
                cc_chunk[tch] = s_cc.n
            es.flush(f"{tag}_ln_ag")

        with ExitStack() as c2:
            def S2(name, shape, dtype=F32):
                t = c2.enter_context(nc.sbuf_tensor(f"sb_{tag}2_{name}",
                                                    list(shape), dtype))
                cp["a_" + name] = t
                return t
            S2("wslab", [128, FC * 512], BF16)
            S2("agb", [128, FC * 512], BF16)
            S2("pstage0", [128, 512], BF16)
            S2("pstage1", [128, 512], BF16)
            if not ffn:
                S2("bqv", [1, 512], BF16)
                S2("bqkv", [128, 12])
                fin = inc(s_wld, 16)
                es.gpsimd(lambda gp, fin=fin:
                          fin(gp.dma_start(out=cp["a_bqv"][:], in_=g["bqv_row"])))
                fin = inc(s_wld, 16)
                es.gpsimd(lambda gp, fin=fin:
                          fin(gp.dma_start(out=cp["a_bqkv"][:], in_=g["bqkv_hg"])))
            _qkv_proj(nc, cp, kind, cc_chunk)
            es.flush(f"{tag}_qkv")

        with ExitStack() as c3:
            def S3(name, shape, dtype=F32):
                t = c3.enter_context(nc.sbuf_tensor(f"sb_{tag}3_{name}",
                                                    list(shape), dtype))
                cp["a_" + name] = t
                return t
            S3("mask01", [128, 8 * 1024], BF16)
            fin = inc(s_wld, 16)
            es.gpsimd(lambda gp, fin=fin:
                      fin(gp.dma_start(out=cp["a_mask01"][:], in_=g["mask01"])))
            for b in range(2):
                S3(f"kh{b}", [128, S], BF16)
                S3(f"vh{b}", [128, S], BF16)
                S3(f"qh{b}", [128, S], BF16)
            S3("pt0", [128, 1024], BF16)
            S3("pt1", [128, 1024], BF16)
            S3("accf", [128, 1024])
            S3("osb", [128, 1024])
            S3("ctxs", [128, 1024], BF16)
            _flash(nc, cp)
            es.flush(f"{tag}_flash")

        with ExitStack() as c4:
            def S4(name, shape, dtype=F32):
                t = c4.enter_context(nc.sbuf_tensor(f"sb_{tag}4_{name}",
                                                    list(shape), dtype))
                cp["a_" + name] = t
                return t
            S4("woslab", [128, 4 * D], BF16)
            S4("ctx4", [128, 4 * 512], BF16)
            S4("dstage0", [128, 512], BF16)
            S4("dstage1", [128, 512], BF16)
            _wo_partial(nc, cp, kind)
            fin = inc(s_cc)
            es.wait("gpsimd", s_st)
            es.gpsimd(lambda gp, fin=fin: fin(gp.collective_compute(
                "ReduceScatter", ALU.add, replica_groups=GROUPS4,
                ins=[g["rs_in2"]], outs=[g["rs_out2"]])))
            es.flush(f"{tag}_wo")

        if not ffn:
            _attnb_epilogue(nc, cp)
            es.flush(f"{tag}_epi")
        else:
            with ExitStack() as c5:
                lnx2 = c5.enter_context(
                    nc.sbuf_tensor(f"sb_{tag}5_lnx", [128, FC * TQ], BF16))
                with ExitStack() as c5b:
                    cp["lnsq"] = c5b.enter_context(
                        nc.sbuf_tensor(f"sb_{tag}5_lnsq", [128, FC * TQ], BF16))
                    _ffn_xtl(nc, cp, lnx2)
                    emit_ln(nc, cp, lnx2, TQ, lnx2, cp["lnsq"],
                            gamma=cp["a_ln2s"], beta=cp["a_ln2b"])
                    es.flush(f"{tag}_xtl")
                def S5(name, shape, dtype=F32):
                    t = c5.enter_context(nc.sbuf_tensor(f"sb_{tag}5_{name}",
                                                        list(shape), dtype))
                    cp["a_" + name] = t
                    return t
                S5("hsb", [128, (FF // 128) * 512], BF16)
                S5("fslab0", [128, FC * 128], BF16)
                S5("fslab1", [128, FC * 128], BF16)
                S5("f2slab", [128, (FF // 128) * 128], BF16)
                S5("cmb0", [128, 512], BF16)
                S5("cmb1", [128, 512], BF16)
                S5("cmb2", [128, 512], BF16)
                S5("cx", [128, 512])
                S5("ct0", [128, 512], BF16)
                S5("ct1", [128, 512], BF16)
                S5("ct2", [128, 512], BF16)
                S5("ct0b", [128, 512])
                S5("ct1b", [128, 512])
                _ffn_mlp_combine(nc, cp, lnx2, call)
                es.flush(f"{tag}_mlp")


def _router_lowrank(nc, cp, xnbuf):
    g, es = cp["g"], cp["es"]
    pk = cp["pk"]
    s_wld, s_pe, s_ve, s_se, s_st = (
        cp[k] for k in ["s_wld", "s_pe", "s_ve", "s_se", "s_st"])
    z = cp["a_z"]

    for dst, srcn, kd in [(cp["a_wr"], "w_router_T", 3),
                          (cp["a_wdn"], "w_down_T", 32)]:
        fin = inc(s_wld, 16)
        es.gpsimd(lambda gp, dst=dst, srcn=srcn, kd=kd, fin=fin:
                  fin(gp.dma_start(
                      out=dst[:].rearrange("p (kc m) -> p kc m", kc=FC),
                      in_=g[srcn].rearrange("(kc p) m -> p kc m", p=128))))
    fin = inc(s_wld, 16)
    es.gpsimd(lambda gp, fin=fin:
              fin(gp.dma_start(out=cp["a_wup"][:], in_=g["w_up_T"])))

    es.wait("tensor", s_wld)
    for li in range(3):
        for nt in range(4):
            if not (li == 0 and nt == 0):
                es.wait("tensor", s_ve)
            fin = inc(s_pe)
            def lmm(t, li=li, nt=nt, fin=fin):
                ins = None
                for kc in range(FC):
                    ins = t.matmul(pk[4][0:1, 0:512],
                                   cp["a_wr"][:, kc * 3 + li:kc * 3 + li + 1],
                                   xnbuf[:, kc * TQ + nt * 512:
                                         kc * TQ + nt * 512 + 512],
                                   start=(kc == 0), stop=(kc == FC - 1))
                fin(ins)
            es.tensor(lmm)
            es.wait("vector", s_pe)
            fin = inc(s_ve)
            es.vector(lambda v, li=li, nt=nt, fin=fin:
                      fin(v.tensor_copy(cp[f"a_l{li}"][0:1, nt * 512:
                                                       nt * 512 + 512],
                                        pk[4][0:1, 0:512])))
    def vstep(fn):
        es.wait("vector", cp["s_ve"])
        fin = inc(cp["s_ve"])
        es.vector(lambda v, fn=fn, fin=fin: fin(fn(v)))
    vstep(lambda v: v.tensor_tensor(cp["a_m0"][0:1, :], cp["a_l0"][0:1, :],
                                    cp["a_l1"][0:1, :], ALU.is_ge))
    vstep(lambda v: v.tensor_tensor(cp["a_m1"][0:1, :], cp["a_l0"][0:1, :],
                                    cp["a_l2"][0:1, :], ALU.is_ge))
    vstep(lambda v: v.tensor_tensor(cp["a_m2"][0:1, :], cp["a_l1"][0:1, :],
                                    cp["a_l2"][0:1, :], ALU.is_ge))
    vstep(lambda v: v.tensor_tensor(cp["a_l0"][0:1, :], cp["a_m0"][0:1, :],
                                    cp["a_m1"][0:1, :], ALU.mult))
    vstep(lambda v: v.tensor_scalar(cp["a_l1"][0:1, :], cp["a_m0"][0:1, :],
                                    -1.0, 1.0, ALU.mult, ALU.add))
    vstep(lambda v: v.tensor_tensor(cp["a_l1"][0:1, :], cp["a_l1"][0:1, :],
                                    cp["a_m2"][0:1, :], ALU.mult))
    vstep(lambda v: v.tensor_copy(cp["a_m0"][0:1, :], cp["a_l0"][0:1, :]))
    vstep(lambda v: v.tensor_copy(cp["a_m1"][0:1, :], cp["a_l1"][0:1, :]))
    vstep(lambda v: v.tensor_tensor(cp["a_m2"][0:1, :], cp["a_m0"][0:1, :],
                                    cp["a_m1"][0:1, :], ALU.add))
    vstep(lambda v: v.tensor_scalar(cp["a_m2"][0:1, :], cp["a_m2"][0:1, :],
                                    -1.0, 1.0, ALU.mult, ALU.add))
    for nt in range(4):
        if nt > 0:
            es.wait("tensor", s_ve)
        fin = inc(s_pe)
        def zmm(t, nt=nt, fin=fin):
            ins = None
            for kc in range(FC):
                ins = t.matmul(pk[5][0:32, 0:512],
                               cp["a_wdn"][:, kc * 32:(kc + 1) * 32],
                               xnbuf[:, kc * TQ + nt * 512:
                                     kc * TQ + nt * 512 + 512],
                               start=(kc == 0), stop=(kc == FC - 1))
            fin(ins)
        es.tensor(zmm)
        es.wait("vector", s_pe)
        fin = inc(s_ve)
        es.vector(lambda v, nt=nt, fin=fin:
                  fin(v.tensor_copy(z[0:32, nt * 512:nt * 512 + 512],
                                    pk[5][0:32, 0:512])))
    es.wait("tensor", s_ve)
    st_h = {}
    for mt in range(FC):
        for nt in range(4):
            it = mt * 4 + nt
            if it >= 2:
                es.wait("tensor", s_se, st_h.get(("se", it - 2), 0))
            fin = inc(s_pe)
            es.tensor(lambda t, mt=mt, nt=nt, fin=fin:
                      fin(t.matmul(pk[nt % 2][:, 0:512],
                                   cp["a_wup"][0:32, mt * 128:(mt + 1) * 128],
                                   z[0:32, nt * 512:nt * 512 + 512],
                                   start=True, stop=True)))
            es.wait("scalar", s_pe)
            if it >= 2:
                es.wait("scalar", s_st, st_h.get(("st", it - 2), 0))
            fin = inc(s_se)
            es.scalar(lambda sc, nt=nt, fin=fin:
                      fin(sc.copy(cp[f"a_pst{nt % 2}"][:, 0:512],
                                  pk[nt % 2][:, 0:512])))
            st_h[("se", it)] = s_se.n
            es.wait("sync", s_se)
            fin = inc(s_st, 16)
            es.sync(lambda sy, mt=mt, nt=nt, fin=fin: fin(sy.dma_start(
                out=g["y1_d"][mt * 128:(mt + 1) * 128,
                              nt * 512:nt * 512 + 512],
                in_=cp[f"a_pst{nt % 2}"][:, 0:512])))
            st_h[("st", it)] = s_st.n


def _qkv_proj(nc, cp, kind, cc_chunk):
    g, es = cp["g"], cp["es"]
    pk = cp["pk"]
    s_wld, s_pe, s_ve, s_se, s_cc, s_st = (
        cp[k] for k in ["s_wld", "s_pe", "s_ve", "s_se", "s_cc", "s_st"])
    ffn = kind == "ffn"

    def ag_tile(gt, kc, w0, w1):
        r, tch = gt // 4, gt % 4
        return g[f"ag_out_{tch}"][r * D + kc * 128:r * D + kc * 128 + 128, w0:w1]

    def slab_load(srcn, col0):
        es.wait("gpsimd", s_pe)
        fin = inc(s_wld, 16)
        es.gpsimd(lambda gp, srcn=srcn, col0=col0, fin=fin: fin(gp.dma_start(
            out=cp["a_wslab"][:].rearrange("p (kc m) -> p kc m", kc=FC),
            in_=g[srcn][:, col0:col0 + 512].rearrange(
                "(kc p) m -> p kc m", p=128))))

    hist = {}
    itc = [0]
    agb = cp["a_agb"]

    def fmaj_part(name, srcn, col0, dst, bias_col):
        slab_load(srcn, col0)
        for tch in range(4):
            for r in range(4):
                gt = r * 4 + tch
                es.wait("gpsimd", s_cc, cc_chunk[tch])
                es.wait("gpsimd", s_pe)
                fin = inc(s_wld, 16)
                es.gpsimd(lambda gp, gt=gt, fin=fin: fin(gp.dma_start(
                    out=agb[:].rearrange("p (kc t) -> p kc t", kc=FC),
                    in_=g[f"ag_out_{gt % 4}"].rearrange(
                        "(r kc p) t -> p r kc t", r=4, p=128)[
                        :, gt // 4, :, :])))
                es.wait("tensor", s_wld)
                for mt in range(4):
                    it = itc[0]
                    if ("se", it - 2) in hist:
                        hv = hist[("se", it - 2)]
                        if isinstance(hv, tuple):
                            es.wait("tensor", s_ve, hv[1])
                        else:
                            es.wait("tensor", s_se, hv)
                    fin = inc(s_pe)
                    def mm(t, mt=mt, it=it, fin=fin):
                        ins = None
                        for kc in range(FC):
                            ins = t.matmul(
                                pk[it % 2][:, 0:512],
                                cp["a_wslab"][:, kc * 512 + mt * 128:
                                              kc * 512 + mt * 128 + 128],
                                agb[:, kc * 512:(kc + 1) * 512],
                                start=(kc == 0), stop=(kc == FC - 1))
                        fin(ins)
                    es.tensor(mm)
                    if bias_col is None:
                        es.wait("scalar", s_pe)
                        if ("st", it - 2) in hist:
                            es.wait("scalar", s_st, hist[("st", it - 2)])
                        fin = inc(s_se)
                        es.scalar(lambda sc, it=it, fin=fin:
                                  fin(sc.copy(cp[f"a_pstage{it % 2}"][:, :],
                                              pk[it % 2][:, 0:512])))
                        hist[("se", it)] = s_se.n
                        es.wait("sync", s_se)
                    else:
                        es.wait("vector", s_pe)
                        if ("st", it - 2) in hist:
                            es.wait("vector", s_st, hist[("st", it - 2)])
                        fin = inc(s_ve)
                        es.vector(lambda v, it=it, mt=mt, fin=fin:
                                  fin(v.tensor_scalar(
                                      cp[f"a_pstage{it % 2}"][:, :],
                                      pk[it % 2][:, 0:512],
                                      g_bias(cp, bias_col, mt), None, ALU.add)))
                        hist[("se", it)] = ("ve", s_ve.n)
                        es.wait("sync", s_ve)
                    fin = inc(s_st, 16)
                    es.sync(lambda sy, gt=gt, mt=mt, it=it, fin=fin:
                            fin(sy.dma_start(
                                out=g[dst][mt * 128:(mt + 1) * 128,
                                           gt * 512:(gt + 1) * 512],
                                in_=cp[f"a_pstage{it % 2}"][:, :])))
                    hist[("st", it)] = s_st.n
                    itc[0] += 1

    def g_bias(cp, qkv_i, mt):
        return cp["a_bqkv"][:, qkv_i * 4 + mt:qkv_i * 4 + mt + 1]

    if ffn:
        fmaj_part("K", "wk_hg", 0, "K_d", None)
        fmaj_part("Q", "wq_hg", 0, "Q_d", None)
        vsrc, vcol = "wv_hg", 0
    else:
        fmaj_part("K", "wqkv_hg", 512, "K_d", 1)
        fmaj_part("Q", "wqkv_hg", 0, "Q_d", 0)
        vsrc, vcol = "wqkv_hg", 1024

    slab_load(vsrc, vcol)
    es.wait("gpsimd", s_cc, cc_chunk[3])
    for vt in range(64):
        gt = vt // 4
        if vt % 4 == 0:
            es.wait("gpsimd", s_pe)
            fin = inc(s_wld, 16)
            es.gpsimd(lambda gp, gt=gt, fin=fin: fin(gp.dma_start(
                out=agb[:].rearrange("p (kc t) -> p kc t", kc=FC),
                in_=g[f"ag_out_{gt % 4}"].rearrange(
                    "(r kc p) t -> p r kc t", r=4, p=128)[:, gt // 4, :, :])))
            es.wait("tensor", s_wld)
        it = itc[0]
        itc[0] += 1
        if ("se", it - 2) in hist:
            hv = hist[("se", it - 2)]
            if isinstance(hv, tuple):
                es.wait("tensor", s_ve, hv[1])
            else:
                es.wait("tensor", s_se, hv)
        fin = inc(s_pe)
        def vmm(t, vt=vt, it=it, fin=fin):
            sub = vt % 4
            withb = kind == "attnb"
            ins = None
            for kc in range(FC):
                ins = t.matmul(
                    pk[it % 2][:, 0:512],
                    agb[:, kc * 512 + sub * 128:kc * 512 + sub * 128 + 128],
                    cp["a_wslab"][:, kc * 512:(kc + 1) * 512],
                    start=(kc == 0), stop=(kc == FC - 1 and not withb))
            if withb:
                ins = t.matmul(pk[it % 2][:, 0:512], cp["ones_row"][:, 0:128],
                               cp["a_bqv"][:, :], start=False, stop=True)
            fin(ins)
        es.tensor(vmm)
        es.wait("scalar", s_pe)
        if ("st", it - 2) in hist:
            es.wait("scalar", s_st, hist[("st", it - 2)])
        fin = inc(s_se)
        es.scalar(lambda sc, it=it, fin=fin:
                  fin(sc.copy(cp[f"a_pstage{it % 2}"][:, :], pk[it % 2][:, 0:512])))
        hist[("se", it)] = s_se.n
        es.wait("sync", s_se)
        fin = inc(s_st, 16)
        es.sync(lambda sy, vt=vt, it=it, fin=fin: fin(sy.dma_start(
            out=g["V_d"][vt * 128:(vt + 1) * 128, :],
            in_=cp[f"a_pstage{it % 2}"][:, :])))
        hist[("st", it)] = s_st.n


def _flash(nc, cp):
    g, es = cp["g"], cp["es"]
    pk = cp["pk"]
    s_wld, s_xld, s_pe, s_ve, s_se, s_ge, s_st = (
        cp[k] for k in ["s_wld", "s_xld", "s_pe", "s_ve", "s_se", "s_ge", "s_st"])

    def load_head(h):
        b = h % 2
        es.wait("sync", s_pe)
        for dst, src in [(cp[f"a_kh{b}"], g["K_d"][h * 128:(h + 1) * 128, :]),
                         (cp[f"a_qh{b}"], g["Q_d"][h * 128:(h + 1) * 128, :])]:
            fin = inc(s_xld, 16)
            es.sync(lambda sy, dst=dst, src=src, fin=fin:
                    fin(sy.dma_start(out=dst[:], in_=src)))
        fin = inc(s_xld, 16)
        es.sync(lambda sy, h=h, b=b, fin=fin: fin(sy.dma_start(
            out=cp[f"a_vh{b}"][:].rearrange("p (kt d) -> p kt d", kt=64),
            in_=g["V_d"].rearrange("(kt p) d -> p kt d", p=128)[
                :, :, h * 128:(h + 1) * 128])))

    ld_done = {}
    load_head(0)
    ld_done[0] = s_xld.n
    for h in range(HG):
        b = h % 2
        if h + 1 < HG:
            load_head(h + 1)
            ld_done[h + 1] = s_xld.n
        es.wait("tensor", s_xld, ld_done[h])
        kh, vh, qh = cp[f"a_kh{b}"], cp[f"a_vh{b}"], cp[f"a_qh{b}"]
        for qm in range(8):
            nkv = (qm + 1) * 8
            exp_h, msk_h, sum_h = {}, {}, {}
            es.wait("tensor", s_se)
            es.wait("tensor", s_ve)
            for kvt in range(nkv + 1):
                if kvt < nkv:
                    sb0, sb1 = (2, 3) if kvt % 2 == 0 else (4, 5)
                    if kvt - 2 in exp_h:
                        es.wait("tensor", s_se, exp_h[kvt - 2])
                    fin = inc(s_pe)
                    es.tensor(lambda t, kvt=kvt, qm=qm, sb0=sb0, sb1=sb1,
                              kh=kh, qh=qh, fin=fin: (
                        t.matmul(pk[sb0][:, :], kh[:, kvt * 128:(kvt + 1) * 128],
                                 qh[:, qm * 1024:qm * 1024 + 512],
                                 start=True, stop=True),
                        fin(t.matmul(pk[sb1][:, :],
                                     kh[:, kvt * 128:(kvt + 1) * 128],
                                     qh[:, qm * 1024 + 512:qm * 1024 + 1024],
                                     start=True, stop=True))))
                av = kvt - 1
                if av >= 0:
                    es.wait("tensor", s_se, exp_h[av])
                    if av in msk_h:
                        es.wait("tensor", s_ge, msk_h[av])
                    fin = inc(s_pe)
                    es.tensor(lambda t, av=av, vh=vh, nkv=nkv, fin=fin: (
                        t.matmul(pk[0][:, :], vh[:, av * 128:(av + 1) * 128],
                                 cp[f"a_pt{av % 2}"][:, 0:512],
                                 start=(av == 0), stop=(av == nkv - 1),
                                 skip_group_check=True),
                        fin(t.matmul(pk[1][:, :], vh[:, av * 128:(av + 1) * 128],
                                     cp[f"a_pt{av % 2}"][:, 512:1024],
                                     start=(av == 0), stop=(av == nkv - 1),
                                     skip_group_check=True))))
                if kvt < nkv:
                    es.wait("scalar", s_pe)
                    if kvt - 2 in sum_h:
                        es.wait("scalar", s_ve, sum_h[kvt - 2])
                    fin = inc(s_se)
                    es.scalar(lambda sc, kvt=kvt, sb0=sb0, sb1=sb1, fin=fin: (
                        sc.activation(cp[f"a_pt{kvt % 2}"][:, 0:512],
                                      pk[sb0][:, :], AF.Exp,
                                      scale=RV, bias=-EXP_SHIFT),
                        fin(sc.activation(cp[f"a_pt{kvt % 2}"][:, 512:1024],
                                          pk[sb1][:, :], AF.Exp,
                                          scale=RV, bias=-EXP_SHIFT))))
                    exp_h[kvt] = s_se.n
                    if kvt >= qm * 8:
                        rel = kvt - qm * 8
                        es.wait("gpsimd", s_se)
                        fin = inc(s_ge)
                        es.gpsimd(lambda gp, kvt=kvt, rel=rel, fin=fin:
                                  fin(gp.tensor_tensor(
                                      cp[f"a_pt{kvt % 2}"][:, :],
                                      cp[f"a_pt{kvt % 2}"][:, :],
                                      cp["a_mask01"][:, rel * 1024:
                                                     (rel + 1) * 1024],
                                      ALU.mult)))
                        msk_h[kvt] = s_ge.n
                    es.wait("vector", s_se)
                    if kvt in msk_h:
                        es.wait("vector", s_ge)
                    es.wait("vector", s_ve)
                    if kvt == 0:
                        es.wait("vector", s_pe)
                    fin = inc(s_ve)
                    if kvt == 0:
                        es.vector(lambda v, kvt=kvt, fin=fin:
                                  fin(v.tensor_copy(cp["a_accf"][:, :],
                                                    cp[f"a_pt{kvt % 2}"][:, :])))
                    else:
                        es.vector(lambda v, kvt=kvt, fin=fin:
                                  fin(v.tensor_tensor(cp["a_accf"][:, :],
                                                      cp["a_accf"][:, :],
                                                      cp[f"a_pt{kvt % 2}"][:, :],
                                                      ALU.add)))
                    sum_h[kvt] = s_ve.n
            es.wait("scalar", s_pe)
            fin = inc(s_se)
            es.scalar(lambda sc, fin=fin: (
                sc.copy(cp["a_osb"][:, 0:512], pk[0][:, :]),
                fin(sc.copy(cp["a_osb"][:, 512:1024], pk[1][:, :]))))
            es.wait("tensor", s_ve)
            es.wait("tensor", s_se)
            fin = inc(s_pe)
            es.tensor(lambda t, fin=fin: (
                t.matmul(pk[2][0:1, :], cp["ones_col_f"][:, :],
                         cp["a_accf"][:, 0:512], start=True, stop=True),
                fin(t.matmul(pk[3][0:1, :], cp["ones_col_f"][:, :],
                             cp["a_accf"][:, 512:1024], start=True, stop=True))))
            es.wait("vector", s_pe)
            fin = inc(s_ve)
            es.vector(lambda v, fin=fin: (
                v.reciprocal(cp["r_mean"][0:1, 0:512], pk[2][0:1, :]),
                fin(v.reciprocal(cp["r_ex2"][0:1, 0:512], pk[3][0:1, :]))))
            es.wait("tensor", s_ve)
            fin = inc(s_pe)
            es.tensor(lambda t, fin=fin: (
                t.matmul(pk[4][:, :], cp["ones_row_f"][:, :],
                         cp["r_mean"][0:1, 0:512], start=True, stop=True),
                fin(t.matmul(pk[5][:, :], cp["ones_row_f"][:, :],
                             cp["r_ex2"][0:1, 0:512], start=True, stop=True))))
            es.wait("vector", s_pe)
            es.wait("vector", s_st)
            fin = inc(s_ve)
            es.vector(lambda v, fin=fin: (
                v.tensor_tensor(cp["a_ctxs"][:, 0:512], cp["a_osb"][:, 0:512],
                                pk[4][:, :], ALU.mult),
                fin(v.tensor_tensor(cp["a_ctxs"][:, 512:1024],
                                    cp["a_osb"][:, 512:1024],
                                    pk[5][:, :], ALU.mult))))
            es.wait("sync", s_ve)
            fin = inc(s_st, 16)
            es.sync(lambda sy, h=h, qm=qm, fin=fin: fin(sy.dma_start(
                out=g["ctx_d"][h * 128:(h + 1) * 128,
                               qm * 1024:(qm + 1) * 1024],
                in_=cp["a_ctxs"][:, :])))


def _wo_partial(nc, cp, kind):
    g, es = cp["g"], cp["es"]
    pk = cp["pk"]
    s_wld, s_xld, s_pe, s_se, s_st = (
        cp[k] for k in ["s_wld", "s_xld", "s_pe", "s_se", "s_st"])
    wsrc = "wo_f_hg" if kind == "ffn" else "wo_b_hg"

    fin = inc(s_wld, 16)
    es.gpsimd(lambda gp, fin=fin: fin(gp.dma_start(
        out=cp["a_woslab"][:].rearrange("p (kc m) -> p kc m", kc=4),
        in_=g[wsrc].rearrange("(kc p) m -> p kc m", p=128))))
    es.wait("sync", s_st)
    hist = {}
    for gt in range(16):
        es.wait("sync", s_pe)
        fin = inc(s_xld, 16)
        es.sync(lambda sy, gt=gt, fin=fin: fin(sy.dma_start(
            out=cp["a_ctx4"][:].rearrange("p (kc t) -> p kc t", kc=4),
            in_=g["ctx_d"][:, gt * 512:(gt + 1) * 512].rearrange(
                "(kc p) t -> p kc t", p=128))))
        es.wait("tensor", s_xld)
        es.wait("tensor", s_wld)
        for ot in range(16):
            it = gt * 16 + ot
            if ("se", it - 2) in hist:
                es.wait("tensor", s_se, hist[("se", it - 2)])
            fin = inc(s_pe)
            def mm(t, ot=ot, it=it, fin=fin):
                ins = None
                for kc in range(4):
                    ins = t.matmul(pk[it % 2][:, :],
                                   cp["a_woslab"][:, kc * D + ot * 128:
                                                  kc * D + ot * 128 + 128],
                                   cp["a_ctx4"][:, kc * 512:(kc + 1) * 512],
                                   start=(kc == 0), stop=(kc == 3))
                fin(ins)
            es.tensor(mm)
            es.wait("scalar", s_pe)
            if ("st", it - 2) in hist:
                es.wait("scalar", s_st, hist[("st", it - 2)])
            fin = inc(s_se)
            es.scalar(lambda sc, it=it, fin=fin:
                      fin(sc.copy(cp[f"a_dstage{it % 2}"][:, :],
                                  pk[it % 2][:, :])))
            hist[("se", it)] = s_se.n
            es.wait("sync", s_se)
            r = gt // 4
            fin = inc(s_st, 16)
            es.sync(lambda sy, gt=gt, ot=ot, it=it, r=r, fin=fin:
                    fin(sy.dma_start(
                        out=g["rs_in2"][r * D + ot * 128:r * D + ot * 128 + 128,
                                        (gt % 4) * 512:(gt % 4) * 512 + 512],
                        in_=cp[f"a_dstage{it % 2}"][:, :])))
            hist[("st", it)] = s_st.n


def _attnb_epilogue(nc, cp):
    g, es, nc_ = cp["g"], cp["es"], nc
    s_xld, s_ve, s_st, s_cc = (cp[k] for k in ["s_xld", "s_ve", "s_st", "s_cc"])
    with ExitStack() as ec:
        cx = ec.enter_context(nc.sbuf_tensor("sb_ab_cx", [128, 512], F32))
        dd = ec.enter_context(nc.sbuf_tensor("sb_ab_dd", [128, 512], BF16))
        bo = ec.enter_context(nc.sbuf_tensor("sb_ab_bo", [128, FC], F32))
        fin = inc(cp["s_wld"], 16)
        es.gpsimd(lambda gp, fin=fin:
                  fin(gp.dma_start(out=bo[:], in_=g["bo_c"])))
        es.wait("sync", cp["s_cc"])
        es.wait("sync", cp["s_wld"])
        for mt in range(FC):
            for nt in range(4):
                es.wait("sync", s_st)
                es.wait("sync", s_ve)
                fin = inc(s_xld, 16)
                es.sync(lambda sy, mt=mt, nt=nt, fin=fin: fin(sy.dma_start(
                    out=cx[:, :], in_=g["x_res"][mt * 128:(mt + 1) * 128,
                                                 nt * 512:nt * 512 + 512])))
                fin = inc(s_xld, 16)
                es.sync(lambda sy, mt=mt, nt=nt, fin=fin: fin(sy.dma_start(
                    out=dd[:, :], in_=g["rs_out2"][mt * 128:(mt + 1) * 128,
                                                   nt * 512:nt * 512 + 512])))
                es.wait("vector", s_xld)
                fin = inc(s_ve)
                es.vector(lambda v, fin=fin:
                          fin(v.tensor_tensor(cx[:, :], cx[:, :], dd[:, :],
                                              ALU.add)))
                es.wait("vector", s_ve)
                fin = inc(s_ve)
                es.vector(lambda v, mt=mt, fin=fin:
                          fin(v.tensor_scalar(cx[:, :], cx[:, :],
                                              bo[:, mt:mt + 1], None, ALU.add)))
                es.wait("sync", s_ve)
                fin = inc(s_st, 16)
                es.sync(lambda sy, mt=mt, nt=nt, fin=fin: fin(sy.dma_start(
                    out=g["x_res"][mt * 128:(mt + 1) * 128,
                                   nt * 512:nt * 512 + 512],
                    in_=cx[:, :])))


_FXN = [0]


def _ffn_xtl(nc, cp, lnx2):
    g, es = cp["g"], cp["es"]
    s_xld, s_ve, s_st = (cp[k] for k in ["s_xld", "s_ve", "s_st"])
    _FXN[0] += 1
    with ExitStack() as ec:
        ta = ec.enter_context(nc.sbuf_tensor(f"sb_fx_ta{_FXN[0]}", [128, 512], BF16))
        tb = ec.enter_context(nc.sbuf_tensor(f"sb_fx_tb{_FXN[0]}", [128, 512], BF16))
        es.wait("sync", cp["s_cc"])
        for cc in range(FC):
            for nt in range(4):
                es.wait("sync", s_ve)
                fin = inc(s_xld, 16)
                es.sync(lambda sy, cc=cc, nt=nt, fin=fin: fin(sy.dma_start(
                    out=ta[:, :], in_=g["xn_loc"][cc * 128:(cc + 1) * 128,
                                                  nt * 512:nt * 512 + 512])))
                fin = inc(s_xld, 16)
                es.sync(lambda sy, cc=cc, nt=nt, fin=fin: fin(sy.dma_start(
                    out=tb[:, :], in_=g["rs_out2"][cc * 128:(cc + 1) * 128,
                                                   nt * 512:nt * 512 + 512])))
                es.wait("vector", s_xld)
                fin = inc(s_ve)
                es.vector(lambda v, cc=cc, nt=nt, fin=fin:
                          fin(v.tensor_tensor(
                              lnx2[:, cc * TQ + nt * 512:cc * TQ + nt * 512 + 512],
                              ta[:, :], tb[:, :], ALU.add)))
        es.wait("sync", s_ve)
        fin = inc(s_st, 16)
        es.sync(lambda sy, fin=fin: fin(sy.dma_start(
            out=g["xtl_d"].rearrange("(cc p) t -> p cc t", p=128),
            in_=lnx2[:].rearrange("p (cc t) -> p cc t", cc=FC))))


def _ffn_mlp_combine(nc, cp, xn2, call):
    g, es = cp["g"], cp["es"]
    pk = cp["pk"]
    s_wld, s_xld, s_pe, s_ve, s_se, s_ge, s_st = (
        cp[k] for k in ["s_wld", "s_xld", "s_pe", "s_ve", "s_se", "s_ge", "s_st"])
    NMT = FF // 128

    def vstep(fn, waits=()):
        for w in waits:
            es.wait("vector", cp[w])
        es.wait("vector", s_ve)
        fin = inc(s_ve)
        es.vector(lambda v, fn=fn, fin=fin: fin(fn(v)))

    for nt in range(4):
        es.wait("tensor", s_ve)
        es.wait("tensor", s_se)
        fin = inc(s_pe)
        es.tensor(lambda t, nt=nt, fin=fin: (
            t.matmul(pk[4][:, :], cp["ones_row"][:, :],
                     cp["a_m0"][0:1, nt * 512:nt * 512 + 512],
                     start=True, stop=True),
            fin(t.matmul(pk[5][:, :], cp["ones_row"][:, :],
                         cp["a_m1"][0:1, nt * 512:nt * 512 + 512],
                         start=True, stop=True))))
        es.wait("vector", s_pe)
        fin = inc(s_ve)
        es.vector(lambda v, fin=fin: (
            v.tensor_copy(cp["a_cmb0"][:, :], pk[4][:, :]),
            fin(v.tensor_copy(cp["a_cmb1"][:, :], pk[5][:, :]))))
        vstep(lambda v: v.tensor_tensor(cp["a_cmb2"][:, :], cp["a_cmb0"][:, :],
                                        cp["a_cmb1"][:, :], ALU.add))
        vstep(lambda v: v.tensor_scalar(cp["a_cmb2"][:, :], cp["a_cmb2"][:, :],
                                        -1.0, 1.0, ALU.mult, ALU.add))
        se_h = {}
        pe_h = {}
        for mt in range(NMT):
            buf = mt % 2
            if mt >= 2:
                es.wait("gpsimd", s_pe, pe_h[mt - 2])
            if call == 0 and nt == 0:
                fin = inc(s_wld, 16)
                es.gpsimd(lambda gp, mt=mt, buf=buf, fin=fin: fin(gp.dma_start(
                    out=cp[f"a_fslab{buf}"][:].rearrange(
                        "p (kc m) -> p kc m", kc=FC),
                    in_=g["w_ff1_T"][:, mt * 128:(mt + 1) * 128].rearrange(
                        "(kc p) m -> p kc m", p=128))))
                es.wait("sync", s_wld)
                fin = inc(s_st, 16)
                es.sync(lambda sy, mt=mt, buf=buf, fin=fin: fin(sy.dma_start(
                    out=g["wff1_bf"][:, mt * 128:(mt + 1) * 128].rearrange(
                        "(kc p) m -> p kc m", p=128),
                    in_=cp[f"a_fslab{buf}"][:].rearrange(
                        "p (kc m) -> p kc m", kc=FC))))
            else:
                fin = inc(s_wld, 16)
                es.gpsimd(lambda gp, mt=mt, buf=buf, fin=fin: fin(gp.dma_start(
                    out=cp[f"a_fslab{buf}"][:].rearrange(
                        "p (kc m) -> p kc m", kc=FC),
                    in_=g["wff1_bf"][:, mt * 128:(mt + 1) * 128].rearrange(
                        "(kc p) m -> p kc m", p=128))))
            es.wait("tensor", s_wld)
            if mt >= 2:
                es.wait("tensor", s_se, se_h[mt - 2])
            fin = inc(s_pe)
            def f1mm(t, mt=mt, nt=nt, buf=buf, fin=fin):
                ins = None
                for kc in range(FC):
                    ins = t.matmul(pk[mt % 2][:, :],
                                   cp[f"a_fslab{buf}"][:, kc * 128:(kc + 1) * 128],
                                   xn2[:, kc * TQ + nt * 512:
                                       kc * TQ + nt * 512 + 512],
                                   start=(kc == 0), stop=(kc == FC - 1))
                fin(ins)
            es.tensor(f1mm)
            pe_h[mt] = s_pe.n
            es.wait("scalar", s_pe)
            fin = inc(s_se)
            es.scalar(lambda sc, mt=mt, fin=fin:
                      fin(sc.activation(cp["a_hsb"][:, mt * 512:(mt + 1) * 512],
                                        pk[mt % 2][:, :], AF.Gelu_apprx_tanh)))
            se_h[mt] = s_se.n
        ve_cmb = {}
        for ot in range(FC):
            es.wait("gpsimd", s_pe)
            if call == 0 and nt == 0:
                fin = inc(s_wld, 16)
                es.gpsimd(lambda gp, ot=ot, fin=fin: fin(gp.dma_start(
                    out=cp["a_f2slab"][:].rearrange("p (kc m) -> p kc m",
                                                    kc=FF // 128),
                    in_=g["w_ff2_T"][:, ot * 128:(ot + 1) * 128].rearrange(
                        "(kc p) m -> p kc m", p=128))))
                es.wait("sync", s_wld)
                fin = inc(s_st, 16)
                es.sync(lambda sy, ot=ot, fin=fin: fin(sy.dma_start(
                    out=g["wff2_bf"][:, ot * 128:(ot + 1) * 128].rearrange(
                        "(kc p) m -> p kc m", p=128),
                    in_=cp["a_f2slab"][:].rearrange("p (kc m) -> p kc m",
                                                    kc=FF // 128))))
            else:
                fin = inc(s_wld, 16)
                es.gpsimd(lambda gp, ot=ot, fin=fin: fin(gp.dma_start(
                    out=cp["a_f2slab"][:].rearrange("p (kc m) -> p kc m",
                                                    kc=FF // 128),
                    in_=g["wff2_bf"][:, ot * 128:(ot + 1) * 128].rearrange(
                        "(kc p) m -> p kc m", p=128))))
            es.wait("tensor", s_wld)
            es.wait("tensor", s_se)
            if ot - 2 in ve_cmb:
                es.wait("tensor", s_ve, ve_cmb[ot - 2])
            fin = inc(s_pe)
            def f2mm(t, ot=ot, fin=fin):
                ins = None
                for kc in range(FF // 128):
                    ins = t.matmul(pk[2 + ot % 2][:, :],
                                   cp["a_f2slab"][:, kc * 128:(kc + 1) * 128],
                                   cp["a_hsb"][:, kc * 512:(kc + 1) * 512],
                                   start=(kc == 0), stop=(kc == FF // 128 - 1))
                fin(ins)
            es.tensor(f2mm)
            es.wait("sync", s_ve)
            for dst, srcn in [("a_ct0", "xn_loc"), ("a_ct1", "y1_d"),
                              ("a_ct2", "xtl_d")]:
                fin = inc(s_xld, 16)
                es.sync(lambda sy, dst=dst, srcn=srcn, ot=ot, nt=nt, fin=fin:
                        fin(sy.dma_start(
                            out=cp[dst][:, :],
                            in_=g[srcn][ot * 128:(ot + 1) * 128,
                                        nt * 512:nt * 512 + 512])))
            fin = inc(s_xld, 16)
            es.sync(lambda sy, ot=ot, nt=nt, fin=fin: fin(sy.dma_start(
                out=cp["a_cx"][:, :],
                in_=g["x_res"][ot * 128:(ot + 1) * 128,
                               nt * 512:nt * 512 + 512])))
            es.wait("vector", s_pe)
            es.wait("vector", s_xld)
            vstep(lambda v, ot=ot: v.tensor_tensor(
                cp["a_ct0b"][:, :], pk[2 + ot % 2][:, :], cp["a_ct2"][:, :],
                ALU.add))
            vstep(lambda v: v.tensor_tensor(cp["a_ct0b"][:, :], cp["a_ct0b"][:, :],
                                            cp["a_cmb2"][:, :], ALU.mult))
            vstep(lambda v, ot=ot: v.tensor_scalar(
                cp["a_ct1b"][:, :], cp["a_ct0"][:, :],
                cp["a_sbc"][:, ot:ot + 1], None, ALU.add))
            vstep(lambda v: v.tensor_tensor(cp["a_ct1b"][:, :], cp["a_ct1b"][:, :],
                                            cp["a_cmb0"][:, :], ALU.mult))
            vstep(lambda v: v.tensor_tensor(cp["a_ct0b"][:, :], cp["a_ct0b"][:, :],
                                            cp["a_ct1b"][:, :], ALU.add))
            vstep(lambda v: v.tensor_tensor(cp["a_ct1b"][:, :], cp["a_ct0"][:, :],
                                            cp["a_ct1"][:, :], ALU.add))
            vstep(lambda v: v.tensor_tensor(cp["a_ct1b"][:, :], cp["a_ct1b"][:, :],
                                            cp["a_cmb1"][:, :], ALU.mult))
            vstep(lambda v: v.tensor_tensor(cp["a_ct0b"][:, :], cp["a_ct0b"][:, :],
                                            cp["a_ct1b"][:, :], ALU.add))
            vstep(lambda v: v.tensor_tensor(cp["a_cx"][:, :], cp["a_cx"][:, :],
                                            cp["a_ct0b"][:, :], ALU.add))
            ve_cmb[ot] = s_ve.n
            es.wait("sync", s_ve)
            fin = inc(s_st, 16)
            es.sync(lambda sy, ot=ot, nt=nt, fin=fin: fin(sy.dma_start(
                out=g["x_res"][ot * 128:(ot + 1) * 128,
                               nt * 512:nt * 512 + 512],
                in_=cp["a_cx"][:, :])))
            es.wait("vector", s_st)


def kernel(**inputs):
    maps = stage_inputs(inputs)
    nc = build_program()
    res = run_bass_kernel_spmd(nc, maps, core_ids=list(range(NCORE)))
    out = np.zeros((B, S, D), np.float32)
    for c in range(NCORE):
        b, q = divmod(c, 4)
        out[b, q * TQ:(q + 1) * TQ, :] = np.asarray(res.results[c]["out_f"]).T
    return out


# version 03 (final)
# speedup vs baseline: 1.2478x; 1.0359x over previous
import math
import numpy as np
import ml_dtypes
from contextlib import ExitStack

import concourse.bass as bass
import concourse.mybir as mybir
from concourse.bass_utils import run_bass_kernel_spmd

F32 = mybir.dt.float32
BF16 = mybir.dt.bfloat16
AF = mybir.ActivationFunctionType
ALU = mybir.AluOpType
AX = mybir.AxisListType
BF16NP = ml_dtypes.bfloat16

B, S, D = 2, 8192, 2048
TQ = 2048
NCORE = 8
FC = D // 128
INTER = 4096
MH, MHD, MST, MCH = 32, 128, 128, 256
CONV_DIM = INTER + 2 * MST
PROJ_M = INTER + CONV_DIM + MH
HBC_MT = CONV_DIM // 128
HALO = 3
XOFF = 4
TQH = TQ + XOFF
NCHK = TQ // MCH
AH, AHD = 16, 128
HG = 4
FF = 8192
EXP_SHIFT = 12.0
NEG = -1.0e9
RV = 1.0 / math.sqrt(AHD)
GROUPS4 = [[0, 1, 2, 3], [4, 5, 6, 7]]

STAGES = ("mamba", "ffn0", "attnb", "ffn1")
M_PARTS = ("ln", "inprojA", "inprojB", "dt", "sloc", "gather", "sbk", "ssd")
DEBUG_TAPS = ()


def _act_consts():
    return [1e-5, 1.0, -1.0, RV, -EXP_SHIFT, 0.5, 1.0 / D, 1.0 / INTER,
            math.sqrt(2.0 / math.pi), 1.0 / MCH, 100.0]


ENGS = ("tensor", "vector", "scalar", "gpsimd", "sync")


class Sem:
    def __init__(self, nc, ctx, name):
        self.h = ctx.enter_context(nc.semaphore(name))
        self.n = 0


class ES:

    def __init__(self, nc):
        self.nc = nc
        self.q = {e: [] for e in ENGS}

    def _emit(self, eng, fn):
        self.q[eng].append(fn)

    def __getattr__(self, eng):
        if eng in ENGS:
            return lambda fn: self._emit(eng, fn)
        raise AttributeError(eng)

    def wait(self, eng, sem, val=None):
        n = sem.n if val is None else val
        assert n <= sem.n, "waiting on a future semaphore value"
        if n > 0:
            self._emit(eng, lambda E: E.wait_ge(sem.h, n))

    def flush(self, name):
        q = self.q
        self.q = {e: [] for e in ENGS}
        with self.nc.Block(name) as blk:
            for eng in ENGS:
                fns = q[eng]
                if fns:
                    getattr(blk, eng)(lambda E, fns=fns: [f(E) for f in fns])


def inc(sem, k=1):
    sem.n += k
    return lambda inst: inst.then_inc(sem.h, k)


def _fmajor(a):
    return np.ascontiguousarray(np.asarray(a).T)


def stage_inputs(inp):
    x = np.asarray(inp["L0_attn_a_hidden_states"])

    shared = {}
    shared["ident_bf"] = np.eye(128, dtype=BF16NP)
    shared["ident_f"] = np.eye(128, dtype=np.float32)
    shared["ones_col_bf"] = np.ones((128, 1), BF16NP)
    shared["ones_row_bf"] = np.ones((1, 128), BF16NP)
    shared["ones_row_f"] = np.ones((1, 128), np.float32)
    shared["ones_col_f"] = np.ones((128, 1), np.float32)
    m01 = np.zeros((128, 8 * 1024), BF16NP)
    for t in range(8):
        kv = t * 128 + np.arange(128)[:, None]
        qq = np.arange(1024)[None, :]
        m01[:, t * 1024:(t + 1) * 1024] = (kv <= qq).astype(BF16NP)
    shared["mask01"] = m01
    sm = np.zeros((128, 2 * MCH), BF16NP)
    for sh in range(2):
        ss = sh * 128 + np.arange(128)[:, None]
        ll = np.arange(MCH)[None, :]
        sm[:, sh * MCH:(sh + 1) * MCH] = (ss <= ll).astype(BF16NP)
    shared["ssd_mask"] = sm
    shared["consts"] = np.repeat(np.asarray(_act_consts(), np.float32)[None, :],
                                 128, axis=0)

    shared["w_inproj_T"] = _fmajor(inp["L0_attn_a_in_proj_w"])
    shared["conv_w_c"] = np.ascontiguousarray(
        np.asarray(inp["L0_attn_a_conv_w"])[:, 0, :]
        .reshape(HBC_MT, 128, 4).transpose(1, 0, 2).reshape(128, -1))
    shared["conv_b_c"] = np.ascontiguousarray(
        np.asarray(inp["L0_attn_a_conv_b"]).reshape(HBC_MT, 128).T)
    shared["A_log"] = np.asarray(inp["L0_attn_a_A_log"]).reshape(MH, 1)
    shared["dt_bias"] = np.asarray(inp["L0_attn_a_dt_bias"]).reshape(MH, 1)
    shared["D_row"] = np.ascontiguousarray(
        np.asarray(inp["L0_attn_a_D"]).reshape(1, MH))
    shared["norm_w_row"] = np.ascontiguousarray(
        np.asarray(inp["L0_attn_a_norm_w"]).reshape(1, INTER))
    shared["w_outproj_T"] = _fmajor(inp["L0_attn_a_out_proj_w"])

    Wqkv = np.asarray(inp["L0_attn_b_Wqkv"])
    bqkv = np.asarray(inp["L0_attn_b_bqkv"])
    Wo = np.asarray(inp["L0_attn_b_Wo"])
    shared["bo_c"] = np.ascontiguousarray(
        np.asarray(inp["L0_attn_b_bo"]).reshape(FC, 128).T)

    shared["w_router_T"] = _fmajor(inp["L0_ffn_w_router"])
    shared["w_down_T"] = _fmajor(inp["L0_ffn_w_down"])
    shared["w_up_T"] = _fmajor(inp["L0_ffn_w_up"])
    shared["w_ff1_T"] = _fmajor(inp["L0_ffn_w_ff1"])
    shared["w_ff2_T"] = _fmajor(inp["L0_ffn_w_ff2"])
    for nm in ["ln1_scale", "ln1_bias", "ln2_scale", "ln2_bias", "simple_bias"]:
        shared[nm + "_c"] = np.ascontiguousarray(
            np.asarray(inp["L0_ffn_" + nm]).reshape(FC, 128).T)
    wq_T = _fmajor(inp["L0_ffn_wq"])
    wk_T = _fmajor(inp["L0_ffn_wk"])
    wv_T = _fmajor(inp["L0_ffn_wv"])
    wo_T = _fmajor(inp["L0_ffn_wo"])

    maps = []
    for c in range(NCORE):
        b, q = divmod(c, 4)
        m = dict(shared)
        xt = np.ascontiguousarray(x[b].T)
        sl = np.zeros((D, TQH), np.float32)
        t0 = q * TQ
        sl[:, XOFF:] = xt[:, t0:t0 + TQ]
        if t0 >= HALO:
            sl[:, XOFF - HALO:XOFF] = xt[:, t0 - HALO:t0]
        m["x_f"] = sl
        qmask = np.zeros((MH, 4), np.float32)
        scmask = np.zeros((MH, 16), np.float32)
        pmask = np.zeros((MH, 4), np.float32)
        for mm in range(4):
            if mm > q:
                qmask[:, mm] = 1.0
                for k in range(q + 1, mm):
                    scmask[:, mm * 4 + k] = 1.0
            if mm == q - 1:
                pmask[:, mm] = 1.0
        m["qmask"] = qmask
        m["scmask"] = scmask
        m["pmask"] = pmask
        h0 = q * HG * AHD
        m["wqkv_hg"] = np.ascontiguousarray(np.concatenate(
            [Wqkv[:, h0:h0 + 512], Wqkv[:, 2048 + h0:2048 + h0 + 512],
             Wqkv[:, 4096 + h0:4096 + h0 + 512]], axis=1))
        bq = np.concatenate([bqkv[h0:h0 + 512], bqkv[2048 + h0:2048 + h0 + 512],
                             bqkv[4096 + h0:4096 + h0 + 512]])
        m["bqkv_hg"] = np.ascontiguousarray(bq.reshape(12, 128).T)
        m["bqv_row"] = np.ascontiguousarray(bqkv[4096 + h0:4096 + h0 + 512]
                                            .reshape(1, 512))
        m["wo_b_hg"] = np.ascontiguousarray(Wo[h0:h0 + 512, :])
        m["wq_hg"] = np.ascontiguousarray(wq_T[:, h0:h0 + 512])
        m["wk_hg"] = np.ascontiguousarray(wk_T[:, h0:h0 + 512])
        m["wv_hg"] = np.ascontiguousarray(wv_T[:, h0:h0 + 512])
        m["wo_f_hg"] = np.ascontiguousarray(wo_T[h0:h0 + 512, :])
        maps.append(m)
    return maps


_TAPS = {
    "tap_xn": ([128, FC * TQH], BF16),
    "tap_hbc": ([CONV_DIM, TQ], BF16),
    "tap_gate": ([TQ, INTER], BF16),
    "tap_dt": ([MH, TQ], F32),
    "tap_acum": ([MH, TQ], F32),
    "tap_sloc": ([NCHK * INTER, MHD], BF16),
    "tap_sin": ([NCHK * INTER, MHD], BF16),
    "tap_stq": ([INTER, MHD], BF16),
    "tap_dec": ([128, 2 * MH * NCHK], F32),
    "tap_yg": ([TQ, INTER], BF16),
    "tap_ybase": ([128, MH * MHD], F32),
    "tap_kd": ([HG * AHD, S], BF16),
    "tap_vd": ([S, HG * AHD], BF16),
    "tap_ctx": ([HG * AHD, S], BF16),
    "tap_xtl": ([D, TQ], BF16),
    "tap_logits": ([3, TQ], F32),
}


def build_program():
    nc = bass.Bass("TRN2", target_bir_lowering=False, debug=False)
    g = {}

    def IN(name, shape, dtype=F32):
        g[name] = nc.dram_tensor(name, list(shape), dtype, kind="ExternalInput").ap()

    def DRI(name, shape, dtype):
        g[name] = nc.dram_tensor(name, list(shape), dtype, kind="Internal").ap()

    IN("x_f", [D, TQH])
    IN("ident_bf", [128, 128], BF16); IN("ident_f", [128, 128])
    IN("ones_col_bf", [128, 1], BF16)
    IN("ones_row_bf", [1, 128], BF16); IN("ones_row_f", [1, 128])
    IN("ones_col_f", [128, 1])
    IN("mask01", [128, 8 * 1024], BF16); IN("ssd_mask", [128, 2 * MCH], BF16)
    IN("consts", [128, len(_act_consts())])
    IN("w_inproj_T", [D, PROJ_M])
    IN("conv_w_c", [128, HBC_MT * 4]); IN("conv_b_c", [128, HBC_MT])
    IN("A_log", [MH, 1]); IN("dt_bias", [MH, 1])
    IN("D_row", [1, MH]); IN("norm_w_row", [1, INTER])
    IN("w_outproj_T", [INTER, D])
    IN("qmask", [MH, 4]); IN("scmask", [MH, 16]); IN("pmask", [MH, 4])
    IN("wqkv_hg", [D, 3 * 512]); IN("bqkv_hg", [128, 12]); IN("bqv_row", [1, 512])
    IN("wo_b_hg", [512, D]); IN("bo_c", [128, FC])
    IN("w_router_T", [D, 3]); IN("w_down_T", [D, 32]); IN("w_up_T", [32, D])
    IN("w_ff1_T", [D, FF]); IN("w_ff2_T", [FF, D])
    for nm in ["ln1_scale", "ln1_bias", "ln2_scale", "ln2_bias", "simple_bias"]:
        IN(nm + "_c", [128, FC])
    IN("wq_hg", [D, 512]); IN("wk_hg", [D, 512]); IN("wv_hg", [D, 512])
    IN("wo_f_hg", [512, D])

    g["out_f"] = nc.dram_tensor("out_f", [D, TQ], F32, kind="ExternalOutput").ap()

    DRI("x_res", [D, TQ], F32)
    DRI("gate_d", [TQ, INTER], BF16)
    DRI("hbc_d", [CONV_DIM, TQ], BF16)
    DRI("acum_d", [MH, TQ], F32)
    DRI("alast_d", [1, MH * NCHK], F32)
    DRI("pfx_d", [1, MH * NCHK], F32)
    DRI("pvrow_d", [1, 4 * MH], F32)
    DRI("sin_d", [NCHK * INTER, MHD], BF16)
    DRI("lastloc_in", [INTER, MHD], BF16)
    DRI("lastloc_out", [4 * INTER, MHD], BF16)
    DRI("wbase_d", [1, 4 * MH], F32)
    DRI("sloc_d", [NCHK * INTER, MHD], BF16)
    DRI("stq_in", [INTER, MHD], BF16)
    DRI("stq_out", [4 * INTER, MHD], BF16)
    DRI("adsum_in", [MH, 1], F32)
    DRI("adsum_out", [4 * MH, 1], F32)
    for tch in range(4):
        DRI(f"xn_ag_{tch}", [D, 512], BF16)
        DRI(f"ag_out_{tch}", [4 * D, 512], BF16)
    DRI("xn_loc", [D, TQ], BF16)
    DRI("y1_d", [D, TQ], BF16)
    DRI("xtl_d", [D, TQ], BF16)
    DRI("K_d", [HG * AHD, S], BF16)
    DRI("Q_d", [HG * AHD, S], BF16)
    DRI("V_d", [S, HG * AHD], BF16)
    DRI("ctx_d", [HG * AHD, S], BF16)
    DRI("rs_in2", [4 * D, TQ], BF16)
    DRI("rs_out2", [D, TQ], BF16)
    DRI("wff1_bf", [D, FF], BF16)
    DRI("wff2_bf", [FF, D], BF16)

    for nm in DEBUG_TAPS:
        shp, dt = _TAPS[nm]
        g[nm] = nc.dram_tensor(nm, list(shp), dt, kind="ExternalOutput").ap()

    with ExitStack() as ctx:
        _build(nc, ctx, g)
    return nc


def _build(nc, ctx, g):
    cp = {"g": g, "nc": nc, "es": ES(nc)}
    for nm in ["s_wld", "s_xld", "s_pe", "s_ve", "s_se", "s_ge", "s_cc", "s_st"]:
        cp[nm] = Sem(nc, ctx, nm)

    cvals = _act_consts()
    c_tile = ctx.enter_context(nc.sbuf_tensor("c_consts", [128, len(cvals)], F32))
    for i, v in enumerate(cvals):
        nc.const_aps.register(F32, float(v), c_tile[:, i:i + 1])
    cp["c_tile"] = c_tile

    def PSB(name, shape, dtype=F32):
        t = ctx.enter_context(nc.sbuf_tensor("sb_" + name, list(shape), dtype))
        cp[name] = t
        return t

    PSB("ident_bf", [128, 128], BF16)
    PSB("ident_f", [128, 128])
    PSB("ones_col", [128, 1], BF16)
    PSB("ones_row", [1, 128], BF16)
    PSB("ones_row_f", [1, 128])
    PSB("ones_col_f", [128, 1])
    PSB("ssd_mask", [128, 2 * MCH], BF16)
    for nm in ["r_mean", "r_ex2", "r_m2", "r_var", "r_std", "r_rstd", "r_b"]:
        PSB(nm, [1, 512])
    PSB("ln_a", [128, 512])
    PSB("ln_b", [128, 512])

    cp["pk"] = [ctx.enter_context(nc.psum_tensor(f"pk{i}", [128, 512], F32))
                for i in range(6)]
    cp["pb"] = ctx.enter_context(nc.psum_tensor("pb", [128, 1024], BF16))

    es = cp["es"]
    s_wld = cp["s_wld"]
    for dst, src in [(c_tile, g["consts"]), (cp["ident_bf"], g["ident_bf"]),
                     (cp["ident_f"], g["ident_f"]), (cp["ones_col"], g["ones_col_bf"]),
                     (cp["ones_row"], g["ones_row_bf"]),
                     (cp["ones_row_f"], g["ones_row_f"]),
                     (cp["ones_col_f"], g["ones_col_f"]),
                     (cp["ssd_mask"], g["ssd_mask"])]:
        fin = inc(s_wld, 16)
        es.gpsimd(lambda gp, dst=dst, src=src, fin=fin:
                  fin(gp.dma_start(out=dst[:], in_=src)))
    es.flush("preload")

    if "mamba" in STAGES:
        stage_mamba(nc, ctx, cp)
    if "ffn0" in STAGES:
        stage_attn_block(nc, ctx, cp, kind="ffn", call=0)
    if "attnb" in STAGES:
        stage_attn_block(nc, ctx, cp, kind="attnb", call=0)
    if "ffn1" in STAGES:
        stage_attn_block(nc, ctx, cp, kind="ffn", call=1)

    _finalize(nc, ctx, cp)


def _finalize(nc, ctx, cp):
    g, es = cp["g"], cp["es"]
    s_st = cp["s_st"]
    for s in ["s_pe", "s_ve", "s_se", "s_ge", "s_st", "s_wld", "s_xld", "s_cc"]:
        es.wait("sync", cp[s])
    fin = inc(s_st, 16)
    es.sync(lambda sy, fin=fin: fin(sy.dma_start(out=g["out_f"], in_=g["x_res"])))
    es.wait("sync", s_st)
    es.flush("finalize")


def emit_ln(nc, cp, src_sb, ntok, dst_sb, sq_sb, gamma=None, beta=None):
    es = cp["es"]
    pk = cp["pk"]
    s_pe, s_ve, s_se = cp["s_pe"], cp["s_ve"], cp["s_se"]

    es.wait("scalar", cp["s_wld"])
    es.wait("scalar", s_ve)
    es.wait("vector", cp["s_wld"])
    fin = inc(s_se)
    es.scalar(lambda sc, fin=fin:
              fin(sc.activation(sq_sb[:, 0:FC * ntok], src_sb[:, 0:FC * ntok],
                                AF.Square)))
    tiles = []
    t0 = 0
    while t0 < ntok:
        w = min(512, ntok - t0)
        tiles.append((t0, w))
        t0 += w

    es.wait("tensor", s_se)
    for (t0_, w) in tiles:
        fin = inc(s_pe)
        def stats(t, t0_=t0_, w=w, fin=fin):
            for cc in range(FC):
                t.matmul(pk[0][0:1, 0:w], cp["ones_col"][:, :],
                         src_sb[:, cc * ntok + t0_: cc * ntok + t0_ + w],
                         start=(cc == 0), stop=(cc == FC - 1))
            ins = None
            for cc in range(FC):
                ins = t.matmul(pk[1][0:1, 0:w], cp["ones_col"][:, :],
                               sq_sb[:, cc * ntok + t0_: cc * ntok + t0_ + w],
                               start=(cc == 0), stop=(cc == FC - 1))
            fin(ins)
        es.tensor(stats)

        es.wait("vector", s_pe)
        fin = inc(s_ve)
        es.vector(lambda v, w=w, fin=fin: (
            v.tensor_scalar(cp["r_mean"][0:1, 0:w], pk[0][0:1, 0:w],
                            1.0 / D, None, ALU.mult),
            fin(v.tensor_scalar(cp["r_ex2"][0:1, 0:w], pk[1][0:1, 0:w],
                                1.0 / D, None, ALU.mult))))
        es.wait("scalar", s_ve)
        fin = inc(s_se)
        es.scalar(lambda sc, w=w, fin=fin:
                  fin(sc.activation(cp["r_m2"][0:1, 0:w], cp["r_mean"][0:1, 0:w],
                                    AF.Square)))
        es.wait("vector", s_se)
        fin = inc(s_ve)
        es.vector(lambda v, w=w, fin=fin:
                  fin(v.tensor_tensor(cp["r_var"][0:1, 0:w], cp["r_ex2"][0:1, 0:w],
                                      cp["r_m2"][0:1, 0:w], ALU.subtract)))
        es.wait("scalar", s_ve)
        fin = inc(s_se)
        es.scalar(lambda sc, w=w, fin=fin:
                  fin(sc.activation(cp["r_std"][0:1, 0:w], cp["r_var"][0:1, 0:w],
                                    AF.Sqrt, bias=1e-5)))
        es.wait("vector", s_se)
        fin = inc(s_ve)
        es.vector(lambda v, w=w, fin=fin:
                  fin(v.reciprocal(cp["r_rstd"][0:1, 0:w], cp["r_std"][0:1, 0:w])))
        es.wait("vector", s_ve)
        fin = inc(s_ve)
        es.vector(lambda v, w=w, fin=fin:
                  fin(v.scalar_tensor_tensor(cp["r_b"][0:1, 0:w],
                                             cp["r_mean"][0:1, 0:w], -1.0,
                                             cp["r_rstd"][0:1, 0:w],
                                             ALU.mult, ALU.mult)))
        es.wait("tensor", s_ve)
        fin = inc(s_pe)
        es.tensor(lambda t, w=w, fin=fin: (
            t.matmul(pk[2][:, 0:w], cp["ones_row_f"][:, :], cp["r_rstd"][0:1, 0:w],
                     start=True, stop=True),
            fin(t.matmul(pk[3][:, 0:w], cp["ones_row_f"][:, :], cp["r_b"][0:1, 0:w],
                         start=True, stop=True))))
        es.wait("vector", s_pe)
        fin = inc(s_ve)
        es.vector(lambda v, w=w, fin=fin: (
            v.tensor_copy(cp["ln_a"][:, 0:w], pk[2][:, 0:w]),
            fin(v.tensor_copy(cp["ln_b"][:, 0:w], pk[3][:, 0:w]))))
        es.wait("vector", s_ve)
        fin = inc(s_ve)
        def apply1(v, t0_=t0_, w=w, fin=fin):
            ins = None
            for cc in range(FC):
                ins = v.tensor_tensor(
                    sq_sb[:, cc * ntok + t0_: cc * ntok + t0_ + w],
                    src_sb[:, cc * ntok + t0_: cc * ntok + t0_ + w],
                    cp["ln_a"][:, 0:w], ALU.mult)
            fin(ins)
        es.vector(apply1)
        es.wait("vector", s_ve)
        fin = inc(s_ve)
        def apply2(v, t0_=t0_, w=w, fin=fin):
            ins = None
            for cc in range(FC):
                ins = v.tensor_tensor(
                    dst_sb[:, cc * ntok + t0_: cc * ntok + t0_ + w],
                    sq_sb[:, cc * ntok + t0_: cc * ntok + t0_ + w],
                    cp["ln_b"][:, 0:w], ALU.add)
            fin(ins)
        es.vector(apply2)
        if gamma is not None:
            es.wait("vector", s_ve)
            fin = inc(s_ve)
            def apply3(v, t0_=t0_, w=w, fin=fin):
                ins = None
                for cc in range(FC):
                    ins = v.tensor_scalar(
                        dst_sb[:, cc * ntok + t0_: cc * ntok + t0_ + w],
                        dst_sb[:, cc * ntok + t0_: cc * ntok + t0_ + w],
                        gamma[:, cc:cc + 1], beta[:, cc:cc + 1],
                        ALU.mult, ALU.add)
                fin(ins)
            es.vector(apply3)
        es.wait("tensor", s_ve)


def stage_mamba(nc, ctx, cp):
    g, es = cp["g"], cp["es"]

    with ExitStack() as octx:
        def OSB(name, shape, dtype=F32):
            t = octx.enter_context(nc.sbuf_tensor("sb_" + name, list(shape), dtype))
            cp[name] = t
            return t

        OSB("normw_bc", [128, INTER], BF16)
        OSB("D_bc", [128, MH])
        OSB("row_D", [1, MH])
        OSB("m_dt", [MH, TQ])
        OSB("m_acum", [MH, TQ])
        OSB("m_decbc", [128, MH * NCHK])
        OSB("m_pfxbc", [128, MH * NCHK])
        OSB("m_alrow", [1, MH * NCHK])
        OSB("m_convw", [128, HBC_MT * 4])
        OSB("m_convb", [128, HBC_MT])
        OSB("m_Alog", [MH, 1]); OSB("m_dtb", [MH, 1]); OSB("m_A", [MH, 1])
        OSB("m_alast", [MH, NCHK]); OSB("m_adsum", [MH, 1])
        OSB("m_pfx", [MH, NCHK])
        OSB("m_qmask", [MH, 4]); OSB("m_scmask", [MH, 16])
        OSB("m_pmask", [MH, 4])
        OSB("m_dtraw", [MH, TQH])

        with ExitStack() as cprep:
            cp["row_nw"] = cprep.enter_context(
                nc.sbuf_tensor("sb_row_nw", [1, INTER], F32))
            _mamba_prep(nc, cp)

        with ExitStack() as cx:
            xb = cx.enter_context(nc.sbuf_tensor("sb_m_xb", [128, FC * TQH], BF16))
            cp["m_xb"] = xb
            with ExitStack() as c1:
                cp["lnsq"] = c1.enter_context(
                    nc.sbuf_tensor("sb_lnsq", [128, FC * TQH], BF16))
                fin = inc(cp["s_wld"], 16)
                es.gpsimd(lambda gp, fin=fin: fin(gp.dma_start(
                    out=xb[:].rearrange("p (cc t) -> p cc t", cc=FC),
                    in_=g["x_f"].rearrange("(cc p) t -> p cc t", p=128))))
                es.flush("m_loadx")
                if "ln" in M_PARTS:
                    emit_ln(nc, cp, xb, TQH, xb, cp["lnsq"])
                if "tap_xn" in DEBUG_TAPS:
                    es.wait("sync", cp["s_ve"])
                    fin = inc(cp["s_st"], 16)
                    es.sync(lambda sy, fin=fin:
                            fin(sy.dma_start(out=g["tap_xn"], in_=xb[:, :])))
                es.flush("m_ln")

            with ExitStack() as c2:
                def SB2(name, shape, dtype=F32):
                    t = c2.enter_context(
                        nc.sbuf_tensor("sb_" + name, list(shape), dtype))
                    cp[name] = t
                    return t
                SB2("m_wslab0", [128, FC * 128], BF16)
                SB2("m_wslab1", [128, FC * 128], BF16)
                SB2("m_craw0", [128, TQH], BF16)
                SB2("m_craw1", [128, TQH], BF16)
                SB2("m_ctmp0", [128, TQ], BF16)
                SB2("m_ctmp1", [128, TQ], BF16)
                SB2("m_hstage0", [128, TQ], BF16)
                SB2("m_hstage1", [128, TQ], BF16)
                if "inprojA" in M_PARTS:
                    _inproj_hbc_dt(nc, cp)
                es.flush("m_inprojA")
            with ExitStack() as c2:
                def SB2(name, shape, dtype=F32):
                    t = c2.enter_context(
                        nc.sbuf_tensor("sb_" + name, list(shape), dtype))
                    cp[name] = t
                    return t
                SB2("m_gslab0", [128, FC * 512], BF16)
                SB2("m_gslab1", [128, FC * 512], BF16)
                SB2("m_gstage0", [128, 512], BF16)
                SB2("m_gstage1", [128, 512], BF16)
                if "inprojB" in M_PARTS:
                    _inproj_gate(nc, cp)
                es.flush("m_inprojB")
        with ExitStack() as c2:
            def SB2(name, shape, dtype=F32):
                t = c2.enter_context(nc.sbuf_tensor("sb_" + name, list(shape), dtype))
                cp[name] = t
                return t
            SB2("m_t1", [MH, TQ]); SB2("m_t2", [MH, TQ]); SB2("m_zeros", [MH, MCH])
            if "dt" in M_PARTS:
                _dt_path(nc, cp)
            es.flush("m_dt")

        with ExitStack() as c3:
            def SB3(name, shape, dtype=F32):
                t = c3.enter_context(nc.sbuf_tensor("sb_" + name, list(shape), dtype))
                cp[name] = t
                return t
            SB3("m_hbc", [128, HBC_MT * MCH], BF16)
            SB3("m_dtT", [128, 2 * MH])
            SB3("m_acT", [128, 2 * MH])
            SB3("m_dsT", [128, 2 * MH])
            SB3("m_ds", [MH, MCH])
            SB3("m_xdtT", [128, 2 * MHD], BF16)
            SB3("m_xrawT", [128, 2 * MHD], BF16)
            SB3("m_bT", [128, 2 * MST], BF16)
            SB3("m_bdTh", [128, 2 * MST], BF16)
            SB3("m_sinc", [128, MH * MHD], BF16)
            with ExitStack() as c4:
                def SB4(name, shape, dtype=F32):
                    t = c4.enter_context(
                        nc.sbuf_tensor("sb_" + name, list(shape), dtype))
                    cp[name] = t
                    return t
                SB4("m_srun", [128, MH * MHD])
                SB4("m_prev", [128, MH * MHD])
                SB4("m_slocp", [128, MH * MHD], BF16)
                SB4("m_slocc", [128, MH * MHD], BF16)
                SB4("m_sstage", [128, MH * MHD], BF16)
                if "sloc" in M_PARTS:
                    _ssd_local_states(nc, cp)
                es.flush("m_sloc")
                with ExitStack() as c5:
                    def SB5(name, shape, dtype=F32):
                        t = c5.enter_context(
                            nc.sbuf_tensor("sb_" + name, list(shape), dtype))
                        cp[name] = t
                        return t
                    SB5("m_adsq", [MH, 4]); SB5("m_wexp", [MH, 4])
                    SB5("m_stq", [128, 4 * MH * MHD], BF16)
                    SB5("m_wrow", [1, 4 * MH]); SB5("m_wbc", [128, 4 * MH])
                    if "gather" in M_PARTS:
                        _state_gather_combine(nc, cp)
                    es.flush("m_gather")
                if "sbk" in M_PARTS:
                    _state_backward(nc, cp)
                es.flush("m_sbk")
            SB3("m_acrow", [1, (MH // 2) * MCH])
            SB3("m_gm", [128, 2 * MCH])
            SB3("m_eh", [128, 2 * MCH])
            SB3("m_mh", [128, 2 * MCH], BF16)
            SB3("m_csc", [128, MCH], BF16)
            SB3("m_sdbc", [128, MCH])
            SB3("m_ych", [128, 2 * INTER], BF16)
            SB3("m_rsum", [128, 1]); SB3("m_rstd", [128, 1])
            SB3("m_gatec", [128, 2 * INTER], BF16)
            SB3("m_ynf", [128, (INTER // 128) * 1024], BF16)
            if "ssd" in M_PARTS:
                _ssd_main(nc, cp)
            es.flush("m_ssd")


def _mamba_prep(nc, cp):
    g, es = cp["g"], cp["es"]
    pk = cp["pk"]
    s_wld, s_pe, s_se = cp["s_wld"], cp["s_pe"], cp["s_se"]

    for dst, src in [(cp["row_nw"], g["norm_w_row"]), (cp["row_D"], g["D_row"]),
                     (cp["m_convw"], g["conv_w_c"]), (cp["m_convb"], g["conv_b_c"]),
                     (cp["m_Alog"], g["A_log"]), (cp["m_dtb"], g["dt_bias"]),
                     (cp["m_qmask"], g["qmask"]), (cp["m_scmask"], g["scmask"]),
                     (cp["m_pmask"], g["pmask"])]:
        fin = inc(s_wld, 16)
        es.gpsimd(lambda gp, dst=dst, src=src, fin=fin:
                  fin(gp.dma_start(out=dst[:], in_=src)))
    es.wait("tensor", s_wld)
    nseg = INTER // 512
    for j4 in range(0, nseg, 4):
        fin = inc(s_pe)
        def bc4(t, j4=j4, fin=fin):
            ins = None
            for j in range(j4, min(j4 + 4, nseg)):
                ins = t.matmul(pk[j % 4][:, :], cp["ones_row_f"][:, :],
                               cp["row_nw"][0:1, j * 512:(j + 1) * 512],
                               start=True, stop=True)
            fin(ins)
        es.tensor(bc4)
        es.wait("scalar", s_pe)
        fin = inc(s_se)
        def ev4(sc, j4=j4, fin=fin):
            ins = None
            for j in range(j4, min(j4 + 4, nseg)):
                ins = sc.copy(cp["normw_bc"][:, j * 512:(j + 1) * 512],
                              pk[j % 4][:, :])
            fin(ins)
        es.scalar(ev4)
        es.wait("tensor", s_se)
    fin = inc(s_pe)
    es.tensor(lambda t, fin=fin:
              fin(t.matmul(pk[0][:, 0:MH], cp["ones_row_f"][:, :],
                           cp["row_D"][0:1, :], start=True, stop=True)))
    es.wait("scalar", s_pe)
    fin = inc(s_se)
    es.scalar(lambda sc, fin=fin: fin(sc.copy(cp["D_bc"][:, :], pk[0][:, 0:MH])))
    es.wait("tensor", s_se)
    es.flush("m_prep")


def _inproj_hbc_dt(nc, cp):
    g, es = cp["g"], cp["es"]
    pk = cp["pk"]
    s_wld, s_pe, s_ve, s_se, s_st = (
        cp[k] for k in ["s_wld", "s_pe", "s_ve", "s_se", "s_st"])
    xn = cp["m_xb"]

    ntiles = [(t0, min(512, TQH - t0)) for t0 in range(0, TQH, 512)]
    bank_guard = {0: 0, 1: 0}
    NMT = HBC_MT + 1
    pe_after_m = []
    cons_after_m = []

    for m in range(NMT):
        mw = 128 if m < HBC_MT else MH
        m0 = INTER + m * 128
        buf = m % 2
        if m >= 2:
            es.wait("gpsimd", s_pe, pe_after_m[m - 1])
        fin = inc(s_wld, 16)
        es.gpsimd(lambda gp, m0=m0, mw=mw, buf=buf, fin=fin: fin(gp.dma_start(
            out=cp[f"m_wslab{buf}"][:, 0:FC * mw].rearrange(
                "p (kc m) -> p kc m", kc=FC),
            in_=g["w_inproj_T"][:, m0:m0 + mw].rearrange(
                "(kc p) m -> p kc m", p=128))))
        es.wait("tensor", s_wld)
        if m >= 2:
            es.wait("tensor", s_se, cons_after_m[m - 2][1])
            es.wait("tensor", s_ve, cons_after_m[m - 2][0])
        for ni, (t0_, w) in enumerate(ntiles):
            if bank_guard[ni % 2]:
                es.wait("tensor", s_se, bank_guard[ni % 2])
            fin = inc(s_pe)
            def mms(t, mw=mw, buf=buf, ni=ni, t0_=t0_, w=w, fin=fin):
                bank = pk[ni % 2]
                ins = None
                for kc in range(FC):
                    ins = t.matmul(bank[0:mw, 0:w],
                                   cp[f"m_wslab{buf}"][:, kc * mw:kc * mw + mw],
                                   xn[:, kc * TQH + t0_: kc * TQH + t0_ + w],
                                   start=(kc == 0), stop=(kc == FC - 1))
                fin(ins)
            es.tensor(mms)
            es.wait("scalar", s_pe)
            fin = inc(s_se)
            es.scalar(lambda sc, mw=mw, buf=buf, ni=ni, t0_=t0_, w=w, fin=fin:
                      fin(sc.copy(cp[f"m_craw{buf}"][0:mw, t0_:t0_ + w],
                                  pk[ni % 2][0:mw, 0:w])))
            bank_guard[ni % 2] = s_se.n
        if m < HBC_MT:
            cw = cp["m_convw"]
            es.wait("vector", s_se)
            fin = inc(s_ve)
            es.vector(lambda v, m=m, buf=buf, fin=fin:
                      fin(v.tensor_scalar(cp[f"m_ctmp{buf}"][:, 0:TQ],
                                          cp[f"m_craw{buf}"][:, 1:1 + TQ],
                                          cw[:, 4 * m:4 * m + 1], None, ALU.mult)))
            for j in range(1, 4):
                es.wait("vector", s_ve)
                fin = inc(s_ve)
                es.vector(lambda v, m=m, j=j, buf=buf, fin=fin:
                          fin(v.scalar_tensor_tensor(
                              cp[f"m_ctmp{buf}"][:, 0:TQ],
                              cp[f"m_craw{buf}"][:, 1 + j:1 + j + TQ],
                              cw[:, 4 * m + j:4 * m + j + 1],
                              cp[f"m_ctmp{buf}"][:, 0:TQ], ALU.mult, ALU.add)))
            es.wait("scalar", s_ve)
            fin = inc(s_se)
            es.scalar(lambda sc, m=m, buf=buf, fin=fin:
                      fin(sc.activation(cp[f"m_hstage{buf}"][:, 0:TQ],
                                        cp[f"m_ctmp{buf}"][:, 0:TQ], AF.Silu,
                                        bias=cp["m_convb"][:, m:m + 1])))
            es.wait("sync", s_se)
            fin = inc(s_st, 16)
            es.sync(lambda sy, m=m, buf=buf, fin=fin:
                    fin(sy.dma_start(out=g["hbc_d"][m * 128:(m + 1) * 128, :],
                                     in_=cp[f"m_hstage{buf}"][:, 0:TQ])))
        else:
            es.wait("vector", s_se)
            fin = inc(s_ve)
            es.vector(lambda v, buf=buf, fin=fin:
                      fin(v.tensor_copy(cp["m_dtraw"][:, 0:TQH],
                                        cp[f"m_craw{buf}"][0:MH, 0:TQH])))
        pe_after_m.append(s_pe.n)
        cons_after_m.append((s_ve.n, s_se.n))

    if "tap_hbc" in DEBUG_TAPS:
        es.wait("sync", s_st)
        fin = inc(s_st, 16)
        es.sync(lambda sy, fin=fin:
                fin(sy.dma_start(out=g["tap_hbc"], in_=g["hbc_d"])))


def _inproj_gate(nc, cp):
    g, es = cp["g"], cp["es"]
    pk = cp["pk"]
    s_wld, s_pe, s_se, s_st = (cp[k] for k in ["s_wld", "s_pe", "s_se", "s_st"])
    xn = cp["m_xb"]

    NMB = INTER // 512
    NTT = TQ // 128
    pe_after_b = []
    se_at = {}
    st_hist = {}

    for mb in range(NMB):
        buf = mb % 2
        if mb >= 2:
            es.wait("gpsimd", s_pe, pe_after_b[mb - 1])
        fin = inc(s_wld, 16)
        es.gpsimd(lambda gp, mb=mb, buf=buf, fin=fin: fin(gp.dma_start(
            out=cp[f"m_gslab{buf}"][:].rearrange("p (kc m) -> p kc m", kc=FC),
            in_=g["w_inproj_T"][:, mb * 512:(mb + 1) * 512].rearrange(
                "(kc p) m -> p kc m", p=128))))
        es.wait("tensor", s_wld)
        for tt in range(NTT):
            it = mb * NTT + tt
            if it >= 2:
                es.wait("tensor", s_se, se_at[it - 2])
            fin = inc(s_pe)
            def mms(t, tt=tt, buf=buf, fin=fin):
                bank = pk[tt % 2]
                ins = None
                for kc in range(FC):
                    ins = t.matmul(
                        bank[:, 0:512],
                        xn[:, kc * TQH + XOFF + tt * 128:
                           kc * TQH + XOFF + tt * 128 + 128],
                        cp[f"m_gslab{buf}"][:, kc * 512:(kc + 1) * 512],
                        start=(kc == 0), stop=(kc == FC - 1))
                fin(ins)
            es.tensor(mms)
            es.wait("scalar", s_pe)
            if it >= 2:
                es.wait("scalar", s_st, st_hist[it - 2])
            fin = inc(s_se)
            es.scalar(lambda sc, tt=tt, fin=fin:
                      fin(sc.activation(cp[f"m_gstage{tt % 2}"][:, 0:512],
                                        pk[tt % 2][:, 0:512], AF.Silu)))
            se_at[it] = s_se.n
            es.wait("sync", s_se)
            fin = inc(s_st, 16)
            es.sync(lambda sy, mb=mb, tt=tt, fin=fin:
                    fin(sy.dma_start(
                        out=g["gate_d"][tt * 128:(tt + 1) * 128,
                                        mb * 512:(mb + 1) * 512],
                        in_=cp[f"m_gstage{tt % 2}"][:, 0:512])))
            st_hist[it] = s_st.n
        pe_after_b.append(s_pe.n)

    if "tap_gate" in DEBUG_TAPS:
        es.wait("sync", s_st)
        fin = inc(s_st, 16)
        es.sync(lambda sy, fin=fin:
                fin(sy.dma_start(out=g["tap_gate"], in_=g["gate_d"])))


def _dt_path(nc, cp):
    g, es = cp["g"], cp["es"]
    s_wld, s_pe, s_ve, s_se, s_st = (
        cp[k] for k in ["s_wld", "s_pe", "s_ve", "s_se", "s_st"])
    dt = cp["m_dt"]
    acum = cp["m_acum"]
    t1, t2 = cp["m_t1"], cp["m_t2"]
    dtr = cp["m_dtraw"]

    def step(eng, sem, fn):
        es.wait(eng, sem)
        fin = inc(cp[{"vector": "s_ve", "scalar": "s_se"}[eng]])
        getattr(es, eng)(lambda E, fn=fn, fin=fin: fin(fn(E)))

    es.wait("vector", s_ve)
    fin = inc(s_ve)
    es.vector(lambda v, fin=fin: (
        v.memset(cp["m_zeros"][:], 0.0),
        fin(v.tensor_scalar(t1[:, :], dtr[:, XOFF:XOFF + TQ],
                            cp["m_dtb"][:, 0:1], None, ALU.add))))
    step("scalar", s_ve,
         lambda sc: sc.activation(t2[:, :], t1[:, :], AF.Abs))
    step("scalar", s_se,
         lambda sc: sc.activation(t2[:, :], t2[:, :], AF.Exp, scale=-1.0))
    step("scalar", s_se,
         lambda sc: sc.activation(t2[:, :], t2[:, :], AF.Ln, bias=1.0))
    step("vector", s_ve,
         lambda v: v.tensor_scalar(dt[:, :], t1[:, :], 0.0, None, ALU.max))
    es.wait("vector", s_se)
    step("vector", s_ve,
         lambda v: v.tensor_tensor(dt[:, :], dt[:, :], t2[:, :], ALU.add))
    step("vector", s_ve,
         lambda v: v.tensor_scalar(dt[:, :], dt[:, :], 100.0, None, ALU.min))
    es.wait("scalar", s_wld)
    step("scalar", s_se,
         lambda sc: sc.activation(cp["m_A"][:, :], cp["m_Alog"][:, :], AF.Exp))
    es.wait("vector", s_se)
    step("vector", s_ve,
         lambda v: v.tensor_scalar(cp["m_A"][:, :], cp["m_A"][:, :],
                                   -1.0, None, ALU.mult))
    step("vector", s_ve,
         lambda v: v.tensor_scalar(t1[:, :], dt[:, :], cp["m_A"][:, 0:1],
                                   None, ALU.mult))

    es.wait("vector", s_ve)
    fin = inc(s_ve)
    def scans(v, fin=fin):
        ins = None
        for c in range(NCHK):
            ins = v.tensor_tensor_scan(acum[:, c * MCH:(c + 1) * MCH],
                                       cp["m_zeros"][:, :],
                                       t1[:, c * MCH:(c + 1) * MCH],
                                       0.0, ALU.add, ALU.add)
        fin(ins)
    es.vector(scans)
    es.wait("vector", s_ve)
    fin = inc(s_ve)
    def lasts(v, fin=fin):
        ins = None
        for c in range(NCHK):
            ins = v.tensor_copy(cp["m_alast"][:, c:c + 1],
                                acum[:, (c + 1) * MCH - 1:(c + 1) * MCH])
        fin(ins)
    es.vector(lasts)
    es.wait("vector", s_ve)
    fin = inc(s_ve)
    es.vector(lambda v, fin=fin:
              fin(v.reduce_sum(cp["m_adsum"][:, :], cp["m_alast"][:, :],
                               axis=AX.X)))
    es.wait("vector", s_ve)
    fin = inc(s_ve)
    es.vector(lambda v, fin=fin:
              fin(v.tensor_tensor_scan(cp["m_pfx"][:, :], cp["m_zeros"][:, 0:NCHK],
                                       cp["m_alast"][:, :], 0.0, ALU.add, ALU.add)))
    es.wait("scalar", s_ve)
    fin = inc(s_se)
    es.scalar(lambda sc, fin=fin: (
        sc.activation(cp["m_pfx"][:, :], cp["m_pfx"][:, :], AF.Exp),
        fin(sc.activation(cp["m_alast"][:, :], cp["m_alast"][:, :], AF.Exp))))
    es.wait("sync", s_se)
    es.wait("sync", s_ve)
    for fn in [
        lambda sy, fin: fin(sy.dma_start(out=g["acum_d"], in_=acum[:, :])),
        lambda sy, fin: fin(sy.dma_start(
            out=g["alast_d"].rearrange("one (h c) -> (one h) c", h=MH),
            in_=cp["m_alast"][:, :])),
        lambda sy, fin: fin(sy.dma_start(out=g["adsum_in"], in_=cp["m_adsum"][:, :])),
        lambda sy, fin: fin(sy.dma_start(
            out=g["pfx_d"].rearrange("one (h c) -> (one h) c", h=MH),
            in_=cp["m_pfx"][:, :])),
    ]:
        fin = inc(s_st, 16)
        es.sync(lambda sy, fn=fn, fin=fin: fn(sy, fin))
    if "tap_dt" in DEBUG_TAPS:
        fin = inc(s_st, 16)
        es.sync(lambda sy, fin=fin: fin(sy.dma_start(out=g["tap_dt"], in_=dt[:, :])))
    if "tap_acum" in DEBUG_TAPS:
        fin = inc(s_st, 16)
        es.sync(lambda sy, fin=fin:
                fin(sy.dma_start(out=g["tap_acum"], in_=acum[:, :])))

    es.wait("gpsimd", s_st)
    fin = inc(s_wld, 16)
    es.gpsimd(lambda gp, fin=fin:
              fin(gp.dma_start(out=cp["m_alrow"][:, :], in_=g["alast_d"])))
    es.wait("tensor", s_wld)
    es.wait("tensor", s_ve)
    fin = inc(cp["s_pe"])
    es.tensor(lambda t, fin=fin:
              fin(t.matmul(cp["pk"][4][:, 0:MH * NCHK], cp["ones_row_f"][:, :],
                           cp["m_alrow"][0:1, :], start=True, stop=True)))
    es.wait("vector", cp["s_pe"])
    fin = inc(s_ve)
    es.vector(lambda v, fin=fin:
              fin(v.tensor_copy(cp["m_decbc"][:, :], cp["pk"][4][:, 0:MH * NCHK])))
    es.wait("gpsimd", s_ve)
    fin = inc(s_wld, 16)
    es.gpsimd(lambda gp, fin=fin:
              fin(gp.dma_start(out=cp["m_alrow"][:, :], in_=g["pfx_d"])))
    es.wait("tensor", s_wld)
    es.wait("tensor", s_ve)
    fin = inc(cp["s_pe"])
    es.tensor(lambda t, fin=fin:
              fin(t.matmul(cp["pk"][4][:, 0:MH * NCHK], cp["ones_row_f"][:, :],
                           cp["m_alrow"][0:1, :], start=True, stop=True)))
    es.wait("vector", cp["s_pe"])
    fin = inc(s_ve)
    es.vector(lambda v, fin=fin:
              fin(v.tensor_copy(cp["m_pfxbc"][:, :], cp["pk"][4][:, 0:MH * NCHK])))


def _ssd_local_states(nc, cp):
    g, es = cp["g"], cp["es"]
    pk, pb = cp["pk"], cp["pb"]
    s_wld, s_pe, s_ve, s_se, s_cc, s_st = (
        cp[k] for k in ["s_wld", "s_pe", "s_ve", "s_se", "s_cc", "s_st"])
    hbc = cp["m_hbc"]
    acum = cp["m_acum"]
    dt_sb = cp["m_dt"]

    for c in range(NCHK):
        es.wait("gpsimd", s_st)
        es.wait("gpsimd", s_pe)
        fin = inc(s_wld, 16)
        es.gpsimd(lambda gp, c=c, fin=fin: fin(gp.dma_start(
            out=hbc[:].rearrange("p (cc t) -> p cc t", cc=HBC_MT),
            in_=g["hbc_d"][:, c * MCH:(c + 1) * MCH].rearrange(
                "(cc p) t -> p cc t", p=128))))
        es.wait("vector", s_ve)
        es.wait("vector", s_pe)
        fin = inc(s_ve)
        es.vector(lambda v, c=c, fin=fin:
                  fin(v.tensor_scalar(cp["m_ds"][:, :],
                                      acum[:, c * MCH:(c + 1) * MCH],
                                      acum[:, (c + 1) * MCH - 1:(c + 1) * MCH],
                                      None, ALU.subtract)))
        es.wait("scalar", s_ve)
        fin = inc(s_se)
        es.scalar(lambda sc, fin=fin:
                  fin(sc.activation(cp["m_ds"][:, :], cp["m_ds"][:, :],
                                    AF.Exp, scale=-1.0)))
        es.wait("tensor", s_se)
        es.wait("tensor", s_wld)
        es.wait("tensor", s_ve)
        fin = inc(s_pe)
        es.tensor(lambda t, c=c, fin=fin: (
            t.transpose(pk[4][0:128, 0:MH], dt_sb[:, c * MCH: c * MCH + 128],
                        cp["ident_f"][0:MH, 0:MH]),
            t.transpose(pk[5][0:128, 0:MH], dt_sb[:, c * MCH + 128: c * MCH + 256],
                        cp["ident_f"][0:MH, 0:MH]),
            t.transpose(pk[4][0:128, MH:2 * MH], cp["m_ds"][:, 0:128],
                        cp["ident_f"][0:MH, 0:MH]),
            fin(t.transpose(pk[5][0:128, MH:2 * MH], cp["m_ds"][:, 128:256],
                            cp["ident_f"][0:MH, 0:MH]))))
        es.wait("vector", s_pe)
        fin = inc(s_ve)
        es.vector(lambda v, fin=fin: (
            v.tensor_copy(cp["m_dtT"][:, 0:MH], pk[4][:, 0:MH]),
            v.tensor_copy(cp["m_dtT"][:, MH:2 * MH], pk[5][:, 0:MH]),
            v.tensor_copy(cp["m_dsT"][:, 0:MH], pk[4][:, MH:2 * MH]),
            fin(v.tensor_copy(cp["m_dsT"][:, MH:2 * MH], pk[5][:, MH:2 * MH]))))
        fin = inc(s_pe)
        es.tensor(lambda t, fin=fin: (
            t.transpose(pb[:, 0:128], hbc[:, 32 * MCH:32 * MCH + 128],
                        cp["ident_bf"][:, :]),
            fin(t.transpose(pb[:, 128:256], hbc[:, 32 * MCH + 128:32 * MCH + 256],
                            cp["ident_bf"][:, :]))))
        es.wait("vector", s_pe)
        fin = inc(s_ve)
        es.vector(lambda v, fin=fin: (
            v.tensor_copy(cp["m_bT"][:, 0:128], pb[:, 0:128]),
            fin(v.tensor_copy(cp["m_bT"][:, 128:256], pb[:, 128:256]))))

        for h in range(MH):
            es.wait("tensor", s_ve)
            fin = inc(s_pe)
            es.tensor(lambda t, h=h, fin=fin: (
                t.transpose(pb[:, 256:384], hbc[:, h * MCH:h * MCH + 128],
                            cp["ident_bf"][:, :]),
                fin(t.transpose(pb[:, 384:512], hbc[:, h * MCH + 128:h * MCH + 256],
                                cp["ident_bf"][:, :]))))
            es.wait("vector", s_pe)
            fin = inc(s_ve)
            es.vector(lambda v, h=h, fin=fin: (
                v.tensor_scalar(cp["m_xdtT"][:, 0:MHD], pb[:, 256:384],
                                cp["m_dtT"][:, h:h + 1], None, ALU.mult),
                v.tensor_scalar(cp["m_xdtT"][:, MHD:2 * MHD], pb[:, 384:512],
                                cp["m_dtT"][:, MH + h:MH + h + 1], None, ALU.mult),
                v.tensor_scalar(cp["m_bdTh"][:, 0:MST], cp["m_bT"][:, 0:MST],
                                cp["m_dsT"][:, h:h + 1], None, ALU.mult),
                fin(v.tensor_scalar(cp["m_bdTh"][:, MST:2 * MST],
                                    cp["m_bT"][:, MST:2 * MST],
                                    cp["m_dsT"][:, MH + h:MH + h + 1],
                                    None, ALU.mult))))
            es.wait("tensor", s_ve)
            fin = inc(s_pe)
            es.tensor(lambda t, h=h, fin=fin: (
                t.matmul(pk[h % 2][:, 0:MHD], cp["m_bdTh"][:, 0:MST],
                         cp["m_xdtT"][:, 0:MHD], start=True, stop=False),
                fin(t.matmul(pk[h % 2][:, 0:MHD], cp["m_bdTh"][:, MST:2 * MST],
                             cp["m_xdtT"][:, MHD:2 * MHD], start=False, stop=True))))
            es.wait("vector", s_pe)
            fin = inc(s_ve)
            if c == 0:
                es.vector(lambda v, h=h, fin=fin: (
                    v.tensor_copy(cp["m_sstage"][:, h * MHD:(h + 1) * MHD],
                                  pk[h % 2][:, 0:MHD]),
                    fin(v.tensor_scalar(cp["m_srun"][:, h * MHD:(h + 1) * MHD],
                                        pk[h % 2][:, 0:MHD],
                                        cp["m_pfxbc"][:, h * NCHK:h * NCHK + 1],
                                        None, ALU.mult))))
            else:
                es.vector(lambda v, h=h, c=c, fin=fin: (
                    v.tensor_copy(cp["m_sstage"][:, h * MHD:(h + 1) * MHD],
                                  pk[h % 2][:, 0:MHD]),
                    fin(v.scalar_tensor_tensor(
                        cp["m_srun"][:, h * MHD:(h + 1) * MHD],
                        pk[h % 2][:, 0:MHD],
                        cp["m_pfxbc"][:, h * NCHK + c:h * NCHK + c + 1],
                        cp["m_srun"][:, h * MHD:(h + 1) * MHD],
                        ALU.mult, ALU.add))))
        es.wait("sync", s_ve)
        fin = inc(s_st, 16)
        es.sync(lambda sy, c=c, fin=fin: fin(sy.dma_start(
            out=g["sloc_d"][c * INTER:(c + 1) * INTER, :].rearrange(
                "(h n) p -> n h p", n=128),
            in_=cp["m_sstage"][:, :].rearrange("n (h p) -> n h p", h=MH))))

    es.wait("vector", s_ve)
    es.wait("vector", s_st)
    fin = inc(s_ve)
    es.vector(lambda v, fin=fin:
              fin(v.tensor_copy(cp["m_sstage"][:, :], cp["m_srun"][:, :])))
    es.wait("sync", s_ve)
    fin = inc(s_st, 16)
    es.sync(lambda sy, fin=fin: fin(sy.dma_start(
        out=g["stq_in"].rearrange("(h n) p -> n h p", n=128),
        in_=cp["m_sstage"][:, :].rearrange("n (h p) -> n h p", h=MH))))
    if "tap_sloc" in DEBUG_TAPS:
        fin = inc(s_st, 16)
        es.sync(lambda sy, fin=fin:
                fin(sy.dma_start(out=g["tap_sloc"], in_=g["sloc_d"])))
    if "tap_stq" in DEBUG_TAPS:
        es.wait("sync", s_st)
        fin = inc(s_st, 16)
        es.sync(lambda sy, fin=fin:
                fin(sy.dma_start(out=g["tap_stq"], in_=g["stq_in"])))
    if "tap_dec" in DEBUG_TAPS:
        fin = inc(s_st, 16)
        es.sync(lambda sy, fin=fin:
                fin(sy.dma_start(out=g["tap_dec"][:, 0:MH * NCHK],
                                 in_=cp["m_decbc"][:, :])))
        fin = inc(s_st, 16)
        es.sync(lambda sy, fin=fin:
                fin(sy.dma_start(out=g["tap_dec"][:, MH * NCHK:],
                                 in_=cp["m_pfxbc"][:, :])))
    es.wait("gpsimd", s_st)
    fin = inc(cp["s_wld"], 16)
    es.gpsimd(lambda gp, fin=fin: fin(gp.dma_start(
        out=g["lastloc_in"], in_=g["sloc_d"][7 * INTER:8 * INTER, :])))
    es.wait("gpsimd", cp["s_wld"])
    fin = inc(s_cc)
    es.gpsimd(lambda gp, fin=fin: fin(gp.collective_compute(
        "AllGather", ALU.bypass, replica_groups=GROUPS4,
        ins=[g["stq_in"]], outs=[g["stq_out"]])))
    fin = inc(s_cc)
    es.gpsimd(lambda gp, fin=fin: fin(gp.collective_compute(
        "AllGather", ALU.bypass, replica_groups=GROUPS4,
        ins=[g["adsum_in"]], outs=[g["adsum_out"]])))
    fin = inc(s_cc)
    es.gpsimd(lambda gp, fin=fin: fin(gp.collective_compute(
        "AllGather", ALU.bypass, replica_groups=GROUPS4,
        ins=[g["lastloc_in"]], outs=[g["lastloc_out"]])))


def _state_gather_combine(nc, cp):
    g, es = cp["g"], cp["es"]
    pk = cp["pk"]
    s_wld, s_pe, s_ve, s_se, s_cc, s_st = (
        cp[k] for k in ["s_wld", "s_pe", "s_ve", "s_se", "s_cc", "s_st"])

    es.wait("gpsimd", s_cc)
    fin = inc(s_wld, 16)
    es.gpsimd(lambda gp, fin=fin: fin(gp.dma_start(
        out=cp["m_adsq"][:, :],
        in_=g["adsum_out"].rearrange("(q h) one -> h (q one)", h=MH),
        allow_slow_non_contiguous=True)))
    for qq in range(4):
        fin = inc(s_wld, 16)
        es.gpsimd(lambda gp, qq=qq, fin=fin: fin(gp.dma_start(
            out=cp["m_stq"][:, qq * MH * MHD:(qq + 1) * MH * MHD].rearrange(
                "n (h p) -> n h p", h=MH),
            in_=g["stq_out"][qq * INTER:(qq + 1) * INTER, :].rearrange(
                "(h n) p -> n h p", n=128))))
    es.wait("vector", s_wld)
    for m in range(4):
        fin = inc(s_ve)
        es.vector(lambda v, m=m, fin=fin:
                  fin(v.tensor_scalar(cp["m_wexp"][:, m:m + 1],
                                      cp["m_adsq"][:, 0:1],
                                      cp["m_scmask"][:, 4 * m:4 * m + 1],
                                      None, ALU.mult)))
        for k in range(1, 4):
            es.wait("vector", s_ve)
            fin = inc(s_ve)
            es.vector(lambda v, m=m, k=k, fin=fin:
                      fin(v.scalar_tensor_tensor(
                          cp["m_wexp"][:, m:m + 1], cp["m_adsq"][:, k:k + 1],
                          cp["m_scmask"][:, 4 * m + k:4 * m + k + 1],
                          cp["m_wexp"][:, m:m + 1], ALU.mult, ALU.add)))
        es.wait("vector", s_ve)
    fin = inc(s_se)
    es.scalar(lambda sc, fin=fin:
              fin(sc.activation(cp["m_wexp"][:, :], cp["m_wexp"][:, :], AF.Exp)))
    es.wait("vector", s_se)
    fin = inc(s_ve)
    es.vector(lambda v, fin=fin:
              fin(v.tensor_tensor(cp["m_wexp"][:, :], cp["m_wexp"][:, :],
                                  cp["m_qmask"][:, :], ALU.mult)))
    es.wait("sync", s_ve)
    fin = inc(s_st, 16)
    es.sync(lambda sy, fin=fin: fin(sy.dma_start(
        out=g["wbase_d"].rearrange("one (h q) -> (one h) q", h=MH),
        in_=cp["m_wexp"][:, :])))
    es.wait("gpsimd", s_st)
    fin = inc(s_wld, 16)
    es.gpsimd(lambda gp, fin=fin:
              fin(gp.dma_start(out=cp["m_wrow"][:, :], in_=g["wbase_d"])))
    es.wait("tensor", s_wld)
    es.wait("tensor", s_ve)
    fin = inc(s_pe)
    es.tensor(lambda t, fin=fin:
              fin(t.matmul(pk[4][:, 0:4 * MH], cp["ones_row_f"][:, :],
                           cp["m_wrow"][0:1, :], start=True, stop=True)))
    es.wait("vector", s_pe)
    fin = inc(s_ve)
    es.vector(lambda v, fin=fin:
              fin(v.tensor_copy(cp["m_wbc"][:, :], pk[4][:, 0:4 * MH])))
    es.wait("vector", s_ve)
    def partials(dst, m, accumulate):
        fin = inc(s_ve)
        def body(v, dst=dst, m=m, accumulate=accumulate, fin=fin):
            ins = None
            for h in range(MH):
                if accumulate:
                    ins = v.scalar_tensor_tensor(
                        cp[dst][:, h * MHD:(h + 1) * MHD],
                        cp["m_stq"][:, (m * MH + h) * MHD:
                                    (m * MH + h + 1) * MHD],
                        cp["m_wbc"][:, h * 4 + m:h * 4 + m + 1],
                        cp[dst][:, h * MHD:(h + 1) * MHD], ALU.mult, ALU.add)
                else:
                    ins = v.tensor_scalar(
                        cp[dst][:, h * MHD:(h + 1) * MHD],
                        cp["m_stq"][:, (m * MH + h) * MHD:
                                    (m * MH + h + 1) * MHD],
                        cp["m_wbc"][:, h * 4 + m:h * 4 + m + 1], None, ALU.mult)
            fin(ins)
        es.vector(body)

    partials("m_srun", 0, False)
    partials("m_prev", 1, False)
    es.wait("vector", s_ve)
    partials("m_srun", 2, True)
    partials("m_prev", 3, True)
    es.wait("vector", s_ve)
    fin = inc(s_ve)
    es.vector(lambda v, fin=fin:
              fin(v.tensor_tensor(cp["m_srun"][:, :], cp["m_srun"][:, :],
                                  cp["m_prev"][:, :], ALU.add)))
    es.wait("vector", s_ve)

    es.wait("sync", s_ve)
    fin = inc(s_st, 16)
    es.sync(lambda sy, fin=fin: fin(sy.dma_start(
        out=g["pvrow_d"].rearrange("one (h q) -> (one h) q", h=MH),
        in_=cp["m_pmask"][:, :])))
    es.wait("gpsimd", s_st)
    fin = inc(s_wld, 16)
    es.gpsimd(lambda gp, fin=fin:
              fin(gp.dma_start(out=cp["m_wrow"][:, :], in_=g["pvrow_d"])))
    for qq in range(4):
        fin = inc(s_wld, 16)
        es.gpsimd(lambda gp, qq=qq, fin=fin: fin(gp.dma_start(
            out=cp["m_stq"][:, qq * MH * MHD:(qq + 1) * MH * MHD].rearrange(
                "n (h p) -> n h p", h=MH),
            in_=g["lastloc_out"][qq * INTER:(qq + 1) * INTER, :].rearrange(
                "(h n) p -> n h p", n=128))))
    es.wait("tensor", s_wld)
    es.wait("tensor", s_ve)
    fin = inc(s_pe)
    es.tensor(lambda t, fin=fin:
              fin(t.matmul(pk[4][:, 0:4 * MH], cp["ones_row_f"][:, :],
                           cp["m_wrow"][0:1, :], start=True, stop=True)))
    es.wait("vector", s_pe)
    fin = inc(s_ve)
    es.vector(lambda v, fin=fin:
              fin(v.tensor_copy(cp["m_wbc"][:, :], pk[4][:, 0:4 * MH])))
    es.wait("vector", s_ve)
    for m in range(4):
        if m == 0:
            fin = inc(s_ve)
            def p0(v, fin=fin):
                ins = None
                for h in range(MH):
                    ins = v.tensor_scalar(
                        cp["m_prev"][:, h * MHD:(h + 1) * MHD],
                        cp["m_stq"][:, (0 * MH + h) * MHD:
                                    (0 * MH + h + 1) * MHD],
                        cp["m_wbc"][:, h * 4 + 0:h * 4 + 1], None, ALU.mult)
                fin(ins)
            es.vector(p0)
        else:
            es.wait("vector", s_ve)
            fin = inc(s_ve)
            def pacc(v, m=m, fin=fin):
                ins = None
                for h in range(MH):
                    ins = v.scalar_tensor_tensor(
                        cp["m_prev"][:, h * MHD:(h + 1) * MHD],
                        cp["m_stq"][:, (m * MH + h) * MHD:
                                    (m * MH + h + 1) * MHD],
                        cp["m_wbc"][:, h * 4 + m:h * 4 + m + 1],
                        cp["m_prev"][:, h * MHD:(h + 1) * MHD],
                        ALU.mult, ALU.add)
                fin(ins)
            es.vector(pacc)
    es.wait("vector", s_ve)


def _state_backward(nc, cp):
    g, es = cp["g"], cp["es"]
    s_wld, s_ve, s_st = cp["s_wld"], cp["s_ve"], cp["s_st"]

    def load_sloc(c, dst):
        es.wait("gpsimd", s_ve)
        fin = inc(s_wld, 16)
        es.gpsimd(lambda gp, c=c, dst=dst, fin=fin: fin(gp.dma_start(
            out=cp[dst][:].rearrange("n (h p) -> n h p", h=MH),
            in_=g["sloc_d"][c * INTER:(c + 1) * INTER, :].rearrange(
                "(h n) p -> n h p", n=128))))

    def store_sin(c):
        es.wait("sync", s_ve)
        fin = inc(s_st, 16)
        es.sync(lambda sy, c=c, fin=fin: fin(sy.dma_start(
            out=g["sin_d"][c * INTER:(c + 1) * INTER, :].rearrange(
                "(h n) p -> n h p", n=128),
            in_=cp["m_sstage"][:, :].rearrange("n (h p) -> n h p", h=MH))))

    bufs = ["m_slocc", "m_slocp"]
    load_sloc(7, bufs[7 % 2])
    for c in range(7, -1, -1):
        loc = bufs[c % 2]
        es.wait("vector", s_wld)
        es.wait("vector", s_ve)
        if c < 7:
            fin = inc(s_ve)
            es.vector(lambda v, loc=loc, fin=fin:
                      fin(v.tensor_tensor(cp["m_sstage"][:, :], cp[loc][:, :],
                                          cp["m_srun"][:, :], ALU.add)))
            store_sin(c + 1)
        es.wait("vector", s_ve)
        fin = inc(s_ve)
        es.vector(lambda v, loc=loc, fin=fin:
                  fin(v.tensor_tensor(cp["m_srun"][:, :], cp[loc][:, :],
                                      cp["m_srun"][:, :], ALU.add)))
        es.wait("vector", s_ve)
        fin = inc(s_ve)
        def scale_h(v, c=c, fin=fin):
            ins = None
            for h in range(MH):
                ins = v.tensor_scalar(
                    cp["m_srun"][:, h * MHD:(h + 1) * MHD],
                    cp["m_srun"][:, h * MHD:(h + 1) * MHD],
                    cp["m_decbc"][:, h * NCHK + c:h * NCHK + c + 1],
                    None, ALU.mult)
            fin(ins)
        es.vector(scale_h)
        if c > 0:
            load_sloc(c - 1, bufs[(c - 1) % 2])
    es.wait("vector", s_ve)
    fin = inc(s_ve)
    es.vector(lambda v, fin=fin:
              fin(v.tensor_tensor(cp["m_sstage"][:, :], cp["m_prev"][:, :],
                                  cp["m_srun"][:, :], ALU.add)))
    store_sin(0)
    if "tap_sin" in DEBUG_TAPS:
        es.wait("sync", s_st)
        fin = inc(s_st, 16)
        es.sync(lambda sy, fin=fin:
                fin(sy.dma_start(out=g["tap_sin"], in_=g["sin_d"])))


def _ssd_main(nc, cp):
    g, es = cp["g"], cp["es"]
    pk, pb = cp["pk"], cp["pb"]
    s_wld, s_xld, s_pe, s_ve, s_se, s_ge, s_st = (
        cp[k] for k in ["s_wld", "s_xld", "s_pe", "s_ve", "s_se", "s_ge", "s_st"])
    hbc = cp["m_hbc"]
    dt_sb = cp["m_dt"]
    acum = cp["m_acum"]

    NBLK = TQ // 1024
    for blk_i in range(NBLK):
        for ci in range(4):
            c = blk_i * 4 + ci
            es.wait("gpsimd", s_pe)
            es.wait("gpsimd", s_ve)
            es.wait("gpsimd", s_se)
            fin = inc(s_wld, 16)
            es.gpsimd(lambda gp, c=c, fin=fin: fin(gp.dma_start(
                out=hbc[:].rearrange("p (cc t) -> p cc t", cc=HBC_MT),
                in_=g["hbc_d"][:, c * MCH:(c + 1) * MCH].rearrange(
                    "(cc p) t -> p cc t", p=128))))
            fin = inc(s_wld, 16)
            es.gpsimd(lambda gp, c=c, fin=fin: fin(gp.dma_start(
                out=cp["m_acrow"][:].rearrange("one (h t) -> one h t", h=MH // 2),
                in_=g["acum_d"].rearrange("(one h) t -> one h t", one=1)[
                    0:1, 0:MH // 2, c * MCH:(c + 1) * MCH])))
            es.wait("gpsimd", s_ve)
            fin = inc(s_wld, 16)
            es.gpsimd(lambda gp, c=c, fin=fin: fin(gp.dma_start(
                out=cp["m_sinc"][:].rearrange("n (h p) -> n h p", h=MH),
                in_=g["sin_d"][c * INTER:(c + 1) * INTER, :].rearrange(
                    "(h n) p -> n h p", n=128))))
            fin = inc(s_wld, 16)
            es.gpsimd(lambda gp, c=c, fin=fin: fin(gp.dma_start(
                out=cp["m_gatec"][:, :].rearrange("p (hh i) -> p hh i", hh=2),
                in_=g["gate_d"][c * MCH:(c + 1) * MCH, :].rearrange(
                    "(hh p) i -> p hh i", p=128))))

            es.wait("tensor", s_wld)
            es.wait("tensor", s_ve)
            fin = inc(s_pe)
            es.tensor(lambda t, c=c, fin=fin: (
                t.transpose(pk[4][0:128, 0:MH], dt_sb[:, c * MCH: c * MCH + 128],
                            cp["ident_f"][0:MH, 0:MH]),
                t.transpose(pk[5][0:128, 0:MH],
                            dt_sb[:, c * MCH + 128: c * MCH + 256],
                            cp["ident_f"][0:MH, 0:MH]),
                t.transpose(pk[4][0:128, MH:2 * MH],
                            acum[:, c * MCH: c * MCH + 128],
                            cp["ident_f"][0:MH, 0:MH]),
                fin(t.transpose(pk[5][0:128, MH:2 * MH],
                                acum[:, c * MCH + 128: c * MCH + 256],
                                cp["ident_f"][0:MH, 0:MH]))))
            es.wait("vector", s_pe)
            fin = inc(s_ve)
            es.vector(lambda v, fin=fin: (
                v.tensor_copy(cp["m_dtT"][:, 0:MH], pk[4][:, 0:MH]),
                v.tensor_copy(cp["m_dtT"][:, MH:2 * MH], pk[5][:, 0:MH]),
                v.tensor_copy(cp["m_acT"][:, 0:MH], pk[4][:, MH:2 * MH]),
                fin(v.tensor_copy(cp["m_acT"][:, MH:2 * MH], pk[5][:, MH:2 * MH]))))
            fin = inc(s_pe)
            es.tensor(lambda t, fin=fin: (
                t.transpose(pb[:, 0:128], hbc[:, 32 * MCH:32 * MCH + 128],
                            cp["ident_bf"][:, :]),
                fin(t.transpose(pb[:, 128:256], hbc[:, 32 * MCH + 128:
                                                    32 * MCH + 256],
                                cp["ident_bf"][:, :]))))
            es.wait("vector", s_pe)
            fin = inc(s_ve)
            es.vector(lambda v, fin=fin: (
                v.tensor_copy(cp["m_bT"][:, 0:128], pb[:, 0:128]),
                fin(v.tensor_copy(cp["m_bT"][:, 128:256], pb[:, 128:256]))))
            fin = inc(s_pe)
            es.tensor(lambda t, fin=fin: (
                t.matmul(pk[2][:, 0:MCH], hbc[:, 32 * MCH:32 * MCH + 128],
                         hbc[:, 33 * MCH:34 * MCH], start=True, stop=True),
                fin(t.matmul(pk[3][:, 0:MCH], hbc[:, 32 * MCH + 128:32 * MCH + 256],
                             hbc[:, 33 * MCH:34 * MCH], start=True, stop=True))))
            es.wait("vector", s_pe)
            fin = inc(s_ve)
            es.vector(lambda v, fin=fin: (
                v.tensor_tensor(cp["m_gm"][:, 0:MCH], pk[2][:, 0:MCH],
                                cp["ssd_mask"][:, 0:MCH], ALU.mult),
                fin(v.tensor_tensor(cp["m_gm"][:, MCH:2 * MCH], pk[3][:, 0:MCH],
                                    cp["ssd_mask"][:, MCH:2 * MCH], ALU.mult))))

            for h in range(MH):
                if h == MH // 2:
                    es.wait("gpsimd", s_pe)
                    fin = inc(s_wld, 16)
                    es.gpsimd(lambda gp, c=c, fin=fin: fin(gp.dma_start(
                        out=cp["m_acrow"][:].rearrange(
                            "one (h t) -> one h t", h=MH // 2),
                        in_=g["acum_d"].rearrange(
                            "(one h) t -> one h t", one=1)[
                            0:1, MH // 2:MH, c * MCH:(c + 1) * MCH])))
                    es.wait("tensor", s_wld)
                hh = h % (MH // 2)
                es.wait("tensor", s_ve)
                es.wait("tensor", s_se)
                fin = inc(s_pe)
                es.tensor(lambda t, h=h, hh=hh, fin=fin: (
                    t.transpose(pb[:, 256:384], hbc[:, h * MCH:h * MCH + 128],
                                cp["ident_bf"][:, :]),
                    t.transpose(pb[:, 384:512], hbc[:, h * MCH + 128:
                                                    h * MCH + 256],
                                cp["ident_bf"][:, :]),
                    fin(t.matmul(pk[4][:, 0:MCH], cp["ones_row_f"][:, :],
                                 cp["m_acrow"][0:1, hh * MCH:(hh + 1) * MCH],
                                 start=True, stop=True))))
                es.wait("vector", s_pe)
                fin = inc(s_ve)
                es.vector(lambda v, h=h, fin=fin: (
                    v.tensor_copy(cp["m_xrawT"][:, 0:MHD], pb[:, 256:384]),
                    v.tensor_copy(cp["m_xrawT"][:, MHD:2 * MHD], pb[:, 384:512]),
                    v.tensor_scalar(cp["m_xdtT"][:, 0:MHD], pb[:, 256:384],
                                    cp["m_dtT"][:, h:h + 1], None, ALU.mult),
                    v.tensor_scalar(cp["m_xdtT"][:, MHD:2 * MHD], pb[:, 384:512],
                                    cp["m_dtT"][:, MH + h:MH + h + 1],
                                    None, ALU.mult),
                    v.tensor_scalar(cp["m_eh"][:, 0:MCH], pk[4][:, 0:MCH],
                                    cp["m_acT"][:, h:h + 1], 0.0,
                                    ALU.subtract, ALU.min),
                    fin(v.tensor_scalar(cp["m_eh"][:, MCH:2 * MCH],
                                        pk[4][:, 0:MCH],
                                        cp["m_acT"][:, MH + h:MH + h + 1], 0.0,
                                        ALU.subtract, ALU.min))))
                es.wait("scalar", s_ve)
                fin = inc(s_se)
                es.scalar(lambda sc, h=h, fin=fin: (
                    sc.activation(cp["m_eh"][:, 0:2 * MCH], cp["m_eh"][:, 0:2 * MCH],
                                  AF.Exp),
                    fin(sc.activation(cp["m_sdbc"][:, 0:MCH], pk[4][:, 0:MCH],
                                      AF.Exp))))
                es.wait("vector", s_se)
                fin = inc(s_ve)
                es.vector(lambda v, h=h, fin=fin: (
                    v.tensor_tensor(cp["m_mh"][:, 0:2 * MCH], cp["m_gm"][:, 0:2 * MCH],
                                    cp["m_eh"][:, 0:2 * MCH], ALU.mult),
                    fin(v.tensor_tensor(cp["m_csc"][:, 0:MCH],
                                        hbc[:, 33 * MCH:34 * MCH],
                                        cp["m_sdbc"][:, 0:MCH], ALU.mult))))
                es.wait("tensor", s_ve)
                fin = inc(s_pe)
                def ymm(t, h=h, fin=fin):
                    ins = None
                    for lh in range(2):
                        bank = pk[lh]
                        t.matmul(bank[:, 0:MHD],
                                 cp["m_mh"][:, lh * 128:lh * 128 + 128],
                                 cp["m_xdtT"][:, 0:MHD], start=True, stop=False)
                        t.matmul(bank[:, 0:MHD],
                                 cp["m_mh"][:, MCH + lh * 128:MCH + lh * 128 + 128],
                                 cp["m_xdtT"][:, MHD:2 * MHD],
                                 start=False, stop=False)
                        ins = t.matmul(bank[:, 0:MHD],
                                       cp["m_csc"][:, lh * 128:lh * 128 + 128],
                                       cp["m_sinc"][:, h * MHD:(h + 1) * MHD],
                                       start=False, stop=True)
                    fin(ins)
                es.tensor(ymm)
                es.wait("vector", s_pe)
                fin = inc(s_ve)
                es.vector(lambda v, h=h, fin=fin: (
                    v.scalar_tensor_tensor(
                        cp["m_ych"][:, 0 * INTER + h * MHD:0 * INTER + (h + 1) * MHD],
                        cp["m_xrawT"][:, 0:MHD], cp["D_bc"][:, h:h + 1],
                        pk[0][:, 0:MHD], ALU.mult, ALU.add),
                    fin(v.scalar_tensor_tensor(
                        cp["m_ych"][:, 1 * INTER + h * MHD:1 * INTER + (h + 1) * MHD],
                        cp["m_xrawT"][:, MHD:2 * MHD], cp["D_bc"][:, h:h + 1],
                        pk[1][:, 0:MHD], ALU.mult, ALU.add))))
                es.wait("vector", s_ve)
                fin = inc(s_ve)
                es.vector(lambda v, h=h, fin=fin: (
                    v.tensor_tensor(
                        cp["m_ych"][:, 0 * INTER + h * MHD:0 * INTER + (h + 1) * MHD],
                        cp["m_ych"][:, 0 * INTER + h * MHD:0 * INTER + (h + 1) * MHD],
                        cp["m_gatec"][:, 0 * INTER + h * MHD:
                                      0 * INTER + (h + 1) * MHD], ALU.mult),
                    fin(v.tensor_tensor(
                        cp["m_ych"][:, 1 * INTER + h * MHD:1 * INTER + (h + 1) * MHD],
                        cp["m_ych"][:, 1 * INTER + h * MHD:1 * INTER + (h + 1) * MHD],
                        cp["m_gatec"][:, 1 * INTER + h * MHD:
                                      1 * INTER + (h + 1) * MHD], ALU.mult))))

            if "tap_yg" in DEBUG_TAPS:
                es.wait("sync", s_ve)
                for lh in range(2):
                    fin = inc(s_st, 16)
                    es.sync(lambda sy, c=c, lh=lh, fin=fin: fin(sy.dma_start(
                        out=g["tap_yg"][c * MCH + lh * 128:
                                        c * MCH + lh * 128 + 128, :],
                        in_=cp["m_ych"][:, lh * INTER:(lh + 1) * INTER])))
                es.wait("vector", s_st)
            for lh in range(2):
                es.wait("scalar", s_ve)
                fin = inc(s_se)
                es.scalar(lambda sc, lh=lh, fin=fin:
                          fin(sc.activation(cp["m_gatec"][:, lh * INTER:
                                                          (lh + 1) * INTER],
                                            cp["m_ych"][:, lh * INTER:
                                                        (lh + 1) * INTER],
                                            AF.Square)))
                es.wait("vector", s_se)
                fin = inc(s_ve)
                es.vector(lambda v, lh=lh, fin=fin:
                          fin(v.reduce_sum(cp["m_rsum"][:, :],
                                           cp["m_gatec"][:, lh * INTER:
                                                         (lh + 1) * INTER],
                                           axis=AX.X)))
                es.wait("scalar", s_ve)
                fin = inc(s_se)
                es.scalar(lambda sc, fin=fin:
                          fin(sc.activation(cp["m_rsum"][:, :], cp["m_rsum"][:, :],
                                            AF.Sqrt, scale=1.0 / INTER, bias=1e-5)))
                es.wait("vector", s_se)
                fin = inc(s_ve)
                es.vector(lambda v, fin=fin:
                          fin(v.reciprocal(cp["m_rstd"][:, :], cp["m_rsum"][:, :])))
                es.wait("vector", s_ve)
                fin = inc(s_ve)
                es.vector(lambda v, lh=lh, fin=fin:
                          fin(v.tensor_scalar(cp["m_ych"][:, lh * INTER:
                                                          (lh + 1) * INTER],
                                              cp["m_ych"][:, lh * INTER:
                                                          (lh + 1) * INTER],
                                              cp["m_rstd"][:, 0:1], None, ALU.mult)))
                es.wait("vector", s_ve)
                fin = inc(s_ve)
                es.vector(lambda v, lh=lh, fin=fin:
                          fin(v.tensor_tensor(cp["m_ych"][:, lh * INTER:
                                                          (lh + 1) * INTER],
                                              cp["m_ych"][:, lh * INTER:
                                                          (lh + 1) * INTER],
                                              cp["normw_bc"][:, :], ALU.mult)))
                tcol = ci * MCH + lh * 128
                for h4 in range(0, INTER // 128, 4):
                    es.wait("tensor", s_ve)
                    es.wait("tensor", s_se)
                    fin = inc(s_pe)
                    def tr4(t, lh=lh, h4=h4, fin=fin):
                        ins = None
                        for j in range(4):
                            ins = t.transpose(
                                pb[:, j * 128:(j + 1) * 128],
                                cp["m_ych"][:, lh * INTER + (h4 + j) * 128:
                                            lh * INTER + (h4 + j + 1) * 128],
                                cp["ident_bf"][:, :])
                        fin(ins)
                    es.tensor(tr4)
                    es.wait("scalar", s_pe)
                    fin = inc(s_se)
                    def ev4(sc, h4=h4, tcol=tcol, fin=fin):
                        ins = None
                        for j in range(4):
                            ins = sc.copy(
                                cp["m_ynf"][:, (h4 + j) * 1024 + tcol:
                                            (h4 + j) * 1024 + tcol + 128],
                                pb[:, j * 128:(j + 1) * 128])
                        fin(ins)
                    es.scalar(ev4)

        pe_after_mt = []
        for mt in range(FC):
            buf = mt % 2
            if mt >= 2:
                es.wait("gpsimd", s_pe, pe_after_mt[mt - 1])
            es.wait("gpsimd", s_ve)
            es.wait("gpsimd", s_se)
            fin = inc(s_wld, 16)
            es.gpsimd(lambda gp, mt=mt, buf=buf, fin=fin: fin(gp.dma_start(
                out=cp["m_ych"][:, buf * 4096:(buf + 1) * 4096].rearrange(
                    "p (kc m) -> p kc m", kc=INTER // 128),
                in_=g["w_outproj_T"][:, mt * 128:(mt + 1) * 128].rearrange(
                    "(kc p) m -> p kc m", p=128))))
            for n2 in range(2):
                tok0 = blk_i * 1024 + n2 * 512
                es.wait("sync", s_ve)
                fin = inc(s_xld, 16)
                es.sync(lambda sy, mt=mt, tok0=tok0, fin=fin: fin(sy.dma_start(
                    out=cp["m_eh"][:, 0:512],
                    in_=g["x_f"][mt * 128:(mt + 1) * 128,
                                 XOFF + tok0:XOFF + tok0 + 512])))
                es.wait("tensor", s_wld)
                es.wait("tensor", s_ve)
                es.wait("tensor", s_se)
                fin = inc(s_pe)
                def opmm(t, mt=mt, n2=n2, buf=buf, blk_i=blk_i, fin=fin):
                    bank = pk[2 + n2]
                    ins = None
                    for kc in range(INTER // 128):
                        ins = t.matmul(
                            bank[:, 0:512],
                            cp["m_ych"][:, buf * 4096 + kc * 128:
                                        buf * 4096 + (kc + 1) * 128],
                            cp["m_ynf"][:, kc * 1024 + n2 * 512:
                                        kc * 1024 + n2 * 512 + 512],
                            start=(kc == 0), stop=(kc == INTER // 128 - 1))
                    fin(ins)
                es.tensor(opmm)
                es.wait("vector", s_pe)
                es.wait("vector", s_xld)
                fin = inc(s_ve)
                es.vector(lambda v, n2=n2, fin=fin:
                          fin(v.tensor_tensor(cp["m_eh"][:, 0:512],
                                              cp["m_eh"][:, 0:512],
                                              pk[2 + n2][:, 0:512], ALU.add)))
                es.wait("sync", s_ve)
                fin = inc(s_st, 16)
                es.sync(lambda sy, mt=mt, tok0=tok0, fin=fin: fin(sy.dma_start(
                    out=g["x_res"][mt * 128:(mt + 1) * 128, tok0:tok0 + 512],
                    in_=cp["m_eh"][:, 0:512])))
                es.wait("vector", s_st)
            pe_after_mt.append(s_pe.n)


def stage_attn_block(nc, ctx, cp, kind, call):
    g, es, nc_ = cp["g"], cp["es"], nc
    pk, pb = cp["pk"], cp["pb"]
    s_wld, s_xld, s_pe, s_ve, s_se, s_ge, s_cc, s_st = (
        cp[k] for k in ["s_wld", "s_xld", "s_pe", "s_ve", "s_se", "s_ge",
                        "s_cc", "s_st"])
    tag = f"{kind}{call}"
    ffn = kind == "ffn"

    with ExitStack() as sctx:
        def SSB(name, shape, dtype=F32):
            t = sctx.enter_context(nc.sbuf_tensor(f"sb_{tag}_{name}",
                                                  list(shape), dtype))
            cp["a_" + name] = t
            return t

        if ffn:
            SSB("m0", [1, TQ], BF16); SSB("m1", [1, TQ], BF16)
            SSB("m2", [1, TQ], BF16)
        cc_chunk = [0, 0, 0, 0]

        if ffn:
            SSB("ln1s", [128, FC]); SSB("ln1b", [128, FC])
            SSB("ln2s", [128, FC]); SSB("ln2b", [128, FC])
            SSB("sbc", [128, FC])
            for nm, srcn in [("ln1s", "ln1_scale_c"), ("ln1b", "ln1_bias_c"),
                             ("ln2s", "ln2_scale_c"), ("ln2b", "ln2_bias_c"),
                             ("sbc", "simple_bias_c")]:
                fin = inc(s_wld, 16)
                es.gpsimd(lambda gp, nm=nm, srcn=srcn, fin=fin:
                          fin(gp.dma_start(out=cp["a_" + nm][:], in_=g[srcn])))

        with ExitStack() as c0:
            lnx = c0.enter_context(
                nc.sbuf_tensor(f"sb_{tag}_lnx", [128, FC * TQ], BF16))
            cp["lnsq"] = c0.enter_context(
                nc.sbuf_tensor(f"sb_{tag}_lnsq", [128, FC * TQ], BF16))
            if ffn:
                for nm, shape, dt_ in [("wr", [128, FC * 3], BF16),
                                       ("wdn", [128, FC * 32], BF16),
                                       ("wup", [32, D], BF16),
                                       ("pst0", [128, 512], BF16),
                                       ("pst1", [128, 512], BF16),
                                       ("z", [32, TQ], BF16),
                                       ("l0", [1, TQ], F32),
                                       ("l1", [1, TQ], F32),
                                       ("l2", [1, TQ], F32)]:
                    cp["a_" + nm] = c0.enter_context(nc.sbuf_tensor(
                        f"sb_{tag}0_{nm}", list(shape), dt_))
            es.wait("gpsimd", s_st)
            fin = inc(s_wld, 16)
            es.gpsimd(lambda gp, fin=fin: fin(gp.dma_start(
                out=lnx[:].rearrange("p (cc t) -> p cc t", cc=FC),
                in_=g["x_res"].rearrange("(cc p) t -> p cc t", p=128))))
            es.flush(f"{tag}_lnload")
            emit_ln(nc, cp, lnx, TQ, lnx, cp["lnsq"])
            if ffn:
                es.wait("sync", s_ve)
                fin = inc(s_st, 16)
                es.sync(lambda sy, fin=fin: fin(sy.dma_start(
                    out=g["xn_loc"].rearrange("(cc p) t -> p cc t", p=128),
                    in_=lnx[:].rearrange("p (cc t) -> p cc t", cc=FC))))
                _router_lowrank(nc, cp, lnx)
                emit_ln(nc, cp, lnx, TQ, lnx, cp["lnsq"],
                        gamma=cp["a_ln1s"], beta=cp["a_ln1b"])
            es.wait("sync", s_ve)
            for tch in range(4):
                fin = inc(s_st, 16)
                es.sync(lambda sy, tch=tch, fin=fin: fin(sy.dma_start(
                    out=g[f"xn_ag_{tch}"].rearrange("(cc p) t -> p cc t", p=128),
                    in_=lnx[:].rearrange("p (cc t) -> p cc t", cc=FC)[
                        :, :, tch * 512:(tch + 1) * 512])))
            es.wait("gpsimd", s_st)
            for tch in range(4):
                fin = inc(s_cc)
                es.gpsimd(lambda gp, tch=tch, fin=fin: fin(gp.collective_compute(
                    "AllGather", ALU.bypass, replica_groups=GROUPS4,
                    ins=[g[f"xn_ag_{tch}"]], outs=[g[f"ag_out_{tch}"]])))
                cc_chunk[tch] = s_cc.n
            es.flush(f"{tag}_ln_ag")

        with ExitStack() as c2:
            def S2(name, shape, dtype=F32):
                t = c2.enter_context(nc.sbuf_tensor(f"sb_{tag}2_{name}",
                                                    list(shape), dtype))
                cp["a_" + name] = t
                return t
            S2("wslab", [128, FC * 512], BF16)
            S2("agb0", [128, FC * 512], BF16)
            S2("agb1", [128, FC * 512], BF16)
            S2("pstage0", [128, 512], BF16)
            S2("pstage1", [128, 512], BF16)
            if not ffn:
                S2("bqv", [1, 512], BF16)
                S2("bqkv", [128, 12])
                fin = inc(s_wld, 16)
                es.gpsimd(lambda gp, fin=fin:
                          fin(gp.dma_start(out=cp["a_bqv"][:], in_=g["bqv_row"])))
                fin = inc(s_wld, 16)
                es.gpsimd(lambda gp, fin=fin:
                          fin(gp.dma_start(out=cp["a_bqkv"][:], in_=g["bqkv_hg"])))
            _qkv_proj(nc, cp, kind, cc_chunk)
            es.flush(f"{tag}_qkv")

        with ExitStack() as c3:
            def S3(name, shape, dtype=F32):
                t = c3.enter_context(nc.sbuf_tensor(f"sb_{tag}3_{name}",
                                                    list(shape), dtype))
                cp["a_" + name] = t
                return t
            S3("mask01", [128, 8 * 1024], BF16)
            fin = inc(s_wld, 16)
            es.gpsimd(lambda gp, fin=fin:
                      fin(gp.dma_start(out=cp["a_mask01"][:], in_=g["mask01"])))
            for b in range(2):
                S3(f"kh{b}", [128, S], BF16)
                S3(f"vh{b}", [128, S], BF16)
                S3(f"qh{b}", [128, S], BF16)
            S3("pt0", [128, 1024], BF16)
            S3("pt1", [128, 1024], BF16)
            S3("accf", [128, 1024])
            S3("osb", [128, 1024])
            S3("ctxs", [128, 1024], BF16)
            _flash(nc, cp)
            es.flush(f"{tag}_flash")

        with ExitStack() as c4:
            def S4(name, shape, dtype=F32):
                t = c4.enter_context(nc.sbuf_tensor(f"sb_{tag}4_{name}",
                                                    list(shape), dtype))
                cp["a_" + name] = t
                return t
            S4("woslab", [128, 4 * D], BF16)
            S4("ctx4", [128, 4 * 512], BF16)
            S4("dstage0", [128, 512], BF16)
            S4("dstage1", [128, 512], BF16)
            _wo_partial(nc, cp, kind)
            fin = inc(s_cc)
            es.wait("gpsimd", s_st)
            es.gpsimd(lambda gp, fin=fin: fin(gp.collective_compute(
                "ReduceScatter", ALU.add, replica_groups=GROUPS4,
                ins=[g["rs_in2"]], outs=[g["rs_out2"]])))
            es.flush(f"{tag}_wo")

        if not ffn:
            _attnb_epilogue(nc, cp)
            es.flush(f"{tag}_epi")
        else:
            with ExitStack() as c5:
                lnx2 = c5.enter_context(
                    nc.sbuf_tensor(f"sb_{tag}5_lnx", [128, FC * TQ], BF16))
                with ExitStack() as c5b:
                    cp["lnsq"] = c5b.enter_context(
                        nc.sbuf_tensor(f"sb_{tag}5_lnsq", [128, FC * TQ], BF16))
                    _ffn_xtl(nc, cp, lnx2)
                    emit_ln(nc, cp, lnx2, TQ, lnx2, cp["lnsq"],
                            gamma=cp["a_ln2s"], beta=cp["a_ln2b"])
                    es.flush(f"{tag}_xtl")
                def S5(name, shape, dtype=F32):
                    t = c5.enter_context(nc.sbuf_tensor(f"sb_{tag}5_{name}",
                                                        list(shape), dtype))
                    cp["a_" + name] = t
                    return t
                S5("hsb", [128, (FF // 128) * 512], BF16)
                S5("fslab0", [128, FC * 128], BF16)
                S5("fslab1", [128, FC * 128], BF16)
                S5("f2slab", [128, (FF // 128) * 128], BF16)
                S5("cmb0", [128, 512], BF16)
                S5("cmb1", [128, 512], BF16)
                S5("cmb2", [128, 512], BF16)
                S5("cx", [128, 512])
                S5("ct0", [128, 512], BF16)
                S5("ct1", [128, 512], BF16)
                S5("ct2", [128, 512], BF16)
                S5("ct0b", [128, 512])
                S5("ct1b", [128, 512])
                _ffn_mlp_combine(nc, cp, lnx2, call)
                es.flush(f"{tag}_mlp")


def _router_lowrank(nc, cp, xnbuf):
    g, es = cp["g"], cp["es"]
    pk = cp["pk"]
    s_wld, s_pe, s_ve, s_se, s_st = (
        cp[k] for k in ["s_wld", "s_pe", "s_ve", "s_se", "s_st"])
    z = cp["a_z"]

    for dst, srcn, kd in [(cp["a_wr"], "w_router_T", 3),
                          (cp["a_wdn"], "w_down_T", 32)]:
        fin = inc(s_wld, 16)
        es.gpsimd(lambda gp, dst=dst, srcn=srcn, kd=kd, fin=fin:
                  fin(gp.dma_start(
                      out=dst[:].rearrange("p (kc m) -> p kc m", kc=FC),
                      in_=g[srcn].rearrange("(kc p) m -> p kc m", p=128))))
    fin = inc(s_wld, 16)
    es.gpsimd(lambda gp, fin=fin:
              fin(gp.dma_start(out=cp["a_wup"][:], in_=g["w_up_T"])))

    es.wait("tensor", s_wld)
    for li in range(3):
        for nt in range(4):
            if not (li == 0 and nt == 0):
                es.wait("tensor", s_ve)
            fin = inc(s_pe)
            def lmm(t, li=li, nt=nt, fin=fin):
                ins = None
                for kc in range(FC):
                    ins = t.matmul(pk[4][0:1, 0:512],
                                   cp["a_wr"][:, kc * 3 + li:kc * 3 + li + 1],
                                   xnbuf[:, kc * TQ + nt * 512:
                                         kc * TQ + nt * 512 + 512],
                                   start=(kc == 0), stop=(kc == FC - 1))
                fin(ins)
            es.tensor(lmm)
            es.wait("vector", s_pe)
            fin = inc(s_ve)
            es.vector(lambda v, li=li, nt=nt, fin=fin:
                      fin(v.tensor_copy(cp[f"a_l{li}"][0:1, nt * 512:
                                                       nt * 512 + 512],
                                        pk[4][0:1, 0:512])))
    def vstep(fn):
        es.wait("vector", cp["s_ve"])
        fin = inc(cp["s_ve"])
        es.vector(lambda v, fn=fn, fin=fin: fin(fn(v)))
    vstep(lambda v: v.tensor_tensor(cp["a_m0"][0:1, :], cp["a_l0"][0:1, :],
                                    cp["a_l1"][0:1, :], ALU.is_ge))
    vstep(lambda v: v.tensor_tensor(cp["a_m1"][0:1, :], cp["a_l0"][0:1, :],
                                    cp["a_l2"][0:1, :], ALU.is_ge))
    vstep(lambda v: v.tensor_tensor(cp["a_m2"][0:1, :], cp["a_l1"][0:1, :],
                                    cp["a_l2"][0:1, :], ALU.is_ge))
    vstep(lambda v: v.tensor_tensor(cp["a_l0"][0:1, :], cp["a_m0"][0:1, :],
                                    cp["a_m1"][0:1, :], ALU.mult))
    vstep(lambda v: v.tensor_scalar(cp["a_l1"][0:1, :], cp["a_m0"][0:1, :],
                                    -1.0, 1.0, ALU.mult, ALU.add))
    vstep(lambda v: v.tensor_tensor(cp["a_l1"][0:1, :], cp["a_l1"][0:1, :],
                                    cp["a_m2"][0:1, :], ALU.mult))
    vstep(lambda v: v.tensor_copy(cp["a_m0"][0:1, :], cp["a_l0"][0:1, :]))
    vstep(lambda v: v.tensor_copy(cp["a_m1"][0:1, :], cp["a_l1"][0:1, :]))
    vstep(lambda v: v.tensor_tensor(cp["a_m2"][0:1, :], cp["a_m0"][0:1, :],
                                    cp["a_m1"][0:1, :], ALU.add))
    vstep(lambda v: v.tensor_scalar(cp["a_m2"][0:1, :], cp["a_m2"][0:1, :],
                                    -1.0, 1.0, ALU.mult, ALU.add))
    for nt in range(4):
        if nt > 0:
            es.wait("tensor", s_ve)
        fin = inc(s_pe)
        def zmm(t, nt=nt, fin=fin):
            ins = None
            for kc in range(FC):
                ins = t.matmul(pk[5][0:32, 0:512],
                               cp["a_wdn"][:, kc * 32:(kc + 1) * 32],
                               xnbuf[:, kc * TQ + nt * 512:
                                     kc * TQ + nt * 512 + 512],
                               start=(kc == 0), stop=(kc == FC - 1))
            fin(ins)
        es.tensor(zmm)
        es.wait("vector", s_pe)
        fin = inc(s_ve)
        es.vector(lambda v, nt=nt, fin=fin:
                  fin(v.tensor_copy(z[0:32, nt * 512:nt * 512 + 512],
                                    pk[5][0:32, 0:512])))
    es.wait("tensor", s_ve)
    st_h = {}
    for mt in range(FC):
        for nt in range(4):
            it = mt * 4 + nt
            if it >= 2:
                es.wait("tensor", s_se, st_h.get(("se", it - 2), 0))
            fin = inc(s_pe)
            es.tensor(lambda t, mt=mt, nt=nt, fin=fin:
                      fin(t.matmul(pk[nt % 2][:, 0:512],
                                   cp["a_wup"][0:32, mt * 128:(mt + 1) * 128],
                                   z[0:32, nt * 512:nt * 512 + 512],
                                   start=True, stop=True)))
            es.wait("scalar", s_pe)
            if it >= 2:
                es.wait("scalar", s_st, st_h.get(("st", it - 2), 0))
            fin = inc(s_se)
            es.scalar(lambda sc, nt=nt, fin=fin:
                      fin(sc.copy(cp[f"a_pst{nt % 2}"][:, 0:512],
                                  pk[nt % 2][:, 0:512])))
            st_h[("se", it)] = s_se.n
            es.wait("sync", s_se)
            fin = inc(s_st, 16)
            es.sync(lambda sy, mt=mt, nt=nt, fin=fin: fin(sy.dma_start(
                out=g["y1_d"][mt * 128:(mt + 1) * 128,
                              nt * 512:nt * 512 + 512],
                in_=cp[f"a_pst{nt % 2}"][:, 0:512])))
            st_h[("st", it)] = s_st.n


def _qkv_proj(nc, cp, kind, cc_chunk):
    g, es = cp["g"], cp["es"]
    pk = cp["pk"]
    s_wld, s_pe, s_ve, s_se, s_cc, s_st = (
        cp[k] for k in ["s_wld", "s_pe", "s_ve", "s_se", "s_cc", "s_st"])
    ffn = kind == "ffn"

    def ag_tile(gt, kc, w0, w1):
        r, tch = gt // 4, gt % 4
        return g[f"ag_out_{tch}"][r * D + kc * 128:r * D + kc * 128 + 128, w0:w1]

    def slab_load(srcn, col0):
        es.wait("gpsimd", s_pe)
        fin = inc(s_wld, 16)
        es.gpsimd(lambda gp, srcn=srcn, col0=col0, fin=fin: fin(gp.dma_start(
            out=cp["a_wslab"][:].rearrange("p (kc m) -> p kc m", kc=FC),
            in_=g[srcn][:, col0:col0 + 512].rearrange(
                "(kc p) m -> p kc m", p=128))))

    hist = {}
    itc = [0]
    gidx = [0]
    agpe = {}

    def ag_load(gt, tch):
        i = gidx[0]
        gidx[0] += 1
        es.wait("gpsimd", s_cc, cc_chunk[tch])
        if i - 2 in agpe:
            es.wait("gpsimd", s_pe, agpe[i - 2])
        fin = inc(s_wld, 16)
        es.gpsimd(lambda gp, gt=gt, i=i, fin=fin: fin(gp.dma_start(
            out=cp[f"a_agb{i % 2}"][:].rearrange("p (kc t) -> p kc t", kc=FC),
            in_=g[f"ag_out_{gt % 4}"].rearrange(
                "(r kc p) t -> p r kc t", r=4, p=128)[:, gt // 4, :, :])))
        return i

    def fmaj_part(name, srcn, col0, dst, bias_col):
        slab_load(srcn, col0)
        for tch in range(4):
            for r in range(4):
                gt = r * 4 + tch
                i = ag_load(gt, tch)
                agb = cp[f"a_agb{i % 2}"]
                es.wait("tensor", s_wld)
                for mt in range(4):
                    it = itc[0]
                    if ("se", it - 2) in hist:
                        hv = hist[("se", it - 2)]
                        if isinstance(hv, tuple):
                            es.wait("tensor", s_ve, hv[1])
                        else:
                            es.wait("tensor", s_se, hv)
                    fin = inc(s_pe)
                    def mm(t, mt=mt, it=it, agb=agb, fin=fin):
                        ins = None
                        for kc in range(FC):
                            ins = t.matmul(
                                pk[it % 2][:, 0:512],
                                cp["a_wslab"][:, kc * 512 + mt * 128:
                                              kc * 512 + mt * 128 + 128],
                                agb[:, kc * 512:(kc + 1) * 512],
                                start=(kc == 0), stop=(kc == FC - 1))
                        fin(ins)
                    es.tensor(mm)
                    if bias_col is None:
                        es.wait("scalar", s_pe)
                        if ("st", it - 2) in hist:
                            es.wait("scalar", s_st, hist[("st", it - 2)])
                        fin = inc(s_se)
                        es.scalar(lambda sc, it=it, fin=fin:
                                  fin(sc.copy(cp[f"a_pstage{it % 2}"][:, :],
                                              pk[it % 2][:, 0:512])))
                        hist[("se", it)] = s_se.n
                        es.wait("sync", s_se)
                    else:
                        es.wait("vector", s_pe)
                        if ("st", it - 2) in hist:
                            es.wait("vector", s_st, hist[("st", it - 2)])
                        fin = inc(s_ve)
                        es.vector(lambda v, it=it, mt=mt, fin=fin:
                                  fin(v.tensor_scalar(
                                      cp[f"a_pstage{it % 2}"][:, :],
                                      pk[it % 2][:, 0:512],
                                      g_bias(cp, bias_col, mt), None, ALU.add)))
                        hist[("se", it)] = ("ve", s_ve.n)
                        es.wait("sync", s_ve)
                    fin = inc(s_st, 16)
                    es.sync(lambda sy, gt=gt, mt=mt, it=it, fin=fin:
                            fin(sy.dma_start(
                                out=g[dst][mt * 128:(mt + 1) * 128,
                                           gt * 512:(gt + 1) * 512],
                                in_=cp[f"a_pstage{it % 2}"][:, :])))
                    hist[("st", it)] = s_st.n
                    itc[0] += 1
                agpe[i] = s_pe.n

    def g_bias(cp, qkv_i, mt):
        return cp["a_bqkv"][:, qkv_i * 4 + mt:qkv_i * 4 + mt + 1]

    if ffn:
        fmaj_part("K", "wk_hg", 0, "K_d", None)
        fmaj_part("Q", "wq_hg", 0, "Q_d", None)
        vsrc, vcol = "wv_hg", 0
    else:
        fmaj_part("K", "wqkv_hg", 512, "K_d", 1)
        fmaj_part("Q", "wqkv_hg", 0, "Q_d", 0)
        vsrc, vcol = "wqkv_hg", 1024

    slab_load(vsrc, vcol)
    vbi = [0]
    for vt in range(64):
        gt = vt // 4
        if vt % 4 == 0:
            vbi[0] = ag_load(gt, 3)
            es.wait("tensor", s_wld)
        agb = cp[f"a_agb{vbi[0] % 2}"]
        it = itc[0]
        itc[0] += 1
        if ("se", it - 2) in hist:
            hv = hist[("se", it - 2)]
            if isinstance(hv, tuple):
                es.wait("tensor", s_ve, hv[1])
            else:
                es.wait("tensor", s_se, hv)
        fin = inc(s_pe)
        def vmm(t, vt=vt, it=it, agb=agb, fin=fin):
            sub = vt % 4
            withb = kind == "attnb"
            ins = None
            for kc in range(FC):
                ins = t.matmul(
                    pk[it % 2][:, 0:512],
                    agb[:, kc * 512 + sub * 128:kc * 512 + sub * 128 + 128],
                    cp["a_wslab"][:, kc * 512:(kc + 1) * 512],
                    start=(kc == 0), stop=(kc == FC - 1 and not withb))
            if withb:
                ins = t.matmul(pk[it % 2][:, 0:512], cp["ones_row"][:, 0:128],
                               cp["a_bqv"][:, :], start=False, stop=True)
            fin(ins)
        es.tensor(vmm)
        es.wait("scalar", s_pe)
        if ("st", it - 2) in hist:
            es.wait("scalar", s_st, hist[("st", it - 2)])
        fin = inc(s_se)
        es.scalar(lambda sc, it=it, fin=fin:
                  fin(sc.copy(cp[f"a_pstage{it % 2}"][:, :], pk[it % 2][:, 0:512])))
        hist[("se", it)] = s_se.n
        es.wait("sync", s_se)
        fin = inc(s_st, 16)
        es.sync(lambda sy, vt=vt, it=it, fin=fin: fin(sy.dma_start(
            out=g["V_d"][vt * 128:(vt + 1) * 128, :],
            in_=cp[f"a_pstage{it % 2}"][:, :])))
        hist[("st", it)] = s_st.n
        if vt % 4 == 3:
            agpe[vbi[0]] = s_pe.n


def _flash(nc, cp):
    g, es = cp["g"], cp["es"]
    pk = cp["pk"]
    s_wld, s_xld, s_pe, s_ve, s_se, s_ge, s_st = (
        cp[k] for k in ["s_wld", "s_xld", "s_pe", "s_ve", "s_se", "s_ge", "s_st"])

    hd_pe = {}

    def load_head(h):
        b = h % 2
        if h - 2 in hd_pe:
            es.wait("sync", s_pe, hd_pe[h - 2])
        for dst, src in [(cp[f"a_kh{b}"], g["K_d"][h * 128:(h + 1) * 128, :]),
                         (cp[f"a_qh{b}"], g["Q_d"][h * 128:(h + 1) * 128, :])]:
            fin = inc(s_xld, 16)
            es.sync(lambda sy, dst=dst, src=src, fin=fin:
                    fin(sy.dma_start(out=dst[:], in_=src)))
        fin = inc(s_xld, 16)
        es.sync(lambda sy, h=h, b=b, fin=fin: fin(sy.dma_start(
            out=cp[f"a_vh{b}"][:].rearrange("p (kt d) -> p kt d", kt=64),
            in_=g["V_d"].rearrange("(kt p) d -> p kt d", p=128)[
                :, :, h * 128:(h + 1) * 128])))

    ld_done = {}
    load_head(0)
    ld_done[0] = s_xld.n
    for h in range(HG):
        b = h % 2
        if h + 1 < HG:
            load_head(h + 1)
            ld_done[h + 1] = s_xld.n
        es.wait("tensor", s_xld, ld_done[h])
        kh, vh, qh = cp[f"a_kh{b}"], cp[f"a_vh{b}"], cp[f"a_qh{b}"]
        hd_pe[h] = None
        for qm in range(8):
            nkv = (qm + 1) * 8
            exp_h, msk_h, sum_h = {}, {}, {}
            es.wait("tensor", s_se)
            es.wait("tensor", s_ve)
            for kvt in range(nkv + 1):
                if kvt < nkv:
                    sb0, sb1 = (2, 3) if kvt % 2 == 0 else (4, 5)
                    if kvt - 2 in exp_h:
                        es.wait("tensor", s_se, exp_h[kvt - 2])
                    fin = inc(s_pe)
                    es.tensor(lambda t, kvt=kvt, qm=qm, sb0=sb0, sb1=sb1,
                              kh=kh, qh=qh, fin=fin: (
                        t.matmul(pk[sb0][:, :], kh[:, kvt * 128:(kvt + 1) * 128],
                                 qh[:, qm * 1024:qm * 1024 + 512],
                                 start=True, stop=True),
                        fin(t.matmul(pk[sb1][:, :],
                                     kh[:, kvt * 128:(kvt + 1) * 128],
                                     qh[:, qm * 1024 + 512:qm * 1024 + 1024],
                                     start=True, stop=True))))
                av = kvt - 1
                if av >= 0:
                    es.wait("tensor", s_se, exp_h[av])
                    if av in msk_h:
                        es.wait("tensor", s_ge, msk_h[av])
                    fin = inc(s_pe)
                    es.tensor(lambda t, av=av, vh=vh, nkv=nkv, fin=fin: (
                        t.matmul(pk[0][:, :], vh[:, av * 128:(av + 1) * 128],
                                 cp[f"a_pt{av % 2}"][:, 0:512],
                                 start=(av == 0), stop=(av == nkv - 1),
                                 skip_group_check=True),
                        fin(t.matmul(pk[1][:, :], vh[:, av * 128:(av + 1) * 128],
                                     cp[f"a_pt{av % 2}"][:, 512:1024],
                                     start=(av == 0), stop=(av == nkv - 1),
                                     skip_group_check=True))))
                if kvt < nkv:
                    es.wait("scalar", s_pe)
                    if kvt - 2 in sum_h:
                        es.wait("scalar", s_ve, sum_h[kvt - 2])
                    fin = inc(s_se)
                    es.scalar(lambda sc, kvt=kvt, sb0=sb0, sb1=sb1, fin=fin: (
                        sc.activation(cp[f"a_pt{kvt % 2}"][:, 0:512],
                                      pk[sb0][:, :], AF.Exp,
                                      scale=RV, bias=-EXP_SHIFT),
                        fin(sc.activation(cp[f"a_pt{kvt % 2}"][:, 512:1024],
                                          pk[sb1][:, :], AF.Exp,
                                          scale=RV, bias=-EXP_SHIFT))))
                    exp_h[kvt] = s_se.n
                    if kvt >= qm * 8:
                        rel = kvt - qm * 8
                        es.wait("gpsimd", s_se)
                        fin = inc(s_ge)
                        es.gpsimd(lambda gp, kvt=kvt, rel=rel, fin=fin:
                                  fin(gp.tensor_tensor(
                                      cp[f"a_pt{kvt % 2}"][:, :],
                                      cp[f"a_pt{kvt % 2}"][:, :],
                                      cp["a_mask01"][:, rel * 1024:
                                                     (rel + 1) * 1024],
                                      ALU.mult)))
                        msk_h[kvt] = s_ge.n
                    es.wait("vector", s_se)
                    if kvt in msk_h:
                        es.wait("vector", s_ge)
                    es.wait("vector", s_ve)
                    if kvt == 0:
                        es.wait("vector", s_pe)
                    fin = inc(s_ve)
                    if kvt == 0:
                        es.vector(lambda v, kvt=kvt, fin=fin:
                                  fin(v.tensor_copy(cp["a_accf"][:, :],
                                                    cp[f"a_pt{kvt % 2}"][:, :])))
                    else:
                        es.vector(lambda v, kvt=kvt, fin=fin:
                                  fin(v.tensor_tensor(cp["a_accf"][:, :],
                                                      cp["a_accf"][:, :],
                                                      cp[f"a_pt{kvt % 2}"][:, :],
                                                      ALU.add)))
                    sum_h[kvt] = s_ve.n
            es.wait("scalar", s_pe)
            fin = inc(s_se)
            es.scalar(lambda sc, fin=fin: (
                sc.copy(cp["a_osb"][:, 0:512], pk[0][:, :]),
                fin(sc.copy(cp["a_osb"][:, 512:1024], pk[1][:, :]))))
            es.wait("tensor", s_ve)
            es.wait("tensor", s_se)
            fin = inc(s_pe)
            es.tensor(lambda t, fin=fin: (
                t.matmul(pk[2][0:1, :], cp["ones_col_f"][:, :],
                         cp["a_accf"][:, 0:512], start=True, stop=True),
                fin(t.matmul(pk[3][0:1, :], cp["ones_col_f"][:, :],
                             cp["a_accf"][:, 512:1024], start=True, stop=True))))
            es.wait("vector", s_pe)
            fin = inc(s_ve)
            es.vector(lambda v, fin=fin: (
                v.reciprocal(cp["r_mean"][0:1, 0:512], pk[2][0:1, :]),
                fin(v.reciprocal(cp["r_ex2"][0:1, 0:512], pk[3][0:1, :]))))
            es.wait("tensor", s_ve)
            fin = inc(s_pe)
            es.tensor(lambda t, fin=fin: (
                t.matmul(pk[4][:, :], cp["ones_row_f"][:, :],
                         cp["r_mean"][0:1, 0:512], start=True, stop=True),
                fin(t.matmul(pk[5][:, :], cp["ones_row_f"][:, :],
                             cp["r_ex2"][0:1, 0:512], start=True, stop=True))))
            es.wait("vector", s_pe)
            es.wait("vector", s_st)
            fin = inc(s_ve)
            es.vector(lambda v, fin=fin: (
                v.tensor_tensor(cp["a_ctxs"][:, 0:512], cp["a_osb"][:, 0:512],
                                pk[4][:, :], ALU.mult),
                fin(v.tensor_tensor(cp["a_ctxs"][:, 512:1024],
                                    cp["a_osb"][:, 512:1024],
                                    pk[5][:, :], ALU.mult))))
            es.wait("sync", s_ve)
            fin = inc(s_st, 16)
            es.sync(lambda sy, h=h, qm=qm, fin=fin: fin(sy.dma_start(
                out=g["ctx_d"][h * 128:(h + 1) * 128,
                               qm * 1024:(qm + 1) * 1024],
                in_=cp["a_ctxs"][:, :])))
        hd_pe[h] = s_pe.n


def _wo_partial(nc, cp, kind):
    g, es = cp["g"], cp["es"]
    pk = cp["pk"]
    s_wld, s_xld, s_pe, s_se, s_st = (
        cp[k] for k in ["s_wld", "s_xld", "s_pe", "s_se", "s_st"])
    wsrc = "wo_f_hg" if kind == "ffn" else "wo_b_hg"

    fin = inc(s_wld, 16)
    es.gpsimd(lambda gp, fin=fin: fin(gp.dma_start(
        out=cp["a_woslab"][:].rearrange("p (kc m) -> p kc m", kc=4),
        in_=g[wsrc].rearrange("(kc p) m -> p kc m", p=128))))
    es.wait("sync", s_st)
    hist = {}
    pe_gt = {}
    for gt in range(16):
        if gt - 1 in pe_gt:
            es.wait("sync", s_pe, pe_gt[gt - 1])
        fin = inc(s_xld, 16)
        es.sync(lambda sy, gt=gt, fin=fin: fin(sy.dma_start(
            out=cp["a_ctx4"][:].rearrange("p (kc t) -> p kc t", kc=4),
            in_=g["ctx_d"][:, gt * 512:(gt + 1) * 512].rearrange(
                "(kc p) t -> p kc t", p=128))))
        es.wait("tensor", s_xld)
        es.wait("tensor", s_wld)
        for ot in range(16):
            it = gt * 16 + ot
            if ("se", it - 2) in hist:
                es.wait("tensor", s_se, hist[("se", it - 2)])
            fin = inc(s_pe)
            def mm(t, ot=ot, it=it, fin=fin):
                ins = None
                for kc in range(4):
                    ins = t.matmul(pk[it % 2][:, :],
                                   cp["a_woslab"][:, kc * D + ot * 128:
                                                  kc * D + ot * 128 + 128],
                                   cp["a_ctx4"][:, kc * 512:(kc + 1) * 512],
                                   start=(kc == 0), stop=(kc == 3))
                fin(ins)
            es.tensor(mm)
            pe_gt[gt] = s_pe.n
            es.wait("scalar", s_pe)
            if ("st", it - 2) in hist:
                es.wait("scalar", s_st, hist[("st", it - 2)])
            fin = inc(s_se)
            es.scalar(lambda sc, it=it, fin=fin:
                      fin(sc.copy(cp[f"a_dstage{it % 2}"][:, :],
                                  pk[it % 2][:, :])))
            hist[("se", it)] = s_se.n
            es.wait("sync", s_se)
            r = gt // 4
            fin = inc(s_st, 16)
            es.sync(lambda sy, gt=gt, ot=ot, it=it, r=r, fin=fin:
                    fin(sy.dma_start(
                        out=g["rs_in2"][r * D + ot * 128:r * D + ot * 128 + 128,
                                        (gt % 4) * 512:(gt % 4) * 512 + 512],
                        in_=cp[f"a_dstage{it % 2}"][:, :])))
            hist[("st", it)] = s_st.n


def _attnb_epilogue(nc, cp):
    g, es, nc_ = cp["g"], cp["es"], nc
    s_xld, s_ve, s_st, s_cc = (cp[k] for k in ["s_xld", "s_ve", "s_st", "s_cc"])
    with ExitStack() as ec:
        cx = ec.enter_context(nc.sbuf_tensor("sb_ab_cx", [128, 512], F32))
        dd = ec.enter_context(nc.sbuf_tensor("sb_ab_dd", [128, 512], BF16))
        bo = ec.enter_context(nc.sbuf_tensor("sb_ab_bo", [128, FC], F32))
        fin = inc(cp["s_wld"], 16)
        es.gpsimd(lambda gp, fin=fin:
                  fin(gp.dma_start(out=bo[:], in_=g["bo_c"])))
        es.wait("sync", cp["s_cc"])
        es.wait("sync", cp["s_wld"])
        for mt in range(FC):
            for nt in range(4):
                es.wait("sync", s_st)
                es.wait("sync", s_ve)
                fin = inc(s_xld, 16)
                es.sync(lambda sy, mt=mt, nt=nt, fin=fin: fin(sy.dma_start(
                    out=cx[:, :], in_=g["x_res"][mt * 128:(mt + 1) * 128,
                                                 nt * 512:nt * 512 + 512])))
                fin = inc(s_xld, 16)
                es.sync(lambda sy, mt=mt, nt=nt, fin=fin: fin(sy.dma_start(
                    out=dd[:, :], in_=g["rs_out2"][mt * 128:(mt + 1) * 128,
                                                   nt * 512:nt * 512 + 512])))
                es.wait("vector", s_xld)
                fin = inc(s_ve)
                es.vector(lambda v, fin=fin:
                          fin(v.tensor_tensor(cx[:, :], cx[:, :], dd[:, :],
                                              ALU.add)))
                es.wait("vector", s_ve)
                fin = inc(s_ve)
                es.vector(lambda v, mt=mt, fin=fin:
                          fin(v.tensor_scalar(cx[:, :], cx[:, :],
                                              bo[:, mt:mt + 1], None, ALU.add)))
                es.wait("sync", s_ve)
                fin = inc(s_st, 16)
                es.sync(lambda sy, mt=mt, nt=nt, fin=fin: fin(sy.dma_start(
                    out=g["x_res"][mt * 128:(mt + 1) * 128,
                                   nt * 512:nt * 512 + 512],
                    in_=cx[:, :])))


_FXN = [0]


def _ffn_xtl(nc, cp, lnx2):
    g, es = cp["g"], cp["es"]
    s_xld, s_ve, s_st = (cp[k] for k in ["s_xld", "s_ve", "s_st"])
    _FXN[0] += 1
    with ExitStack() as ec:
        ta = ec.enter_context(nc.sbuf_tensor(f"sb_fx_ta{_FXN[0]}", [128, 512], BF16))
        tb = ec.enter_context(nc.sbuf_tensor(f"sb_fx_tb{_FXN[0]}", [128, 512], BF16))
        es.wait("sync", cp["s_cc"])
        for cc in range(FC):
            for nt in range(4):
                es.wait("sync", s_ve)
                fin = inc(s_xld, 16)
                es.sync(lambda sy, cc=cc, nt=nt, fin=fin: fin(sy.dma_start(
                    out=ta[:, :], in_=g["xn_loc"][cc * 128:(cc + 1) * 128,
                                                  nt * 512:nt * 512 + 512])))
                fin = inc(s_xld, 16)
                es.sync(lambda sy, cc=cc, nt=nt, fin=fin: fin(sy.dma_start(
                    out=tb[:, :], in_=g["rs_out2"][cc * 128:(cc + 1) * 128,
                                                   nt * 512:nt * 512 + 512])))
                es.wait("vector", s_xld)
                fin = inc(s_ve)
                es.vector(lambda v, cc=cc, nt=nt, fin=fin:
                          fin(v.tensor_tensor(
                              lnx2[:, cc * TQ + nt * 512:cc * TQ + nt * 512 + 512],
                              ta[:, :], tb[:, :], ALU.add)))
        es.wait("sync", s_ve)
        fin = inc(s_st, 16)
        es.sync(lambda sy, fin=fin: fin(sy.dma_start(
            out=g["xtl_d"].rearrange("(cc p) t -> p cc t", p=128),
            in_=lnx2[:].rearrange("p (cc t) -> p cc t", cc=FC))))


def _ffn_mlp_combine(nc, cp, xn2, call):
    g, es = cp["g"], cp["es"]
    pk = cp["pk"]
    s_wld, s_xld, s_pe, s_ve, s_se, s_ge, s_st = (
        cp[k] for k in ["s_wld", "s_xld", "s_pe", "s_ve", "s_se", "s_ge", "s_st"])
    NMT = FF // 128

    def vstep(fn, waits=()):
        for w in waits:
            es.wait("vector", cp[w])
        es.wait("vector", s_ve)
        fin = inc(s_ve)
        es.vector(lambda v, fn=fn, fin=fin: fin(fn(v)))

    for nt in range(4):
        es.wait("tensor", s_ve)
        es.wait("tensor", s_se)
        fin = inc(s_pe)
        es.tensor(lambda t, nt=nt, fin=fin: (
            t.matmul(pk[4][:, :], cp["ones_row"][:, :],
                     cp["a_m0"][0:1, nt * 512:nt * 512 + 512],
                     start=True, stop=True),
            fin(t.matmul(pk[5][:, :], cp["ones_row"][:, :],
                         cp["a_m1"][0:1, nt * 512:nt * 512 + 512],
                         start=True, stop=True))))
        es.wait("vector", s_pe)
        fin = inc(s_ve)
        es.vector(lambda v, fin=fin: (
            v.tensor_copy(cp["a_cmb0"][:, :], pk[4][:, :]),
            fin(v.tensor_copy(cp["a_cmb1"][:, :], pk[5][:, :]))))
        vstep(lambda v: v.tensor_tensor(cp["a_cmb2"][:, :], cp["a_cmb0"][:, :],
                                        cp["a_cmb1"][:, :], ALU.add))
        vstep(lambda v: v.tensor_scalar(cp["a_cmb2"][:, :], cp["a_cmb2"][:, :],
                                        -1.0, 1.0, ALU.mult, ALU.add))
        se_h = {}
        pe_h = {}
        for mt in range(NMT):
            buf = mt % 2
            if mt >= 2:
                es.wait("gpsimd", s_pe, pe_h[mt - 2])
            if call == 0 and nt == 0:
                fin = inc(s_wld, 16)
                es.gpsimd(lambda gp, mt=mt, buf=buf, fin=fin: fin(gp.dma_start(
                    out=cp[f"a_fslab{buf}"][:].rearrange(
                        "p (kc m) -> p kc m", kc=FC),
                    in_=g["w_ff1_T"][:, mt * 128:(mt + 1) * 128].rearrange(
                        "(kc p) m -> p kc m", p=128))))
                es.wait("sync", s_wld)
                fin = inc(s_st, 16)
                es.sync(lambda sy, mt=mt, buf=buf, fin=fin: fin(sy.dma_start(
                    out=g["wff1_bf"][:, mt * 128:(mt + 1) * 128].rearrange(
                        "(kc p) m -> p kc m", p=128),
                    in_=cp[f"a_fslab{buf}"][:].rearrange(
                        "p (kc m) -> p kc m", kc=FC))))
            else:
                fin = inc(s_wld, 16)
                es.gpsimd(lambda gp, mt=mt, buf=buf, fin=fin: fin(gp.dma_start(
                    out=cp[f"a_fslab{buf}"][:].rearrange(
                        "p (kc m) -> p kc m", kc=FC),
                    in_=g["wff1_bf"][:, mt * 128:(mt + 1) * 128].rearrange(
                        "(kc p) m -> p kc m", p=128))))
            es.wait("tensor", s_wld)
            if mt >= 2:
                es.wait("tensor", s_se, se_h[mt - 2])
            fin = inc(s_pe)
            def f1mm(t, mt=mt, nt=nt, buf=buf, fin=fin):
                ins = None
                for kc in range(FC):
                    ins = t.matmul(pk[mt % 2][:, :],
                                   cp[f"a_fslab{buf}"][:, kc * 128:(kc + 1) * 128],
                                   xn2[:, kc * TQ + nt * 512:
                                       kc * TQ + nt * 512 + 512],
                                   start=(kc == 0), stop=(kc == FC - 1))
                fin(ins)
            es.tensor(f1mm)
            pe_h[mt] = s_pe.n
            es.wait("scalar", s_pe)
            fin = inc(s_se)
            es.scalar(lambda sc, mt=mt, fin=fin:
                      fin(sc.activation(cp["a_hsb"][:, mt * 512:(mt + 1) * 512],
                                        pk[mt % 2][:, :], AF.Gelu_apprx_tanh)))
            se_h[mt] = s_se.n
        ve_cmb = {}
        for ot in range(FC):
            es.wait("gpsimd", s_pe)
            if call == 0 and nt == 0:
                fin = inc(s_wld, 16)
                es.gpsimd(lambda gp, ot=ot, fin=fin: fin(gp.dma_start(
                    out=cp["a_f2slab"][:].rearrange("p (kc m) -> p kc m",
                                                    kc=FF // 128),
                    in_=g["w_ff2_T"][:, ot * 128:(ot + 1) * 128].rearrange(
                        "(kc p) m -> p kc m", p=128))))
                es.wait("sync", s_wld)
                fin = inc(s_st, 16)
                es.sync(lambda sy, ot=ot, fin=fin: fin(sy.dma_start(
                    out=g["wff2_bf"][:, ot * 128:(ot + 1) * 128].rearrange(
                        "(kc p) m -> p kc m", p=128),
                    in_=cp["a_f2slab"][:].rearrange("p (kc m) -> p kc m",
                                                    kc=FF // 128))))
            else:
                fin = inc(s_wld, 16)
                es.gpsimd(lambda gp, ot=ot, fin=fin: fin(gp.dma_start(
                    out=cp["a_f2slab"][:].rearrange("p (kc m) -> p kc m",
                                                    kc=FF // 128),
                    in_=g["wff2_bf"][:, ot * 128:(ot + 1) * 128].rearrange(
                        "(kc p) m -> p kc m", p=128))))
            es.wait("tensor", s_wld)
            es.wait("tensor", s_se)
            if ot - 2 in ve_cmb:
                es.wait("tensor", s_ve, ve_cmb[ot - 2])
            fin = inc(s_pe)
            def f2mm(t, ot=ot, fin=fin):
                ins = None
                for kc in range(FF // 128):
                    ins = t.matmul(pk[2 + ot % 2][:, :],
                                   cp["a_f2slab"][:, kc * 128:(kc + 1) * 128],
                                   cp["a_hsb"][:, kc * 512:(kc + 1) * 512],
                                   start=(kc == 0), stop=(kc == FF // 128 - 1))
                fin(ins)
            es.tensor(f2mm)
            es.wait("sync", s_ve)
            for dst, srcn in [("a_ct0", "xn_loc"), ("a_ct1", "y1_d"),
                              ("a_ct2", "xtl_d")]:
                fin = inc(s_xld, 16)
                es.sync(lambda sy, dst=dst, srcn=srcn, ot=ot, nt=nt, fin=fin:
                        fin(sy.dma_start(
                            out=cp[dst][:, :],
                            in_=g[srcn][ot * 128:(ot + 1) * 128,
                                        nt * 512:nt * 512 + 512])))
            fin = inc(s_xld, 16)
            es.sync(lambda sy, ot=ot, nt=nt, fin=fin: fin(sy.dma_start(
                out=cp["a_cx"][:, :],
                in_=g["x_res"][ot * 128:(ot + 1) * 128,
                               nt * 512:nt * 512 + 512])))
            es.wait("vector", s_pe)
            es.wait("vector", s_xld)
            vstep(lambda v, ot=ot: v.tensor_tensor(
                cp["a_ct0b"][:, :], pk[2 + ot % 2][:, :], cp["a_ct2"][:, :],
                ALU.add))
            vstep(lambda v: v.tensor_tensor(cp["a_ct0b"][:, :], cp["a_ct0b"][:, :],
                                            cp["a_cmb2"][:, :], ALU.mult))
            vstep(lambda v, ot=ot: v.tensor_scalar(
                cp["a_ct1b"][:, :], cp["a_ct0"][:, :],
                cp["a_sbc"][:, ot:ot + 1], None, ALU.add))
            vstep(lambda v: v.tensor_tensor(cp["a_ct1b"][:, :], cp["a_ct1b"][:, :],
                                            cp["a_cmb0"][:, :], ALU.mult))
            vstep(lambda v: v.tensor_tensor(cp["a_ct0b"][:, :], cp["a_ct0b"][:, :],
                                            cp["a_ct1b"][:, :], ALU.add))
            vstep(lambda v: v.tensor_tensor(cp["a_ct1b"][:, :], cp["a_ct0"][:, :],
                                            cp["a_ct1"][:, :], ALU.add))
            vstep(lambda v: v.tensor_tensor(cp["a_ct1b"][:, :], cp["a_ct1b"][:, :],
                                            cp["a_cmb1"][:, :], ALU.mult))
            vstep(lambda v: v.tensor_tensor(cp["a_ct0b"][:, :], cp["a_ct0b"][:, :],
                                            cp["a_ct1b"][:, :], ALU.add))
            vstep(lambda v: v.tensor_tensor(cp["a_cx"][:, :], cp["a_cx"][:, :],
                                            cp["a_ct0b"][:, :], ALU.add))
            ve_cmb[ot] = s_ve.n
            es.wait("sync", s_ve)
            fin = inc(s_st, 16)
            es.sync(lambda sy, ot=ot, nt=nt, fin=fin: fin(sy.dma_start(
                out=g["x_res"][ot * 128:(ot + 1) * 128,
                               nt * 512:nt * 512 + 512],
                in_=cp["a_cx"][:, :])))
            es.wait("vector", s_st)


def kernel(**inputs):
    maps = stage_inputs(inputs)
    nc = build_program()
    res = run_bass_kernel_spmd(nc, maps, core_ids=list(range(NCORE)))
    out = np.zeros((B, S, D), np.float32)
    for c in range(NCORE):
        b, q = divmod(c, 4)
        out[b, q * TQ:(q + 1) * TQ, :] = np.asarray(res.results[c]["out_f"]).T
    return out
